# version 06
# baseline (speedup 1.0000x reference)
import contextlib
import numpy as np
import concourse.bass as bass
import concourse.mybir as mybir
from concourse.bass_utils import run_bass_kernel_spmd

F32 = mybir.dt.float32
BF16 = mybir.dt.bfloat16
AF = mybir.ActivationFunctionType
ALU = mybir.AluOpType
AX = mybir.AxisListType

D, S, NC8, H, DH, FF, NE = 1024, 2048, 8, 16, 64, 4096, 8
TCH, NCH = 512, 4
C0, C1, EPS = 8.0, 4.0, 1e-5
K0, ESCW = 1920, 3968

TAP = None


class Sem:
    __slots__ = ("h", "n")

    def __init__(self, h):
        self.h = h
        self.n = 0


def MM(out, lhsT, rhs, start, stop):
    return lambda e, o=out, a=lhsT, r=rhs, s1=start, s2=stop: e.matmul(
        o, a, r, start=s1, stop=s2)


def CPY(out, in_):
    return lambda e, o=out, i=in_: e.tensor_copy(o, i)


def TT(out, a, b_, op):
    return lambda e, o=out, x=a, y=b_, p=op: e.tensor_tensor(o, x, y, p)


def TSC(out, in_, s1, s2, op0, op1=None):
    if op1 is None:
        return lambda e, o=out, i=in_, a=s1, p=op0: e.tensor_scalar(o, i, a, None, p)
    return lambda e, o=out, i=in_, a=s1, c=s2, p=op0, q=op1: e.tensor_scalar(
        o, i, a, c, p, q)


def STT(out, in0, sc, in1, op0, op1):
    return lambda e, o=out, i=in0, s=sc, j=in1, p=op0, q=op1: \
        e.scalar_tensor_tensor(o, i, s, j, p, q)


def ACT(out, in_, func, bias=0.0, scale=1.0, accum_out=None):
    return lambda e, o=out, i=in_, f=func, b=bias, s=scale, a=accum_out: \
        e.activation(o, i, f, bias=b, scale=s, accum_out=a)


def DMA(out, in_):
    return lambda e, o=out, i=in_: e.dma_start(out=o, in_=i)


def RED(out, in_, op, absv=False):
    return lambda e, o=out, i=in_, p=op, a=absv: e.tensor_reduce(
        o, i, AX.X, p, apply_absolute_value=a)


def RECIP(out, in_):
    return lambda e, o=out, i=in_: e.reciprocal(o, i)


class Bld:
    def __init__(self, nc):
        self.nc = nc
        self.q = {k: [] for k in "ytvsg"}
        self.rd = {k: set() for k in "vsg"}
        self.wr = {k: set() for k in "vsg"}

    def drain(self, ek):
        if ek in self.rd:
            self.rd[ek].clear()
            self.wr[ek].clear()
        self.q[ek].append(lambda e: e.drain())

    def op(self, ek, fn, reads=(), writes=(), inc=None, inck=1):
        if ek in self.rd:
            rs, ws = set(reads), set(writes)
            if (rs & self.wr[ek]) or (ws & (self.wr[ek] | self.rd[ek])):
                self.q[ek].append(lambda e: e.drain())
                self.rd[ek].clear()
                self.wr[ek].clear()
            self.rd[ek] |= rs
            self.wr[ek] |= ws
        if inc is not None:
            inc.n += inck
            self.q[ek].append(lambda e, f=fn, s=inc.h, k=inck: f(e).then_inc(s, k))
        else:
            self.q[ek].append(fn)

    def w(self, ek, sem, val=None):
        v = sem.n if val is None else val
        if v <= 0:
            return
        self.q[ek].append(lambda e, s=sem.h, vv=v: e.wait_ge(s, vv))

    def barrier(self, ph):
        for ek in "ytvsg":
            if ek in self.rd:
                self.rd[ek].clear()
                self.wr[ek].clear()
            self.q[ek].append(lambda e: e.drain())
            ph.n += 1
            self.q[ek].append(lambda e, s=ph.h: e.sem_inc(s, 1))
        for ek in "ytvsg":
            self.q[ek].append(lambda e, s=ph.h, v=ph.n: e.wait_ge(s, v))

    def replay(self, block):
        q = self.q

        @block.sync
        def _(e):
            for f in q["y"]:
                f(e)

        @block.tensor
        def _(e):
            for f in q["t"]:
                f(e)

        @block.vector
        def _(e):
            for f in q["v"]:
                f(e)

        @block.scalar
        def _(e):
            for f in q["s"]:
                f(e)

        @block.gpsimd
        def _(e):
            for f in q["g"]:
                f(e)


def build_program(tap=None):
    nc = bass.Bass("TRN2", target_bir_lowering=False)
    dtn = lambda n, sh, kind: nc.dram_tensor(n, sh, F32, kind=kind).ap()
    dr = {
        "xT": dtn("xT", [D, S], "ExternalInput"),
        "wq": dtn("wq", [D, D], "ExternalInput"),
        "wk": dtn("wk", [D, D], "ExternalInput"),
        "wv": dtn("wv", [D, D], "ExternalInput"),
        "wo": dtn("wo", [D, D], "ExternalInput"),
        "gwT": dtn("gwT", [D, NE], "ExternalInput"),
        "w1T": dtn("w1T", [NE, D, FF], "ExternalInput"),
        "w2T": dtn("w2T", [NE, FF, D], "ExternalInput"),
        "qkvT": dtn("qkvT", [D, 3 * D], "ExternalInput"),
        "woT1": dtn("woT1", [D, D], "ExternalInput"),
        "mw1T": dtn("mw1T", [D, FF], "ExternalInput"),
        "mw2T": dtn("mw2T", [FF, D], "ExternalInput"),
        "fw1T": dtn("fw1T", [D, FF], "ExternalInput"),
        "fw2T": dtn("fw2T", [FF, D], "ExternalInput"),
        "prm": dtn("prm", [128, 224], "ExternalInput"),
        "alibi": dtn("alibi", [H, 128, ESCW], "ExternalInput"),
        "spill": dtn("spill", [D, S], "Internal"),
        "fq1": dtn("fq1", [D, FF], "Internal"),
        "fq2": dtn("fq2", [FF, D], "Internal"),
        "outT": dtn("outT", [D, S], "ExternalOutput"),
    }
    dch = lambda name: dr[name].rearrange("(c p) t -> c p t", p=128)

    es = contextlib.ExitStack()
    with es:
        sbt = lambda n, sh: es.enter_context(nc.sbuf_tensor(n, sh, F32)).ap()
        pst = lambda n, sh: es.enter_context(nc.psum_tensor(n, sh, F32)).ap()
        A1 = sbt("A1", [128, NC8 * S])
        A2 = sbt("A2", [128, NC8 * S])
        POOL = sbt("POOL", [128, 12288])
        PT = [sbt("PTa", [128, 512]), sbt("PTb", [128, 512])]
        TMP0 = sbt("TMP0", [128, 2048])
        PK = sbt("PK", [128, 192])
        STG32 = sbt("STG32", [32, 128])
        ROW_A = sbt("ROW_A", [1, 512])
        ROW_B = sbt("ROW_B", [1, 512])
        ROWN = sbt("ROWN", [1, 512])
        GTb = sbt("GTb", [128, 8])
        GS1 = sbt("GS1", [128, 8])
        GS2 = sbt("GS2", [128, 8])
        GCOL = sbt("GCOL", [128, 4])
        I128 = sbt("I128", [128, 128])
        MBAND = sbt("MBAND", [128, 1408])
        ONESR = sbt("ONESR", [97, 128])
        ONESC = sbt("ONESC", [128, 1])
        CONSTS = sbt("CONSTS", [128, 8])
        PRM = sbt("PRM", [128, 224])
        GWT = sbt("GWT", [128, 64])
        GWTB = sbt("GWTB", [128, 64])
        QV = sbt("QV", [1, 4])
        SCB = sbt("SCB", [128, 2])
        QSC = sbt("QSC", [128, 2])
        CTS = STG32[0:8, :]
        P = [pst("P%d" % i, [128, 512]) for i in range(8)]

        SMN = ["PH", "QW0", "QW1", "QX", "QG", "QR0", "QR1", "QO",
               "PE", "VE", "SE", "GE"]
        SM = {n: Sem(es.enter_context(nc.semaphore(n))) for n in SMN}
        PH, QX, QG, QO = SM["PH"], SM["QX"], SM["QG"], SM["QO"]
        PE, VE, SE, GE = SM["PE"], SM["VE"], SM["SE"], SM["GE"]
        QW = [SM["QW0"], SM["QW1"]]
        QR = [SM["QR0"], SM["QR1"]]

        b = Bld(nc)

        CC = {"eps": 0, "negC0": 1, "negC1": 2, "k0": 3}
        cc = lambda n: CONSTS[:, CC[n]:CC[n] + 1]
        PKC = {n: i * 16 for i, n in enumerate(
            ["s1", "s2", "m", "v", "a", "bb", "A", "B", "t0", "t1", "t2", "t3"])}
        pkv = lambda n, w=16: PK[:, PKC[n]:PKC[n] + w]
        PRL = [("g0a", 8), ("b0a", 8), ("g1", 8), ("b1", 8), ("g2", 8), ("b2", 8),
               ("mb1", 32), ("mb2", 8), ("fg1", 8), ("fb1", 8), ("fb1v", 32),
               ("fg2", 32), ("fb2", 32), ("fb2v", 8)]
        PRO = {}
        _o = 0
        for _nm, _wd in PRL:
            PRO[_nm] = _o
            _o += _wd
        assert _o <= 224
        prm = lambda nm, j: PRM[:, PRO[nm] + j:PRO[nm] + j + 1]

        QKVW_C, QT_C, KT_C, VAUG_C, ESC_C = 0, 3072, 5120, 7168, 8208
        WSA_C, WSB_C, XNC_C = 0, 4096, 8192
        pl = lambda c0, w, r=128: POOL[0:r, c0:c0 + w]
        asl = lambda ar, c, t0, w: ar[:, c * S + t0: c * S + t0 + w]
        POOLB = POOL.bitcast(BF16)
        A1B = A1.bitcast(BF16)
        A2B = A2.bitcast(BF16)
        GWTBB = GWTB.bitcast(BF16)
        plb = lambda c0, w, r=128: POOLB[0:r, c0:c0 + w]

        def mm_group(psum_ap, pairs, inc=None):
            n = len(pairs)
            for i, (a_, r_) in enumerate(pairs):
                b.op("t", MM(psum_ap, a_, r_, i == 0, i == n - 1),
                     inc=inc if i == n - 1 else None)

        b.op("y", DMA(PRM[:, :], dr["prm"][:, :]), inc=QX, inck=16)
        b.op("y", DMA(GWT[:, :].rearrange("p (c f) -> p c f", f=8),
                      dr["gwT"].rearrange("(c p) f -> p c f", p=128)), inc=QX, inck=16)
        for c in range(NC8):
            b.op("y", DMA(asl(A1, c, 0, S), dch("xT")[c]), inc=QX, inck=16)
        for nm, val in [("eps", EPS), ("negC0", -C0), ("negC1", -C1),
                        ("k0", float(K0))]:
            b.op("g", lambda e, a=cc(nm), v=val: e.memset(a, v),
                 writes=[("C", nm)], inc=GE)
        b.op("g", lambda e: e.memset(PK[:, :], 0.0), writes=["PKZ"], inc=GE)
        for _r in (0, 32, 64, 96):
            b.op("g", lambda e, a=ONESR[_r:_r + 1, :]: e.memset(a, 1.0),
                 writes=["ONESR"], inc=GE)
        b.op("g", lambda e: e.memset(ONESC[:, :], 1.0), writes=["ONESC"], inc=GE)
        b.op("g", lambda e: e.iota(I128[:, :], [[1, 128]], channel_multiplier=-1,
                                   allow_small_or_imprecise_dtypes=True),
             writes=["I128"], inc=GE)
        b.op("g", lambda e: e.iota(MBAND[:, :], [[1, 1408]], channel_multiplier=-1,
                                   allow_small_or_imprecise_dtypes=True),
             writes=["MBAND"], inc=GE)
        for t in range(16):
            b.op("g", lambda e, a=pl(VAUG_C + t * 65 + 64, 1): e.memset(a, 1.0),
                 writes=[("VAUG1", t)], inc=GE)
        b.w("v", GE)
        b.op("v", TSC(I128[:, :], I128[:, :], 0.0, None, ALU.is_equal),
             reads=["I128"], writes=["I128"], inc=VE)
        b.op("v", TSC(TMP0[:, 0:1408], MBAND[:, :], 384.0, None, ALU.is_ge),
             reads=["MBAND"], writes=["TMP0"])
        b.op("v", TSC(pl(ESC_C, 1408), MBAND[:, :], 896.0, None, ALU.is_lt),
             reads=["MBAND"], writes=[("POOL", "mb")])
        b.op("v", TT(MBAND[:, :], TMP0[:, 0:1408], pl(ESC_C, 1408), ALU.mult),
             reads=["TMP0", ("POOL", "mb")], writes=["MBAND"], inc=VE)
        b.w("v", QX)
        b.op("v", CPY(GWTBB[:, 0:64], GWT[:, :]), reads=["GWT"],
             writes=[("GWTB", 0)])
        b.op("v", TT(GWTBB[:, 64:128], GWT[:, :], GWTBB[:, 0:64], ALU.subtract),
             reads=["GWT", ("GWTB", 0)], writes=[("GWTB", 1)], inc=VE)
        b.w("t", QX)
        b.w("s", QX)
        b.barrier(PH)

        def _stats_math(nct, nf, inner_too):
            b.op("v", TSC(pkv("m")[:, 0:nct], pkv("s1")[:, 0:nct], 1.0 / nf, None,
                          ALU.mult), reads=[("PK", "s1")], writes=[("PK", "m")])
            b.op("v", TSC(pkv("v")[:, 0:nct], pkv("s2")[:, 0:nct], 1.0 / nf, None,
                          ALU.mult), reads=[("PK", "s2")], writes=[("PK", "v")])
            b.op("v", TT(pkv("t0")[:, 0:nct], pkv("m")[:, 0:nct], pkv("m")[:, 0:nct],
                         ALU.mult), reads=[("PK", "m")], writes=[("PK", "t0")])
            b.op("v", TT(pkv("v")[:, 0:nct], pkv("v")[:, 0:nct], pkv("t0")[:, 0:nct],
                         ALU.subtract), reads=[("PK", "t0"), ("PK", "v")],
                 writes=[("PK", "v")], inc=VE)
            b.w("s", VE)
            b.op("s", ACT(pkv("t0")[:, 0:nct], pkv("v")[:, 0:nct], AF.Ln,
                          bias=cc("eps")), reads=[("PK", "v")], writes=[("PK", "t0")])
            b.op("s", ACT(pkv("a")[:, 0:nct], pkv("t0")[:, 0:nct], AF.Exp, scale=-0.5),
                 reads=[("PK", "t0")], writes=[("PK", "a")], inc=SE)
            b.w("v", SE)
            b.op("v", STT(pkv("bb")[:, 0:nct], pkv("m")[:, 0:nct], -1.0,
                          pkv("a")[:, 0:nct], ALU.mult, ALU.mult),
                 reads=[("PK", "m"), ("PK", "a")], writes=[("PK", "bb")], inc=VE)
            if inner_too:
                b.op("v", TSC(pkv("t1")[:, 0:nct], pkv("v")[:, 0:nct], EPS, None,
                              ALU.add), reads=[("PK", "v")], writes=[("PK", "t1")])
                b.op("v", RECIP(pkv("t2")[:, 0:nct], pkv("t1")[:, 0:nct]),
                     reads=[("PK", "t1")], writes=[("PK", "t2")])
                b.op("v", TT(pkv("t1")[:, 0:nct], pkv("v")[:, 0:nct],
                             pkv("t2")[:, 0:nct], ALU.mult),
                     reads=[("PK", "v"), ("PK", "t2")], writes=[("PK", "t1")], inc=VE)
                b.w("s", VE)
                b.op("s", ACT(pkv("t2")[:, 0:nct], pkv("t1")[:, 0:nct], AF.Ln,
                              bias=cc("eps")), reads=[("PK", "t1")],
                     writes=[("PK", "t2")])
                b.op("s", ACT(pkv("t1")[:, 0:nct], pkv("t2")[:, 0:nct], AF.Exp,
                              scale=-0.5), reads=[("PK", "t2")],
                     writes=[("PK", "t1")], inc=SE)
                b.w("v", SE)
                b.op("v", TT(pkv("A")[:, 0:nct], pkv("a")[:, 0:nct],
                             pkv("t1")[:, 0:nct], ALU.mult),
                     reads=[("PK", "a"), ("PK", "t1")], writes=[("PK", "A")])
                b.op("v", TT(pkv("B")[:, 0:nct], pkv("bb")[:, 0:nct],
                             pkv("t1")[:, 0:nct], ALU.mult),
                     reads=[("PK", "bb"), ("PK", "t1")], writes=[("PK", "B")], inc=VE)

        def stats_pass(src, n_feat, tok0, tokens, inner_too):
            nct = tokens // 128
            nsl_ = tokens // 512
            b.w("t", VE, VE.n)
            b.w("s", VE, VE.n)
            b.drain("s")
            sqw = [TMP0[:, i * 512:(i + 1) * 512] for i in range(4)]
            pe_mark = [0] * 4
            for s in range(nsl_):
                t0 = tok0 + s * 512
                for wave in range(2):
                    cs = [4 * wave + i for i in range(4)]
                    for i, c in enumerate(cs):
                        b.w("s", PE, pe_mark[i])
                        b.op("s", ACT(sqw[i], asl(src, c, t0, 512), AF.Square),
                             reads=[(src.name, "st", c)], writes=[("SQ", i)],
                             inc=SE)
                    se_mark = SE.n
                    for tt in range(4):
                        for c in cs:
                            b.op("t", MM(P[tt][:, s:s + 1],
                                         asl(src, c, t0 + tt * 128, 128),
                                         ONESC[:, :], c == 0, c == NC8 - 1))
                    b.w("t", SE, se_mark)
                    for tt in range(4):
                        for i, c in enumerate(cs):
                            b.op("t", MM(P[4 + tt][:, s:s + 1],
                                         sqw[i][:, tt * 128:(tt + 1) * 128],
                                         ONESC[:, :], c == 0, c == NC8 - 1),
                                 inc=PE if (tt == 3 and i == 3) else None)
                    for i in range(4):
                        pe_mark[i] = PE.n
            b.w("v", PE)
            for tt in range(4):
                dstp = pkv("s1").rearrange("p (s t) -> p s t", t=4)[:, 0:nsl_,
                                                                   tt:tt + 1]
                srcp = P[tt][:, 0:nsl_].rearrange("p (s o) -> p s o", o=1)
                b.op("v", CPY(dstp, srcp), writes=[("PK", "s1")])
                dstq = pkv("s2").rearrange("p (s t) -> p s t", t=4)[:, 0:nsl_,
                                                                   tt:tt + 1]
                srcq = P[4 + tt][:, 0:nsl_].rearrange("p (s o) -> p s o", o=1)
                b.op("v", CPY(dstq, srcq), writes=[("PK", "s2")],
                     inc=VE if tt == 3 else None)
            _stats_math(nct, float(n_feat), inner_too)

        def rows_from(atag, btag, scol):
            b.op("v", CPY(pkv("t2")[:, 0:4], pkv(atag)[:, scol:scol + 4]),
                 reads=[("PK", atag)], writes=[("PK", "t2")])
            b.op("v", CPY(pkv("t3")[:, 0:4], pkv(btag)[:, scol:scol + 4]),
                 reads=[("PK", btag)], writes=[("PK", "t3")], inc=VE)
            b.w("t", VE)
            b.w("t", QG, QG.n)
            b.op("t", MM(P[2][0:32, 0:128], PK[:, PKC["t2"]:PKC["t3"] + 16],
                         I128[:, :], True, True), inc=PE)
            b.w("v", PE)
            b.w("v", QG, QG.n)
            b.op("v", CPY(STG32[:, :], P[2][0:32, 0:128]), writes=["STG32"], inc=VE)
            b.w("g", VE)
            b.op("g", DMA(ROW_A[0:1, 0:512], STG32[0:4, :]),
                 reads=["STG32"], writes=["ROW_A"], inc=QG, inck=16)
            b.op("g", DMA(ROW_B[0:1, 0:512], STG32[16:20, :]),
                 reads=["STG32"], writes=["ROW_B"], inc=QG, inck=16)

        def affine_apply(get_src, get_dst, atag, btag, gnm, bnm, pk_s0,
                         tokens, tag):
            nsl_ = tokens // 512
            for s in range(nsl_):
                rows_from(atag, btag, pk_s0 + s * 4)
                b.w("t", QG, QG.n)
                b.w("t", VE)
                b.op("t", MM(P[3][:, 0:512], ONESR[0:1, :], ROW_A[0:1, 0:512],
                             True, True))
                b.op("t", MM(P[4][:, 0:512], ONESR[0:1, :], ROW_B[0:1, 0:512],
                             True, True), inc=PE)
                b.w("v", PE)
                for c in range(NC8):
                    b.op("v", TT(PT[0][:, :], get_src(c, s), P[3][:, 0:512],
                                 ALU.mult), reads=[(tag, "src", c, s)],
                         writes=[("PT", 0)])
                    if gnm is not None:
                        b.op("v", TT(PT[0][:, :], PT[0][:, :], P[4][:, 0:512],
                                     ALU.add), reads=[("PT", 0)], writes=[("PT", 0)])
                        b.op("v", lambda e, o=get_dst(c, s), i=PT[0][:, :],
                             g=prm(gnm, c), bt=prm(bnm, c):
                             e.tensor_scalar(o, i, g, bt, ALU.mult, ALU.add),
                             reads=[("PT", 0)], writes=[(tag, "dst", c, s)], inc=VE)
                    else:
                        b.op("v", TT(get_dst(c, s), PT[0][:, :], P[4][:, 0:512],
                                     ALU.add), reads=[("PT", 0)],
                             writes=[(tag, "dst", c, s)], inc=VE)

        stats_pass(A1, D, 0, S, inner_too=True)
        b.barrier(PH)
        affine_apply(lambda c, s: asl(A1, c, s * 512, 512),
                     lambda c, s: asl(A2, c, s * 512, 512),
                     "A", "B", "g0a", "b0a", 0, S, "a12")
        b.w("g", VE)
        for c in range(NC8):
            b.op("g", DMA(dch("spill")[c], asl(A1, c, 0, S)),
                 reads=[("a12", "spx", c)], inc=QG, inck=16)
        b.w("g", QG)
        b.barrier(PH)

        def finalize(src):
            for c in range(NC8):
                b.op("g", DMA(dch("outT")[c], asl(src, c, 0, S)),
                     reads=[(src.name, "fin", c)], inc=QO, inck=16)
            b.w("g", QO)
            b.w("y", QO)
            block = es.enter_context(nc.Block())
            b.replay(block)
            return nc

        if tap == "xn2":
            return finalize(A2)

        def ldw_qkv(hd, slot, l1):
            base = QKVW_C + slot * 1536
            srcs = ([(dr["qkvT"], hd * DH), (dr["qkvT"], D + hd * DH),
                     (dr["qkvT"], 2 * D + hd * DH)] if l1 else
                    [(dr["wq"], hd * DH), (dr["wk"], hd * DH),
                     (dr["wv"], hd * DH)])
            for j, (mat, col0) in enumerate(srcs):
                src = mat[:, col0:col0 + DH].rearrange("(c p) f -> p c f", p=128)
                dst = POOL[:, base + j * 512: base + j * 512 + 512].rearrange(
                    "p (c f) -> p c f", f=DH)
                b.op("y", DMA(dst, src), inc=QW[slot], inck=16)

        qkvw = lambda slot, j, c: POOL[:, QKVW_C + slot * 1536 + j * 512 + c * DH:
                                       QKVW_C + slot * 1536 + j * 512 + (c + 1) * DH]

        def head_attn(l1, src, oall, hd):
            slot = hd % 2
            slope = float(2.0 ** (-8.0 / (H - hd)))
            b.w("t", QW[slot], QW[slot].n)
            ve_bank = {}
            b.w("t", VE, VE.n)
            b.w("t", SE, SE.n)
            for j, dc0 in ((0, QT_C), (1, KT_C)):
                for s in range(4):
                    bi = (j * 4 + s) % 2
                    bank = P[4 + bi]
                    if (j * 4 + s) >= 2:
                        b.w("t", VE, ve_bank[bi])
                    mm_group(bank[0:64, 0:512],
                             [(qkvw(slot, j, c), asl(src, c, s * 512, 512))
                              for c in range(NC8)], inc=PE)
                    b.w("v", PE, PE.n)
                    b.op("v", CPY(POOL[0:64, dc0 + s * 512: dc0 + (s + 1) * 512],
                                  bank[0:64, 0:512]),
                         writes=[("POOL", "qk", j, s)], inc=VE)
                    ve_bank[bi] = VE.n
            for half in range(2):
                bank = P[6 + half]
                for tt in range(8):
                    t8 = half * 8 + tt
                    mm_group(bank[:, tt * 64:(tt + 1) * 64],
                             [(asl(src, c, t8 * 128, 128), qkvw(slot, 2, c))
                              for c in range(NC8)],
                             inc=PE if tt == 7 else None)
                b.w("v", PE, PE.n)
                sap = bank[:, 0:512].rearrange("p (t f) -> p t f", f=64)
                dap = POOL[:, VAUG_C + half * 520: VAUG_C + half * 520 + 520]\
                    .rearrange("p (t f) -> p t f", f=65)[:, :, 0:64]
                b.op("v", CPY(dap, sap), writes=[("POOL", "vaug", half)], inc=VE)
            if not l1:
                b.w("y", VE, VE.n)
                b.w("y", SE, SE.n)
                b.op("y", DMA(pl(ESC_C, ESCW), dr["alibi"][hd]), inc=QX, inck=16)
                b.w("s", QX)
                b.op("s", ACT(pl(ESC_C, ESCW), pl(ESC_C, ESCW), AF.Exp),
                     reads=[("POOL", "esc")], writes=[("POOL", "esc")], inc=SE)
                b.w("v", SE)
            negC = cc("negC1") if l1 else cc("negC0")
            SB = [P[0], P[1], P[5], P[6]]
            PTB = [PT[0], PT[1], TMP0[:, 512:1024], TMP0[:, 1024:1536]]
            PTK = [("PT", 0), ("PT", 1), "TMP0", "TMP0"]
            pending_evac = [None]

            def emit_evac():
                if pending_evac[0] is None:
                    return
                pvbank, hd_, qj_, pe_pv = pending_evac[0]
                pending_evac[0] = None
                b.w("s", PE, pe_pv)
                b.op("s", ACT(ROWN[0:1, :], pvbank[64:65, 0:512], AF.Ln),
                     writes=["ROWN"])
                b.op("s", ACT(ROWN[0:1, :], ROWN[0:1, :], AF.Exp, scale=-1.0),
                     reads=["ROWN"], writes=["ROWN"], inc=SE)
                b.w("t", SE, SE.n)
                b.op("t", MM(P[4][0:64, 0:512], ONESR[0:1, 0:64], ROWN[0:1, :],
                             True, True), inc=PE)
                b.w("v", PE, PE.n)
                b.op("v", CPY(TMP0[0:64, 0:512], P[4][0:64, 0:512]),
                     writes=["TMP0"], inc=VE)
                orow = (hd_ % 2) * 64
                oc0 = (hd_ // 2) * S + qj_ * 512
                b.op("v", TT(oall[orow:orow + 64, oc0:oc0 + 512],
                             pvbank[0:64, 0:512], TMP0[0:64, 0:512], ALU.mult),
                     reads=["TMP0"], writes=[(oall.name, "o", hd_, qj_)],
                     inc=VE)

            for qj in range(4):
                pvbank = P[2 + (qj % 2)]
                kts = list(range(16)) if not l1 else \
                    (list(range(4, 8)) if qj == 0 else list(range(8)))
                n_kt = len(kts)
                se_marks, ve_marks, pe_score = {}, {}, {}

                def kslice(kt):
                    if not l1:
                        return POOL[0:64, KT_C + kt * 128: KT_C + (kt + 1) * 128]
                    kg0 = (qj - 1) * 512 + kt * 128
                    return POOL[0:64, KT_C + kg0: KT_C + kg0 + 128]

                def emit_score(idx):
                    kt = kts[idx]
                    r = idx % 4
                    if idx >= 4:
                        b.w("t", SE, se_marks[idx - 4])
                    b.op("t", MM(SB[r][:, 0:512], kslice(kt),
                                 POOL[0:64, QT_C + qj * 512:
                                      QT_C + (qj + 1) * 512],
                                 True, True), inc=PE)
                    pe_score[idx] = PE.n
                    b.w("s", PE, pe_score[idx])
                    if idx >= 4:
                        b.w("s", VE, ve_marks[idx - 4])
                    b.op("s", ACT(PTB[r], SB[r][:, 0:512], AF.Exp, bias=negC,
                                  scale=0.125), writes=[PTK[r]], inc=SE)
                    se_marks[idx] = SE.n
                    b.w("v", SE, se_marks[idx])
                    kt_ = kts[idx]
                    if not l1:
                        OFF = K0 - 128 * kt_ + 512 * qj
                        msl = POOL[:, ESC_C + OFF: ESC_C + OFF + 512]
                        rk = ("POOL", "esc")
                    else:
                        msl = MBAND[:, 896 - 128 * kt_: 1408 - 128 * kt_]
                        rk = "MBAND"
                    b.op("v", TT(PTB[r], PTB[r], msl, ALU.mult),
                         reads=[PTK[r], rk], writes=[PTK[r]], inc=VE)
                    ve_marks[idx] = VE.n

                def emit_pv(idx):
                    r = idx % 4
                    b.w("t", VE, ve_marks[idx])
                    kt = kts[idx]
                    vt = kt if not l1 else (qj - 1) * 4 + kt
                    vaug_t = POOL[:, VAUG_C + vt * 65: VAUG_C + vt * 65 + 65]
                    b.op("t", MM(pvbank[0:65, 0:512], vaug_t, PTB[r],
                                 idx == 0, idx == n_kt - 1),
                         inc=PE if idx == n_kt - 1 else None)

                b.w("t", VE, VE.n)
                emit_score(0)
                emit_evac()
                emit_score(1)
                if n_kt > 2:
                    emit_score(2)
                for idx in range(n_kt):
                    if idx + 3 < n_kt:
                        emit_score(idx + 3)
                    emit_pv(idx)
                pending_evac[0] = (pvbank, hd, qj, PE.n)
            emit_evac()

        def attn_layer(l1, src, oall):
            b.w("g", PE, PE.n)
            b.w("g", VE, VE.n)
            for t in range(16):
                b.op("g", lambda e, a=pl(VAUG_C + t * 65 + 64, 1): e.memset(a, 1.0),
                     writes=[("VAUG1", t)], inc=GE)
            b.w("t", GE, GE.n)
            proj_pe_done = [0, 0]
            ldw_qkv(0, 0, l1)
            for hd in range(H):
                head_attn(l1, src, oall, hd)
                proj_pe_done[hd % 2] = PE.n
                if hd + 1 < H:
                    b.w("y", PE, proj_pe_done[(hd + 1) % 2])
                    ldw_qkv(hd + 1, (hd + 1) % 2, l1)

        attn_layer(False, A2, A1)
        b.barrier(PH)
        if tap == "l0heads":
            return finalize(A1)

        def wo_phase(oall, dst, wod, add_affine):
            for i in range(NC8):
                half = WSA_C if i < 4 else WSB_C
                d0 = half + (i % 4) * 1024
                b.op("y", DMA(POOL[:, d0:d0 + 1024],
                              wod.rearrange("(c p) t -> c p t", p=128)[i]),
                     inc=QX, inck=16)
            b.w("t", QX)
            for s_ in range(4):
                if add_affine:
                    rows_from("a", "bb", s_ * 4)
                b.w("g", VE, VE.n)
                for c in range(NC8):
                    b.op("g", DMA(pl(XNC_C + c * 512, 512),
                                  dch("spill")[c][:, s_ * 512:(s_ + 1) * 512]),
                         writes=[("POOL", "xnc", c)], inc=QR[0], inck=16)
                if add_affine:
                    b.w("t", QG, QG.n)
                    b.w("t", VE)
                    b.op("t", MM(P[7][:, 0:512], ONESR[0:1, :],
                                 ROW_A[0:1, 0:512], True, True))
                    b.op("t", MM(P[3][:, 0:512], ONESR[0:1, :],
                                 ROW_B[0:1, 0:512], True, True), inc=PE)
                b.w("v", QR[0], QR[0].n)
                for dc in range(NC8):
                    bank = P[5 + (dc % 2)]
                    b.w("t", VE)
                    pairs = []
                    for i in range(NC8):
                        half = WSA_C if i < 4 else WSB_C
                        lh = POOL[:, half + (i % 4) * 1024 + dc * 128:
                                  half + (i % 4) * 1024 + (dc + 1) * 128]
                        pairs.append((lh, oall[:, i * S + s_ * 512:
                                               i * S + s_ * 512 + 512]))
                    mm_group(bank[:, 0:512], pairs, inc=PE)
                    b.w("v", PE)
                    if add_affine:
                        b.op("v", TT(PT[1][:, :], pl(XNC_C + dc * 512, 512),
                                     P[7][:, 0:512], ALU.mult),
                             reads=[("POOL", "xnc", dc)], writes=[("PT", 1)])
                        b.op("v", TT(PT[1][:, :], PT[1][:, :], P[3][:, 0:512],
                                     ALU.add), reads=[("PT", 1)], writes=[("PT", 1)])
                        b.op("v", TT(asl(dst, dc, s_ * 512, 512), PT[1][:, :],
                                     bank[:, 0:512], ALU.add),
                             reads=[("PT", 1)],
                             writes=[(dst.name, "wo", dc, s_)], inc=VE)
                    else:
                        b.op("v", TT(asl(dst, dc, s_ * 512, 512),
                                     pl(XNC_C + dc * 512, 512),
                                     bank[:, 0:512], ALU.add),
                             reads=[("POOL", "xnc", dc)],
                             writes=[(dst.name, "wo", dc, s_)], inc=VE)

        wo_phase(A1, A2, dr["wo"], add_affine=False)
        b.barrier(PH)
        if tap == "x1pre":
            return finalize(A2)

        def ffn_A(w1d, harena_b, act_bias, ch, evac_fp32=None):
            pe_slab = [PE.n, PE.n]
            ve_split = [VE.n, VE.n]
            for fg in range(8):
                sl = fg % 2
                wc = WSA_C if sl == 0 else WSB_C
                b.w("y", PE, pe_slab[sl])
                b.op("y", DMA(POOL[:, wc:wc + 4096]
                              .rearrange("p (c f) -> p c f", f=512),
                              w1d[:, fg * 512:(fg + 1) * 512]
                              .rearrange("(c p) f -> p c f", p=128)),
                     inc=QW[sl], inck=16)
                b.w("s", QW[sl], QW[sl].n)
                stg = TMP0[:, 1024:1536]
                for c in range(NC8):
                    base = 2 * wc + c * 1024
                    b.w("s", VE, VE.n)
                    b.op("s", ACT(stg, POOL[:, wc + c * 512: wc + c * 512 + 512],
                                  AF.Identity),
                         reads=[("POOL", "wsl", sl)], writes=["TMP0"])
                    b.op("s", ACT(POOLB[:, base: base + 512], stg, AF.Identity),
                         reads=["TMP0"], writes=[("POOL", "wsl", sl)],
                         inc=SE)
                    b.w("v", SE, SE.n)
                    b.op("v", TT(POOLB[:, base + 512: base + 1024], stg,
                                 POOLB[:, base: base + 512], ALU.subtract),
                         reads=["TMP0", ("POOL", "wsl", sl)],
                         writes=[("POOL", "wsl", sl)], inc=VE)
                b.w("t", VE, VE.n)
                for ft4 in range(4):
                    ft = fg * 4 + ft4
                    bank = P[ft % 2]
                    b.w("t", SE, max(0, SE.n - 1))
                    for c in range(NC8):
                        base = 2 * wc + c * 1024
                        wh = POOLB[:, base + ft4 * 128: base + ft4 * 128 + 128]
                        wl = POOLB[:, base + 512 + ft4 * 128:
                                   base + 512 + ft4 * 128 + 128]
                        xb = 2 * XNC_C + c * 1024
                        xh = POOLB[:, xb: xb + 512]
                        xl = POOLB[:, xb + 512: xb + 1024]
                        b.op("t", MM(bank[:, 0:512], wh, xh, c == 0, False))
                        b.op("t", MM(bank[:, 0:512], wh, xl, False, False))
                        b.op("t", MM(bank[:, 0:512], wl, xh, False, c == NC8 - 1),
                             inc=PE if c == NC8 - 1 else None)
                    b.w("s", PE, PE.n)
                    if evac_fp32 is not None:
                        evac_fp32(ft, bank)
                    else:
                        stg2 = PT[0][:, :]
                        b.w("s", VE, ve_split[0])
                        if act_bias is None:
                            b.op("s", ACT(stg2, bank[:, 0:512], AF.Relu),
                                 writes=[("PT", 0)])
                        else:
                            b.op("s", ACT(stg2, bank[:, 0:512], AF.Gelu,
                                          bias=prm(act_bias, ft)),
                                 writes=[("PT", 0)])
                        b.op("s", ACT(harena_b[:, ft * 1024: ft * 1024 + 512],
                                      stg2, AF.Identity),
                             reads=[("PT", 0)], writes=[("HB", ft)], inc=SE)
                        b.w("v", SE, SE.n)
                        b.op("v", TT(harena_b[:, ft * 1024 + 512:
                                              ft * 1024 + 1024], stg2,
                                     harena_b[:, ft * 1024: ft * 1024 + 512],
                                     ALU.subtract),
                             reads=[("PT", 0), ("HB", ft)],
                             writes=[("HB", ft)], inc=VE)
                        ve_split[0] = VE.n
                pe_slab[sl] = PE.n
            b.w("t", SE)
            b.w("t", VE, VE.n)

        def ffn_B(w2d, hsrc_b, ch, evac):
            b.w("t", VE, VE.n)
            pe_slab = [PE.n, PE.n]
            for fg in range(8):
                sl = fg % 2
                wc = WSA_C if sl == 0 else WSB_C
                b.w("y", PE, pe_slab[sl])
                b.op("y", DMA(POOL[:, wc:wc + 4096]
                              .rearrange("p (c f) -> p c f", f=1024),
                              w2d[fg * 512:(fg + 1) * 512, :]
                              .rearrange("(s p) d -> p s d", p=128)),
                     inc=QW[sl], inck=16)
                b.w("s", QW[sl], QW[sl].n)
                stg = TMP0[:, 1024:2048]
                for s4 in range(4):
                    base = 2 * wc + s4 * 2048
                    b.w("s", VE, VE.n)
                    b.op("s", ACT(stg, POOL[:, wc + s4 * 1024:
                                            wc + s4 * 1024 + 1024],
                                  AF.Identity),
                         reads=[("POOL", "wsl", sl)], writes=["TMP0"])
                    b.op("s", ACT(POOLB[:, base: base + 1024], stg, AF.Identity),
                         reads=["TMP0"], writes=[("POOL", "wsl", sl)],
                         inc=SE)
                    b.w("v", SE, SE.n)
                    b.op("v", TT(POOLB[:, base + 1024: base + 2048], stg,
                                 POOLB[:, base: base + 1024], ALU.subtract),
                         reads=["TMP0", ("POOL", "wsl", sl)],
                         writes=[("POOL", "wsl", sl)], inc=VE)
                b.w("t", VE, VE.n)
                for ft4 in range(4):
                    ft = fg * 4 + ft4
                    hh = hsrc_b[:, ft * 1024: ft * 1024 + 512]
                    hl = hsrc_b[:, ft * 1024 + 512: ft * 1024 + 1024]
                    for dc in range(NC8):
                        base = 2 * wc + ft4 * 2048 + dc * 128
                        w_h = POOLB[:, base: base + 128]
                        w_l = POOLB[:, base + 1024: base + 1024 + 128]
                        b.op("t", MM(P[dc][:, 0:512], w_h, hh, ft == 0, False))
                        b.op("t", MM(P[dc][:, 0:512], w_h, hl, False, False))
                        b.op("t", MM(P[dc][:, 0:512], w_l, hh, False, ft == 31),
                             inc=PE if (ft4 == 3 and dc == NC8 - 1) else None)
                pe_slab[sl] = PE.n
            b.w("v", PE)
            b.w("s", PE)
            for dc in range(NC8):
                evac(dc)

        def affine_xnc_pairs(get_src, atag, btag, gnm, bnm, pk_s0, tag):
            rows_from(atag, btag, pk_s0)
            b.w("t", QG, QG.n)
            b.w("t", VE)
            b.op("t", MM(P[3][:, 0:512], ONESR[0:1, :], ROW_A[0:1, 0:512],
                         True, True))
            b.op("t", MM(P[4][:, 0:512], ONESR[0:1, :], ROW_B[0:1, 0:512],
                         True, True), inc=PE)
            b.w("v", PE)
            stg = TMP0[:, 1536:2048]
            for c in range(NC8):
                xb = 2 * XNC_C + c * 1024
                b.op("v", TT(PT[0][:, :], get_src(c), P[3][:, 0:512], ALU.mult),
                     reads=[(tag, "src", c)], writes=[("PT", 0)])
                if gnm is not None:
                    b.op("v", TT(PT[0][:, :], PT[0][:, :], P[4][:, 0:512],
                                 ALU.add), reads=[("PT", 0)], writes=[("PT", 0)])
                    b.op("v", lambda e, o=stg, i=PT[0][:, :], g=prm(gnm, c),
                         bt=prm(bnm, c): e.tensor_scalar(o, i, g, bt, ALU.mult,
                                                         ALU.add),
                         reads=[("PT", 0)], writes=["TMP0"], inc=VE)
                else:
                    b.op("v", TT(stg, PT[0][:, :], P[4][:, 0:512], ALU.add),
                         reads=[("PT", 0)], writes=["TMP0"], inc=VE)
                b.w("s", VE, VE.n)
                b.op("s", ACT(POOLB[:, xb: xb + 512], stg, AF.Identity),
                     reads=["TMP0"], writes=[(tag, "xp", c)], inc=SE)
                b.w("v", SE, SE.n)
                b.op("v", TT(POOLB[:, xb + 512: xb + 1024], stg,
                             POOLB[:, xb: xb + 512], ALU.subtract),
                     reads=["TMP0", (tag, "xp", c)],
                     writes=[(tag, "xp", c)], inc=VE)

        def moe_phase():
            for ch in range(NCH):
                t0 = ch * TCH
                b.w("g", PE, PE.n)
                b.w("g", VE, VE.n)
                for c in range(NC8):
                    b.op("g", DMA(pl(XNC_C + c * 512, 512),
                                  dch("spill")[c][:, t0:t0 + 512]),
                         writes=[("POOL", "xnc", c)], inc=QR[0], inck=16)
                b.w("v", QR[0], QR[0].n)
                affine_xnc_pairs(lambda c: pl(XNC_C + c * 512, 512),
                                 "a", "bb", None, None, ch * 4, "POOLxnc")
                for tt in range(4):
                    b.w("t", VE)
                    for c in range(NC8):
                        xb = 2 * XNC_C + c * 1024
                        xh = POOLB[:, xb + tt * 128: xb + tt * 128 + 128]
                        xl = POOLB[:, xb + 512 + tt * 128: xb + 512 + tt * 128 + 128]
                        gh = GWTBB[:, c * 8:(c + 1) * 8]
                        gl = GWTBB[:, 64 + c * 8: 64 + (c + 1) * 8]
                        b.op("t", MM(P[5][:, 0:8], xh, gh, c == 0, False))
                        b.op("t", MM(P[5][:, 0:8], xh, gl, False, False))
                        b.op("t", MM(P[5][:, 0:8], xl, gh, False, c == NC8 - 1),
                             inc=PE if c == NC8 - 1 else None)
                    b.w("v", PE)
                    b.op("v", CPY(GTb[:, :], P[5][:, 0:8]), writes=["GTb"], inc=VE)
                    b.op("v", RED(GCOL[:, 0:1], GTb[:, :], ALU.max),
                         reads=["GTb"], writes=[("GC", 0)])
                    b.op("v", lambda e: e.tensor_scalar(
                        GS1[:, :], GTb[:, :], GCOL[:, 0:1], None, ALU.is_ge),
                        reads=["GTb", ("GC", 0)], writes=["GS1"])
                    b.op("v", STT(GS2[:, :], GS1[:, :], -1e30, GTb[:, :],
                                  ALU.mult, ALU.add), reads=["GS1", "GTb"],
                         writes=["GS2"])
                    b.op("v", RED(GCOL[:, 1:2], GS2[:, :], ALU.max),
                         reads=["GS2"], writes=[("GC", 1)])
                    b.op("v", lambda e: e.tensor_scalar(
                        GS2[:, :], GS2[:, :], GCOL[:, 1:2], None, ALU.is_ge),
                        reads=["GS2", ("GC", 1)], writes=["GS2"])
                    b.op("v", TT(GCOL[:, 2:3], GCOL[:, 0:1], GCOL[:, 1:2],
                                 ALU.subtract), reads=[("GC", 0), ("GC", 1)],
                         writes=[("GC", 2)], inc=VE)
                    b.w("s", VE)
                    b.op("s", ACT(GCOL[:, 3:4], GCOL[:, 2:3], AF.Sigmoid,
                                  scale=-1.0), reads=[("GC", 2)],
                         writes=[("GC", 3)])
                    b.op("s", ACT(GCOL[:, 2:3], GCOL[:, 2:3], AF.Sigmoid),
                         reads=[("GC", 2)], writes=[("GC", 2)], inc=SE)
                    b.w("v", SE)
                    b.op("v", lambda e: e.tensor_scalar(
                        GS1[:, :], GS1[:, :], GCOL[:, 2:3], None, ALU.mult),
                        reads=["GS1", ("GC", 2)], writes=["GS1"])
                    b.op("v", lambda e: e.scalar_tensor_tensor(
                        GS1[:, :], GS2[:, :], GCOL[:, 3:4], GS1[:, :],
                        ALU.mult, ALU.add),
                        reads=["GS1", "GS2", ("GC", 3)], writes=["GS1"], inc=VE)
                    b.w("t", VE)
                    b.op("t", MM(P[6][0:8, 0:128], GS1[:, :], I128[:, :],
                                 True, True), inc=PE)
                    b.w("v", PE)
                    b.w("v", QG, QG.n)
                    b.op("v", CPY(CTS[:, :], P[6][0:8, 0:128]), writes=["STG32"],
                         inc=VE)
                    b.w("g", VE)
                    b.w("g", PE, PE.n)
                    crd0 = bass.AP(TMP0.tensor, tt * 128, [[2048 * 32, 4], [1, 128]])
                    crd1 = bass.AP(TMP0.tensor, 512 + tt * 128,
                                   [[2048 * 32, 4], [1, 128]])
                    b.op("g", DMA(crd0, CTS[0:4, :]), reads=["STG32"],
                         inc=QG, inck=16)
                    b.op("g", DMA(crd1, CTS[4:8, :]), reads=["STG32"],
                         inc=QG, inck=16)
                b.w("t", QG, QG.n)
                for e_ in range(NE):
                    r0 = 32 * (e_ % 4)
                    crow = TMP0[r0:r0 + 1, (e_ // 4) * 512:(e_ // 4) * 512 + 512]
                    b.w("t", VE)
                    b.op("t", lambda e, o=P[5][:, 0:512],
                         a=ONESR[r0:r0 + 1, :], r=crow, tp=(r0, 0):
                         e.matmul(o, a, r, start=True, stop=True,
                                  tile_position=tp), inc=PE)
                    b.w("v", PE)
                    b.op("v", CPY(PT[1][:, :], P[5][:, 0:512]),
                         writes=[("PT", 1)], inc=VE)
                    b.w("t", VE)
                    ffn_A(dr["w1T"][e_], A1B, None, ch)

                    def moe_evac(dc, t0=t0, ch=ch):
                        b.op("v", TT(PT[0][:, :], P[dc][:, 0:512], PT[1][:, :],
                                     ALU.mult), reads=[("PT", 1)],
                             writes=[("PT", 0)])
                        b.op("v", TT(asl(A2, dc, t0, 512), asl(A2, dc, t0, 512),
                                     PT[0][:, :], ALU.add),
                             reads=[("PT", 0), ("A2", "x1", dc, ch)],
                             writes=[("A2", "x1", dc, ch)], inc=VE)
                    ffn_B(dr["w2T"][e_], A1B, ch, moe_evac)

        moe_phase()
        b.barrier(PH)
        if tap == "x1":
            return finalize(A2)

        stats_pass(A2, D, 0, S, inner_too=True)
        b.barrier(PH)
        affine_apply(lambda c, s: asl(A2, c, s * 512, 512),
                     lambda c, s: asl(A1, c, s * 512, 512),
                     "A", "B", "g1", "b1", 0, S, "a21")
        b.w("g", VE)
        for c in range(NC8):
            b.op("g", DMA(dch("spill")[c], asl(A2, c, 0, S)),
                 reads=[("a21", "spx", c)], inc=QG, inck=16)
        b.w("g", QG)
        b.barrier(PH)
        if tap == "xn2p":
            return finalize(A1)

        attn_layer(True, A1, A2)
        b.barrier(PH)
        if tap == "l1heads":
            return finalize(A2)

        wo_phase(A2, A1, dr["woT1"], add_affine=True)
        b.barrier(PH)
        if tap == "l1y":
            return finalize(A1)

        def mlp_phase():
            for ch in range(NCH):
                t0 = ch * TCH
                stats_pass(A1, D, t0, 512, inner_too=False)
                affine_xnc_pairs(lambda c: asl(A1, c, t0, 512),
                                 "a", "bb", "g2", "b2", 0, "mlph")
                b.w("t", VE)
                ffn_A(dr["mw1T"], A2B, "mb1", ch)

                def mlp_evac(dc, t0=t0, ch=ch):
                    tb = TMP0[:, 0:512] if dc % 2 == 0 else TMP0[:, 512:1024]
                    if dc >= 2:
                        b.w("s", VE, VE.n - 1)
                    b.op("s", ACT(tb, P[dc][:, 0:512], AF.Identity,
                                  bias=prm("mb2", dc)),
                         writes=["TMP0"], inc=SE)
                    b.w("v", SE, SE.n)
                    b.op("v", TT(asl(A1, dc, t0, 512), asl(A1, dc, t0, 512), tb,
                                 ALU.add),
                         reads=["TMP0", ("A1", "acc", dc, ch)],
                         writes=[("A1", "acc", dc, ch)], inc=VE)
                ffn_B(dr["mw2T"], A2B, ch, mlp_evac)

        mlp_phase()
        b.barrier(PH)
        if tap == "l1branch":
            return finalize(A1)

        def quant(wsrc, wdst, axis_d):
            b.w("g", VE, VE.n)
            b.op("g", lambda e: e.memset(QSC[:, 0:1], 0.0),
                 writes=[("QSC", 0)], inc=GE)
            b.w("v", GE, GE.n)

            def slab_aps(sl_):
                if axis_d:
                    src_ = wsrc[:, sl_ * 512:(sl_ + 1) * 512].rearrange(
                        "(c p) f -> p c f", p=128)
                    dst_ = wdst[:, sl_ * 512:(sl_ + 1) * 512].rearrange(
                        "(c p) f -> p c f", p=128)
                    z = 512
                else:
                    src_ = wsrc[sl_ * 512:(sl_ + 1) * 512, :].rearrange(
                        "(s p) d -> p s d", p=128)
                    dst_ = wdst[sl_ * 512:(sl_ + 1) * 512, :].rearrange(
                        "(s p) d -> p s d", p=128)
                    z = 1024
                return src_, dst_, z

            for sl_ in range(8):
                src_, _, z = slab_aps(sl_)
                b.w("y", VE, VE.n)
                b.w("y", QG, QG.n)
                b.op("y", DMA(POOL[:, WSA_C:WSA_C + 4096].rearrange(
                    "p (a z) -> p a z", z=z), src_), inc=QW[0], inck=16)
                b.w("v", QW[0], QW[0].n)
                b.op("v", RED(QSC[:, 1:2], POOL[:, WSA_C:WSA_C + 4096], ALU.add,
                              absv=True), reads=[("POOL", "qs")],
                     writes=[("QSC", 1)])
                b.op("v", TT(QSC[:, 0:1], QSC[:, 0:1], QSC[:, 1:2], ALU.add),
                     reads=[("QSC", 0), ("QSC", 1)], writes=[("QSC", 0)], inc=VE)
            b.w("t", VE)
            b.op("t", MM(P[5][0:1, 0:1], QSC[:, 0:1], ONESC[:, :], True, True),
                 inc=PE)
            b.w("v", PE)
            b.op("v", CPY(QV[0:1, 0:1], P[5][0:1, 0:1]), writes=[("QV", 0)])
            b.op("v", TSC(QV[0:1, 1:2], QV[0:1, 0:1], 1.0 / (D * FF), 1e-5,
                          ALU.mult, ALU.max),
                 reads=[("QV", 0)], writes=[("QV", 1)])
            b.op("v", RECIP(QV[0:1, 2:3], QV[0:1, 1:2]), reads=[("QV", 1)],
                 writes=[("QV", 2)])
            b.op("v", RECIP(QV[0:1, 3:4], QV[0:1, 2:3]), reads=[("QV", 2)],
                 writes=[("QV", 3)], inc=VE)
            b.w("t", VE)
            b.op("t", MM(P[5][:, 0:2], ONESR[0:1, :], QV[0:1, 2:4], True, True),
                 inc=PE)
            b.w("v", PE)
            b.op("v", CPY(SCB[:, :], P[5][:, 0:2]), writes=["SCB"], inc=VE)
            for sl_ in range(8):
                src_, dst_, z = slab_aps(sl_)
                b.w("y", QG, QG.n)
                b.w("y", VE, VE.n)
                b.op("y", DMA(POOL[:, WSA_C:WSA_C + 4096].rearrange(
                    "p (a z) -> p a z", z=z), src_), inc=QW[0], inck=16)
                b.w("v", QW[0], QW[0].n)
                for hh in range(2):
                    seg = POOL[:, WSA_C + hh * 2048: WSA_C + hh * 2048 + 2048]
                    b.op("v", lambda e, o=TMP0[:, :], i=seg: e.tensor_scalar(
                        o, i, SCB[:, 0:1], 0.5, ALU.mult, ALU.is_gt),
                        reads=[("POOL", "qs", hh), "SCB"],
                        writes=["TMP0"])
                    b.op("v", lambda e, o=pl(WSB_C, 2048), i=seg: e.tensor_scalar(
                        o, i, SCB[:, 0:1], -0.5, ALU.mult, ALU.is_lt),
                        reads=[("POOL", "qs", hh), "SCB"],
                        writes=[("POOL", "ql")])
                    b.op("v", TT(seg, TMP0[:, :], pl(WSB_C, 2048), ALU.subtract),
                         reads=["TMP0", ("POOL", "ql")],
                         writes=[("POOL", "qs", hh)])
                    b.op("v", lambda e, o=seg, i=seg: e.tensor_scalar(
                        o, i, SCB[:, 1:2], None, ALU.mult),
                        reads=[("POOL", "qs", hh), "SCB"],
                        writes=[("POOL", "qs", hh)], inc=VE)
                b.w("g", VE)
                b.op("g", DMA(dst_, POOL[:, WSA_C:WSA_C + 4096].rearrange(
                    "p (a z) -> p a z", z=z)),
                    reads=[("POOL", "qs")], inc=QG, inck=16)
            b.w("g", QG)

        quant(dr["fw1T"], dr["fq1"], True)
        quant(dr["fw2T"], dr["fq2"], False)
        b.barrier(PH)

        def l1ffn_phase():
            for ch in range(NCH):
                t0 = ch * TCH
                b.w("g", PE, PE.n)
                b.w("g", VE, VE.n)
                for c in range(NC8):
                    b.op("g", DMA(pl(XNC_C + c * 512, 512),
                                  dch("spill")[c][:, t0:t0 + 512]),
                         writes=[("POOL", "xnc", c)], inc=QR[0], inck=16)
                b.w("v", QR[0], QR[0].n)
                affine_xnc_pairs(lambda c: pl(XNC_C + c * 512, 512),
                                 "A", "B", "fg1", "fb1", ch * 4, "POOLxnc")
                b.w("t", VE)

                def ffnq_evac(ft, bank, ch=ch):
                    b.op("s", ACT(A2[:, ft * 512:(ft + 1) * 512], bank[:, 0:512],
                                  AF.Gelu, bias=prm("fb1v", ft)),
                         writes=[("A2", "h", ft, ch)], inc=SE)
                ffn_A(dr["fq1"], None, None, ch, evac_fp32=ffnq_evac)
                b.w("s", VE, VE.n)
                b.drain("s")
                b.w("t", VE, VE.n)
                sqw = [TMP0[:, i * 512:(i + 1) * 512] for i in range(4)]
                pe_sq = [0] * 4
                for g4 in range(8):
                    for i in range(4):
                        ft = g4 * 4 + i
                        b.w("s", PE, pe_sq[i])
                        b.op("s", ACT(sqw[i], A2[:, ft * 512:(ft + 1) * 512],
                                      AF.Square), reads=[("A2", "h", ft, ch)],
                             writes=[("SQ", i)], inc=SE)
                    se_mark = SE.n
                    for tt in range(4):
                        for i in range(4):
                            ft = g4 * 4 + i
                            b.op("t", MM(P[tt][:, 0:1],
                                         A2[:, ft * 512 + tt * 128:
                                            ft * 512 + (tt + 1) * 128],
                                         ONESC[:, :], ft == 0, ft == 31))
                    b.w("t", SE, se_mark)
                    for tt in range(4):
                        for i in range(4):
                            ft = g4 * 4 + i
                            b.op("t", MM(P[4 + tt][:, 0:1],
                                         sqw[i][:, tt * 128:(tt + 1) * 128],
                                         ONESC[:, :], ft == 0, ft == 31),
                                 inc=PE if (tt == 3 and i == 3) else None)
                    for i in range(4):
                        pe_sq[i] = PE.n
                b.w("v", PE)
                for tt in range(4):
                    b.op("v", CPY(pkv("s1")[:, tt:tt + 1], P[tt][:, 0:1]),
                         writes=[("PK", "s1")])
                    b.op("v", CPY(pkv("s2")[:, tt:tt + 1], P[4 + tt][:, 0:1]),
                         writes=[("PK", "s2")],
                         inc=VE if tt == 3 else None)
                _stats_math(4, float(FF), False)
                rows_from("a", "bb", 0)
                b.w("t", QG, QG.n)
                b.w("t", VE)
                b.op("t", MM(P[3][:, 0:512], ONESR[0:1, :], ROW_A[0:1, 0:512],
                             True, True))
                b.op("t", MM(P[4][:, 0:512], ONESR[0:1, :], ROW_B[0:1, 0:512],
                             True, True), inc=PE)
                b.w("v", PE)
                stgh = TMP0[:, 1536:2048]
                for ft in range(32):
                    b.op("v", TT(PT[0][:, :], A2[:, ft * 512:(ft + 1) * 512],
                                 P[3][:, 0:512], ALU.mult),
                         reads=[("A2", "h", ft, ch)], writes=[("PT", 0)])
                    b.op("v", TT(PT[0][:, :], PT[0][:, :], P[4][:, 0:512],
                                 ALU.add), reads=[("PT", 0)], writes=[("PT", 0)])
                    b.op("v", lambda e, o=stgh, i=PT[0][:, :], g=prm("fg2", ft),
                         bt=prm("fb2", ft):
                         e.tensor_scalar(o, i, g, bt, ALU.mult, ALU.add),
                         reads=[("PT", 0)], writes=["TMP0"], inc=VE)
                    b.w("s", VE, VE.n)
                    b.op("s", ACT(A2B[:, ft * 1024: ft * 1024 + 512], stgh,
                                  AF.Identity),
                         reads=["TMP0"], writes=[("A2", "h", ft, ch)], inc=SE)
                    b.w("v", SE, SE.n)
                    b.op("v", TT(A2B[:, ft * 1024 + 512: ft * 1024 + 1024], stgh,
                                 A2B[:, ft * 1024: ft * 1024 + 512],
                                 ALU.subtract),
                         reads=["TMP0", ("A2", "h", ft, ch)],
                         writes=[("A2", "h", ft, ch)], inc=VE)

                def ffn_evac(dc, t0=t0, ch=ch):
                    tb = TMP0[:, 0:512] if dc % 2 == 0 else TMP0[:, 512:1024]
                    if dc >= 2:
                        b.w("s", VE, VE.n - 1)
                    b.op("s", ACT(tb, P[dc][:, 0:512], AF.Identity,
                                  bias=prm("fb2v", dc)),
                         writes=["TMP0"], inc=SE)
                    b.w("v", SE, SE.n)
                    b.op("v", TT(asl(A1, dc, t0, 512), asl(A1, dc, t0, 512), tb,
                                 ALU.add),
                         reads=["TMP0", ("A1", "acc2", dc, ch)],
                         writes=[("A1", "acc2", dc, ch)], inc=VE)
                ffn_B(dr["fq2"], A2B, ch, ffn_evac)

        l1ffn_phase()
        b.barrier(PH)

        for c in range(NC8):
            half = c % 2
            rl = pl(XNC_C + half * 2048, 2048)
            ob = pl(WSA_C + half * 2048, 2048)
            qr = QR[half]
            b.w("g", VE, VE.n)
            b.op("g", DMA(rl, dch("spill")[c]), writes=[("POOL", "rl", half)],
                 inc=qr, inck=16)
            b.w("v", qr, qr.n)
            b.w("v", QO, QO.n)
            b.op("v", TT(ob, rl, asl(A1, c, 0, S), ALU.add),
                 reads=[("POOL", "rl", half), ("A1", "fin", c)],
                 writes=[("POOL", "ob", half)], inc=VE)
            b.w("g", VE)
            b.op("g", DMA(dch("outT")[c], ob), reads=[("POOL", "ob", half)],
                 inc=QO, inck=16)
        b.w("g", QO)
        b.w("y", QO)
        block = es.enter_context(nc.Block())
        b.replay(block)
    return nc


def _alibi_table():
    xu = np.array([1060439283, 1060168325, 1059866229, 1059527426, 1059144984,
                   1058710189, 1058211980, 1057636198, 1056964608, 1055382913,
                   1053503477, 1051256226, 1048576000, 1042367767, 1031798784,
                   998244352], dtype=np.uint32)
    sl = xu.view(np.float32)
    p = np.arange(128)
    c = np.arange(ESCW)
    d32 = -np.abs(p[:, None] - c[None, :] + K0).astype(np.float32)
    return np.ascontiguousarray((sl[:, None, None] * d32[None]).astype(np.float32))



def _prep_inputs(ins):
    f32 = lambda a: np.ascontiguousarray(np.asarray(a), dtype=np.float32)
    pack = lambda v: np.ascontiguousarray(f32(v).reshape(-1, 128).T)
    prm = np.zeros((128, 224), np.float32)
    off = 0
    for nm, wd, val in [
            ("g0a", 8, ins["L0_attn_ln_gamma"]), ("b0a", 8, ins["L0_attn_ln_beta"]),
            ("g1", 8, ins["L1_attn_ln1_g"]), ("b1", 8, ins["L1_attn_ln1_b"]),
            ("g2", 8, ins["L1_attn_ln2_g"]), ("b2", 8, ins["L1_attn_ln2_b"]),
            ("mb1", 32, ins["L1_attn_mlp_b1"]), ("mb2", 8, ins["L1_attn_mlp_b2"]),
            ("fg1", 8, ins["L1_ffn_ln1_g"]), ("fb1", 8, ins["L1_ffn_ln1_b"]),
            ("fb1v", 32, ins["L1_ffn_b1"]), ("fg2", 32, ins["L1_ffn_ln2_g"]),
            ("fb2", 32, ins["L1_ffn_ln2_b"]), ("fb2v", 8, ins["L1_ffn_b2"])]:
        prm[:, off:off + wd] = pack(val)
        off += wd
    shared = {
        "wq": f32(ins["L0_attn_Wq"]),
        "wk": f32(ins["L0_attn_Wk"]),
        "wv": f32(ins["L0_attn_Wv"]),
        "wo": f32(ins["L0_attn_Wo"]),
        "gwT": np.ascontiguousarray(f32(ins["L0_ffn_gate_w"]).T),
        "w1T": np.ascontiguousarray(f32(ins["L0_ffn_w1"]).transpose(0, 2, 1)),
        "w2T": np.ascontiguousarray(f32(ins["L0_ffn_w2"]).transpose(0, 2, 1)),
        "qkvT": np.ascontiguousarray(f32(ins["L1_attn_w_qkv"]).T),
        "woT1": np.ascontiguousarray(f32(ins["L1_attn_w_o"]).T),
        "mw1T": np.ascontiguousarray(f32(ins["L1_attn_mlp_w1"]).T),
        "mw2T": np.ascontiguousarray(f32(ins["L1_attn_mlp_w2"]).T),
        "fw1T": np.ascontiguousarray(f32(ins["L1_ffn_w1"]).T),
        "fw2T": np.ascontiguousarray(f32(ins["L1_ffn_w2"]).T),
        "prm": prm,
        "alibi": _alibi_table(),
    }
    x = f32(ins["L0_attn_x"])
    in_maps = []
    for c in range(8):
        m = dict(shared)
        m["xT"] = np.ascontiguousarray(x[c].T)
        in_maps.append(m)
    return in_maps


def kernel(**inputs):
    in_maps = _prep_inputs(inputs)
    nc = build_program(tap=TAP)
    res = run_bass_kernel_spmd(nc, in_maps, core_ids=list(range(8)))
    return np.stack([np.asarray(r["outT"]).T for r in res.results])


# version 07
# speedup vs baseline: 1.2465x; 1.2465x over previous
import contextlib
import numpy as np
import concourse.bass as bass
import concourse.mybir as mybir
from concourse.bass_utils import run_bass_kernel_spmd

F32 = mybir.dt.float32
BF16 = mybir.dt.bfloat16
AF = mybir.ActivationFunctionType
ALU = mybir.AluOpType
AX = mybir.AxisListType

D, S, NC8, H, DH, FF, NE = 1024, 2048, 8, 16, 64, 4096, 8
TCH, NCH = 512, 4
C0, C1, EPS = 8.0, 4.0, 1e-5
K0, ESCW = 1920, 3968

TAP = None


class Sem:
    __slots__ = ("h", "n")

    def __init__(self, h):
        self.h = h
        self.n = 0


def MM(out, lhsT, rhs, start, stop):
    return lambda e, o=out, a=lhsT, r=rhs, s1=start, s2=stop: e.matmul(
        o, a, r, start=s1, stop=s2)


def CPY(out, in_):
    return lambda e, o=out, i=in_: e.tensor_copy(o, i)


def TT(out, a, b_, op):
    return lambda e, o=out, x=a, y=b_, p=op: e.tensor_tensor(o, x, y, p)


def TSC(out, in_, s1, s2, op0, op1=None):
    if op1 is None:
        return lambda e, o=out, i=in_, a=s1, p=op0: e.tensor_scalar(o, i, a, None, p)
    return lambda e, o=out, i=in_, a=s1, c=s2, p=op0, q=op1: e.tensor_scalar(
        o, i, a, c, p, q)


def STT(out, in0, sc, in1, op0, op1):
    return lambda e, o=out, i=in0, s=sc, j=in1, p=op0, q=op1: \
        e.scalar_tensor_tensor(o, i, s, j, p, q)


def ACT(out, in_, func, bias=0.0, scale=1.0, accum_out=None):
    return lambda e, o=out, i=in_, f=func, b=bias, s=scale, a=accum_out: \
        e.activation(o, i, f, bias=b, scale=s, accum_out=a)


def DMA(out, in_):
    return lambda e, o=out, i=in_: e.dma_start(out=o, in_=i)


def RED(out, in_, op, absv=False):
    return lambda e, o=out, i=in_, p=op, a=absv: e.tensor_reduce(
        o, i, AX.X, p, apply_absolute_value=a)


def RECIP(out, in_):
    return lambda e, o=out, i=in_: e.reciprocal(o, i)


class Bld:
    def __init__(self, nc):
        self.nc = nc
        self.q = {k: [] for k in "ytvsg"}
        self.rd = {k: set() for k in "vsg"}
        self.wr = {k: set() for k in "vsg"}

    def drain(self, ek):
        if ek in self.rd:
            self.rd[ek].clear()
            self.wr[ek].clear()
        self.q[ek].append(lambda e: e.drain())

    def op(self, ek, fn, reads=(), writes=(), inc=None, inck=1):
        if ek in self.rd:
            rs, ws = set(reads), set(writes)
            if (rs & self.wr[ek]) or (ws & (self.wr[ek] | self.rd[ek])):
                self.q[ek].append(lambda e: e.drain())
                self.rd[ek].clear()
                self.wr[ek].clear()
            self.rd[ek] |= rs
            self.wr[ek] |= ws
        if inc is not None:
            inc.n += inck
            self.q[ek].append(lambda e, f=fn, s=inc.h, k=inck: f(e).then_inc(s, k))
        else:
            self.q[ek].append(fn)

    def w(self, ek, sem, val=None):
        v = sem.n if val is None else val
        if v <= 0:
            return
        self.q[ek].append(lambda e, s=sem.h, vv=v: e.wait_ge(s, vv))

    def barrier(self, ph):
        for ek in "ytvsg":
            if ek in self.rd:
                self.rd[ek].clear()
                self.wr[ek].clear()
            self.q[ek].append(lambda e: e.drain())
            ph.n += 1
            self.q[ek].append(lambda e, s=ph.h: e.sem_inc(s, 1))
        for ek in "ytvsg":
            self.q[ek].append(lambda e, s=ph.h, v=ph.n: e.wait_ge(s, v))

    def replay(self, block):
        q = self.q

        @block.sync
        def _(e):
            for f in q["y"]:
                f(e)

        @block.tensor
        def _(e):
            for f in q["t"]:
                f(e)

        @block.vector
        def _(e):
            for f in q["v"]:
                f(e)

        @block.scalar
        def _(e):
            for f in q["s"]:
                f(e)

        @block.gpsimd
        def _(e):
            for f in q["g"]:
                f(e)


def build_program(tap=None):
    nc = bass.Bass("TRN2", target_bir_lowering=False)
    dtn = lambda n, sh, kind: nc.dram_tensor(n, sh, F32, kind=kind).ap()
    dr = {
        "xT": dtn("xT", [D, S], "ExternalInput"),
        "wq": dtn("wq", [D, D], "ExternalInput"),
        "wk": dtn("wk", [D, D], "ExternalInput"),
        "wv": dtn("wv", [D, D], "ExternalInput"),
        "wo": dtn("wo", [D, D], "ExternalInput"),
        "gwT": dtn("gwT", [D, NE], "ExternalInput"),
        "w1T": dtn("w1T", [NE, D, FF], "ExternalInput"),
        "w2T": dtn("w2T", [NE, FF, D], "ExternalInput"),
        "qkvT": dtn("qkvT", [D, 3 * D], "ExternalInput"),
        "woT1": dtn("woT1", [D, D], "ExternalInput"),
        "mw1T": dtn("mw1T", [D, FF], "ExternalInput"),
        "mw2T": dtn("mw2T", [FF, D], "ExternalInput"),
        "fw1T": dtn("fw1T", [D, FF], "ExternalInput"),
        "fw2T": dtn("fw2T", [FF, D], "ExternalInput"),
        "prm": dtn("prm", [128, 224], "ExternalInput"),
        "alibi": dtn("alibi", [H, 128, ESCW], "ExternalInput"),
        "spill": dtn("spill", [D, S], "Internal"),
        "fq1": dtn("fq1", [D, FF], "Internal"),
        "fq2": dtn("fq2", [FF, D], "Internal"),
        "outT": dtn("outT", [D, S], "ExternalOutput"),
        "wpcA": nc.dram_tensor("wpcA", [10, 8, 128, 8192], BF16,
                               kind="Internal").ap(),
        "wpcB": nc.dram_tensor("wpcB", [10, 8, 128, 8192], BF16,
                               kind="Internal").ap(),
    }
    dch = lambda name: dr[name].rearrange("(c p) t -> c p t", p=128)
    wpc = lambda ci, fg: (dr["wpcA"][ci, fg] if ci < 10
                          else dr["wpcB"][ci - 10, fg])

    es = contextlib.ExitStack()
    with es:
        sbt = lambda n, sh: es.enter_context(nc.sbuf_tensor(n, sh, F32)).ap()
        pst = lambda n, sh: es.enter_context(nc.psum_tensor(n, sh, F32)).ap()
        A1 = sbt("A1", [128, NC8 * S])
        A2 = sbt("A2", [128, NC8 * S])
        POOL = sbt("POOL", [128, 12288])
        PT = [sbt("PTa", [128, 512]), sbt("PTb", [128, 512])]
        TMP0 = sbt("TMP0", [128, 2048])
        PK = sbt("PK", [128, 192])
        STG32 = sbt("STG32", [32, 128])
        ROW_A = sbt("ROW_A", [1, 512])
        ROW_B = sbt("ROW_B", [1, 512])
        ROWN = sbt("ROWN", [1, 512])
        GTb = sbt("GTb", [128, 8])
        GS1 = sbt("GS1", [128, 8])
        GS2 = sbt("GS2", [128, 8])
        GCOL = sbt("GCOL", [128, 4])
        I128 = sbt("I128", [128, 128])
        MBAND = sbt("MBAND", [128, 1408])
        ONESR = sbt("ONESR", [97, 128])
        ONESC = sbt("ONESC", [128, 1])
        CONSTS = sbt("CONSTS", [128, 8])
        PRM = sbt("PRM", [128, 224])
        GWT = sbt("GWT", [128, 64])
        GWTB = sbt("GWTB", [128, 64])
        QV = sbt("QV", [1, 4])
        SCB = sbt("SCB", [128, 2])
        QSC = sbt("QSC", [128, 2])
        CTS = STG32[0:8, :]
        P = [pst("P%d" % i, [128, 512]) for i in range(8)]

        SMN = ["PH", "QW0", "QW1", "QX", "QG", "QR0", "QR1", "QO",
               "PE", "VE", "SE", "GE", "QC0", "QC1"]
        SM = {n: Sem(es.enter_context(nc.semaphore(n))) for n in SMN}
        PH, QX, QG, QO = SM["PH"], SM["QX"], SM["QG"], SM["QO"]
        PE, VE, SE, GE = SM["PE"], SM["VE"], SM["SE"], SM["GE"]
        QW = [SM["QW0"], SM["QW1"]]
        QR = [SM["QR0"], SM["QR1"]]
        QC = [SM["QC0"], SM["QC1"]]
        qcg = [0, 0]

        b = Bld(nc)

        CC = {"eps": 0, "negC0": 1, "negC1": 2, "k0": 3}
        cc = lambda n: CONSTS[:, CC[n]:CC[n] + 1]
        PKC = {n: i * 16 for i, n in enumerate(
            ["s1", "s2", "m", "v", "a", "bb", "A", "B", "t0", "t1", "t2", "t3"])}
        pkv = lambda n, w=16: PK[:, PKC[n]:PKC[n] + w]
        PRL = [("g0a", 8), ("b0a", 8), ("g1", 8), ("b1", 8), ("g2", 8), ("b2", 8),
               ("mb1", 32), ("mb2", 8), ("fg1", 8), ("fb1", 8), ("fb1v", 32),
               ("fg2", 32), ("fb2", 32), ("fb2v", 8)]
        PRO = {}
        _o = 0
        for _nm, _wd in PRL:
            PRO[_nm] = _o
            _o += _wd
        assert _o <= 224
        prm = lambda nm, j: PRM[:, PRO[nm] + j:PRO[nm] + j + 1]

        QKVW_C, QT_C, KT_C, VAUG_C, ESC_C = 0, 3072, 5120, 7168, 8208
        WSA_C, WSB_C, XNC_C = 0, 4096, 8192
        pl = lambda c0, w, r=128: POOL[0:r, c0:c0 + w]
        asl = lambda ar, c, t0, w: ar[:, c * S + t0: c * S + t0 + w]
        POOLB = POOL.bitcast(BF16)
        A1B = A1.bitcast(BF16)
        A2B = A2.bitcast(BF16)
        GWTBB = GWTB.bitcast(BF16)
        plb = lambda c0, w, r=128: POOLB[0:r, c0:c0 + w]

        def mm_group(psum_ap, pairs, inc=None):
            n = len(pairs)
            for i, (a_, r_) in enumerate(pairs):
                b.op("t", MM(psum_ap, a_, r_, i == 0, i == n - 1),
                     inc=inc if i == n - 1 else None)

        b.op("y", DMA(PRM[:, :], dr["prm"][:, :]), inc=QX, inck=16)
        b.op("y", DMA(GWT[:, :].rearrange("p (c f) -> p c f", f=8),
                      dr["gwT"].rearrange("(c p) f -> p c f", p=128)), inc=QX, inck=16)
        for c in range(NC8):
            b.op("y", DMA(asl(A1, c, 0, S), dch("xT")[c]), inc=QX, inck=16)
        for nm, val in [("eps", EPS), ("negC0", -C0), ("negC1", -C1),
                        ("k0", float(K0))]:
            b.op("g", lambda e, a=cc(nm), v=val: e.memset(a, v),
                 writes=[("C", nm)], inc=GE)
        b.op("g", lambda e: e.memset(PK[:, :], 0.0), writes=["PKZ"], inc=GE)
        for _r in (0, 32, 64, 96):
            b.op("g", lambda e, a=ONESR[_r:_r + 1, :]: e.memset(a, 1.0),
                 writes=["ONESR"], inc=GE)
        b.op("g", lambda e: e.memset(ONESC[:, :], 1.0), writes=["ONESC"], inc=GE)
        b.op("g", lambda e: e.iota(I128[:, :], [[1, 128]], channel_multiplier=-1,
                                   allow_small_or_imprecise_dtypes=True),
             writes=["I128"], inc=GE)
        b.op("g", lambda e: e.iota(MBAND[:, :], [[1, 1408]], channel_multiplier=-1,
                                   allow_small_or_imprecise_dtypes=True),
             writes=["MBAND"], inc=GE)
        for t in range(16):
            b.op("g", lambda e, a=pl(VAUG_C + t * 65 + 64, 1): e.memset(a, 1.0),
                 writes=[("VAUG1", t)], inc=GE)
        b.w("v", GE)
        b.op("v", TSC(I128[:, :], I128[:, :], 0.0, None, ALU.is_equal),
             reads=["I128"], writes=["I128"], inc=VE)
        b.op("v", TSC(TMP0[:, 0:1408], MBAND[:, :], 384.0, None, ALU.is_ge),
             reads=["MBAND"], writes=["TMP0"])
        b.op("v", TSC(pl(ESC_C, 1408), MBAND[:, :], 896.0, None, ALU.is_lt),
             reads=["MBAND"], writes=[("POOL", "mb")])
        b.op("v", TT(MBAND[:, :], TMP0[:, 0:1408], pl(ESC_C, 1408), ALU.mult),
             reads=["TMP0", ("POOL", "mb")], writes=["MBAND"], inc=VE)
        b.w("v", QX)
        b.op("v", CPY(GWTBB[:, 0:64], GWT[:, :]), reads=["GWT"],
             writes=[("GWTB", 0)])
        b.op("v", TT(GWTBB[:, 64:128], GWT[:, :], GWTBB[:, 0:64], ALU.subtract),
             reads=["GWT", ("GWTB", 0)], writes=[("GWTB", 1)], inc=VE)
        b.w("t", QX)
        b.w("s", QX)
        b.barrier(PH)

        def _stats_math(nct, nf, inner_too):
            b.op("v", TSC(pkv("m")[:, 0:nct], pkv("s1")[:, 0:nct], 1.0 / nf, None,
                          ALU.mult), reads=[("PK", "s1")], writes=[("PK", "m")])
            b.op("v", TSC(pkv("v")[:, 0:nct], pkv("s2")[:, 0:nct], 1.0 / nf, None,
                          ALU.mult), reads=[("PK", "s2")], writes=[("PK", "v")])
            b.op("v", TT(pkv("t0")[:, 0:nct], pkv("m")[:, 0:nct], pkv("m")[:, 0:nct],
                         ALU.mult), reads=[("PK", "m")], writes=[("PK", "t0")])
            b.op("v", TT(pkv("v")[:, 0:nct], pkv("v")[:, 0:nct], pkv("t0")[:, 0:nct],
                         ALU.subtract), reads=[("PK", "t0"), ("PK", "v")],
                 writes=[("PK", "v")], inc=VE)
            b.w("s", VE)
            b.op("s", ACT(pkv("t0")[:, 0:nct], pkv("v")[:, 0:nct], AF.Ln,
                          bias=cc("eps")), reads=[("PK", "v")], writes=[("PK", "t0")])
            b.op("s", ACT(pkv("a")[:, 0:nct], pkv("t0")[:, 0:nct], AF.Exp, scale=-0.5),
                 reads=[("PK", "t0")], writes=[("PK", "a")], inc=SE)
            b.w("v", SE)
            b.op("v", STT(pkv("bb")[:, 0:nct], pkv("m")[:, 0:nct], -1.0,
                          pkv("a")[:, 0:nct], ALU.mult, ALU.mult),
                 reads=[("PK", "m"), ("PK", "a")], writes=[("PK", "bb")], inc=VE)
            if inner_too:
                b.op("v", TSC(pkv("t1")[:, 0:nct], pkv("v")[:, 0:nct], EPS, None,
                              ALU.add), reads=[("PK", "v")], writes=[("PK", "t1")])
                b.op("v", RECIP(pkv("t2")[:, 0:nct], pkv("t1")[:, 0:nct]),
                     reads=[("PK", "t1")], writes=[("PK", "t2")])
                b.op("v", TT(pkv("t1")[:, 0:nct], pkv("v")[:, 0:nct],
                             pkv("t2")[:, 0:nct], ALU.mult),
                     reads=[("PK", "v"), ("PK", "t2")], writes=[("PK", "t1")], inc=VE)
                b.w("s", VE)
                b.op("s", ACT(pkv("t2")[:, 0:nct], pkv("t1")[:, 0:nct], AF.Ln,
                              bias=cc("eps")), reads=[("PK", "t1")],
                     writes=[("PK", "t2")])
                b.op("s", ACT(pkv("t1")[:, 0:nct], pkv("t2")[:, 0:nct], AF.Exp,
                              scale=-0.5), reads=[("PK", "t2")],
                     writes=[("PK", "t1")], inc=SE)
                b.w("v", SE)
                b.op("v", TT(pkv("A")[:, 0:nct], pkv("a")[:, 0:nct],
                             pkv("t1")[:, 0:nct], ALU.mult),
                     reads=[("PK", "a"), ("PK", "t1")], writes=[("PK", "A")])
                b.op("v", TT(pkv("B")[:, 0:nct], pkv("bb")[:, 0:nct],
                             pkv("t1")[:, 0:nct], ALU.mult),
                     reads=[("PK", "bb"), ("PK", "t1")], writes=[("PK", "B")], inc=VE)

        def stats_pass(src, n_feat, tok0, tokens, inner_too):
            nct = tokens // 128
            nsl_ = tokens // 512
            b.w("t", VE, VE.n)
            b.w("s", VE, VE.n)
            b.drain("s")
            sqw = [TMP0[:, i * 512:(i + 1) * 512] for i in range(4)]
            pe_mark = [0] * 4
            for s in range(nsl_):
                t0 = tok0 + s * 512
                for wave in range(2):
                    cs = [4 * wave + i for i in range(4)]
                    for i, c in enumerate(cs):
                        b.w("s", PE, pe_mark[i])
                        b.op("s", ACT(sqw[i], asl(src, c, t0, 512), AF.Square),
                             reads=[(src.name, "st", c)], writes=[("SQ", i)],
                             inc=SE)
                    se_mark = SE.n
                    for tt in range(4):
                        for c in cs:
                            b.op("t", MM(P[tt][:, s:s + 1],
                                         asl(src, c, t0 + tt * 128, 128),
                                         ONESC[:, :], c == 0, c == NC8 - 1))
                    b.w("t", SE, se_mark)
                    for tt in range(4):
                        for i, c in enumerate(cs):
                            b.op("t", MM(P[4 + tt][:, s:s + 1],
                                         sqw[i][:, tt * 128:(tt + 1) * 128],
                                         ONESC[:, :], c == 0, c == NC8 - 1),
                                 inc=PE if (tt == 3 and i == 3) else None)
                    for i in range(4):
                        pe_mark[i] = PE.n
            b.w("v", PE)
            for tt in range(4):
                dstp = pkv("s1").rearrange("p (s t) -> p s t", t=4)[:, 0:nsl_,
                                                                   tt:tt + 1]
                srcp = P[tt][:, 0:nsl_].rearrange("p (s o) -> p s o", o=1)
                b.op("v", CPY(dstp, srcp), writes=[("PK", "s1")])
                dstq = pkv("s2").rearrange("p (s t) -> p s t", t=4)[:, 0:nsl_,
                                                                   tt:tt + 1]
                srcq = P[4 + tt][:, 0:nsl_].rearrange("p (s o) -> p s o", o=1)
                b.op("v", CPY(dstq, srcq), writes=[("PK", "s2")],
                     inc=VE if tt == 3 else None)
            _stats_math(nct, float(n_feat), inner_too)

        def rows_from(atag, btag, scol):
            b.op("v", CPY(pkv("t2")[:, 0:4], pkv(atag)[:, scol:scol + 4]),
                 reads=[("PK", atag)], writes=[("PK", "t2")])
            b.op("v", CPY(pkv("t3")[:, 0:4], pkv(btag)[:, scol:scol + 4]),
                 reads=[("PK", btag)], writes=[("PK", "t3")], inc=VE)
            b.w("t", VE)
            b.w("t", QG, QG.n)
            b.op("t", MM(P[2][0:32, 0:128], PK[:, PKC["t2"]:PKC["t3"] + 16],
                         I128[:, :], True, True), inc=PE)
            b.w("v", PE)
            b.w("v", QG, QG.n)
            b.op("v", CPY(STG32[:, :], P[2][0:32, 0:128]), writes=["STG32"], inc=VE)
            b.w("g", VE)
            b.op("g", DMA(ROW_A[0:1, 0:512], STG32[0:4, :]),
                 reads=["STG32"], writes=["ROW_A"], inc=QG, inck=16)
            b.op("g", DMA(ROW_B[0:1, 0:512], STG32[16:20, :]),
                 reads=["STG32"], writes=["ROW_B"], inc=QG, inck=16)

        def affine_apply(get_src, get_dst, atag, btag, gnm, bnm, pk_s0,
                         tokens, tag):
            nsl_ = tokens // 512
            for s in range(nsl_):
                rows_from(atag, btag, pk_s0 + s * 4)
                b.w("t", QG, QG.n)
                b.w("t", VE)
                b.op("t", MM(P[3][:, 0:512], ONESR[0:1, :], ROW_A[0:1, 0:512],
                             True, True))
                b.op("t", MM(P[4][:, 0:512], ONESR[0:1, :], ROW_B[0:1, 0:512],
                             True, True), inc=PE)
                b.w("v", PE)
                for c in range(NC8):
                    b.op("v", TT(PT[0][:, :], get_src(c, s), P[3][:, 0:512],
                                 ALU.mult), reads=[(tag, "src", c, s)],
                         writes=[("PT", 0)])
                    if gnm is not None:
                        b.op("v", TT(PT[0][:, :], PT[0][:, :], P[4][:, 0:512],
                                     ALU.add), reads=[("PT", 0)], writes=[("PT", 0)])
                        b.op("v", lambda e, o=get_dst(c, s), i=PT[0][:, :],
                             g=prm(gnm, c), bt=prm(bnm, c):
                             e.tensor_scalar(o, i, g, bt, ALU.mult, ALU.add),
                             reads=[("PT", 0)], writes=[(tag, "dst", c, s)], inc=VE)
                    else:
                        b.op("v", TT(get_dst(c, s), PT[0][:, :], P[4][:, 0:512],
                                     ALU.add), reads=[("PT", 0)],
                             writes=[(tag, "dst", c, s)], inc=VE)

        stats_pass(A1, D, 0, S, inner_too=True)
        b.barrier(PH)
        affine_apply(lambda c, s: asl(A1, c, s * 512, 512),
                     lambda c, s: asl(A2, c, s * 512, 512),
                     "A", "B", "g0a", "b0a", 0, S, "a12")
        b.w("g", VE)
        for c in range(NC8):
            b.op("g", DMA(dch("spill")[c], asl(A1, c, 0, S)),
                 reads=[("a12", "spx", c)], inc=QG, inck=16)
        b.w("g", QG)
        b.barrier(PH)

        def finalize(src):
            for c in range(NC8):
                b.op("g", DMA(dch("outT")[c], asl(src, c, 0, S)),
                     reads=[(src.name, "fin", c)], inc=QO, inck=16)
            b.w("g", QO)
            b.w("y", QO)
            block = es.enter_context(nc.Block())
            b.replay(block)
            return nc

        if tap == "xn2":
            return finalize(A2)

        def ldw_qkv(hd, slot, l1):
            base = QKVW_C + slot * 1536
            srcs = ([(dr["qkvT"], hd * DH), (dr["qkvT"], D + hd * DH),
                     (dr["qkvT"], 2 * D + hd * DH)] if l1 else
                    [(dr["wq"], hd * DH), (dr["wk"], hd * DH),
                     (dr["wv"], hd * DH)])
            for j, (mat, col0) in enumerate(srcs):
                src = mat[:, col0:col0 + DH].rearrange("(c p) f -> p c f", p=128)
                dst = POOL[:, base + j * 512: base + j * 512 + 512].rearrange(
                    "p (c f) -> p c f", f=DH)
                b.op("y", DMA(dst, src), inc=QW[slot], inck=16)

        qkvw = lambda slot, j, c: POOL[:, QKVW_C + slot * 1536 + j * 512 + c * DH:
                                       QKVW_C + slot * 1536 + j * 512 + (c + 1) * DH]

        def head_attn(l1, src, oall, hd):
            slot = hd % 2
            slope = float(2.0 ** (-8.0 / (H - hd)))
            b.w("t", QW[slot], QW[slot].n)
            ve_bank = {}
            b.w("t", VE, VE.n)
            b.w("t", SE, SE.n)
            for j, dc0 in ((0, QT_C), (1, KT_C)):
                for s in range(4):
                    bi = (j * 4 + s) % 2
                    bank = P[4 + bi]
                    if (j * 4 + s) >= 2:
                        b.w("t", VE, ve_bank[bi])
                    mm_group(bank[0:64, 0:512],
                             [(qkvw(slot, j, c), asl(src, c, s * 512, 512))
                              for c in range(NC8)], inc=PE)
                    b.w("v", PE, PE.n)
                    b.op("v", CPY(POOL[0:64, dc0 + s * 512: dc0 + (s + 1) * 512],
                                  bank[0:64, 0:512]),
                         writes=[("POOL", "qk", j, s)], inc=VE)
                    ve_bank[bi] = VE.n
            for half in range(2):
                bank = P[6 + half]
                for tt in range(8):
                    t8 = half * 8 + tt
                    mm_group(bank[:, tt * 64:(tt + 1) * 64],
                             [(asl(src, c, t8 * 128, 128), qkvw(slot, 2, c))
                              for c in range(NC8)],
                             inc=PE if tt == 7 else None)
                b.w("v", PE, PE.n)
                sap = bank[:, 0:512].rearrange("p (t f) -> p t f", f=64)
                dap = POOL[:, VAUG_C + half * 520: VAUG_C + half * 520 + 520]\
                    .rearrange("p (t f) -> p t f", f=65)[:, :, 0:64]
                b.op("v", CPY(dap, sap), writes=[("POOL", "vaug", half)], inc=VE)
            if not l1:
                b.w("y", VE, VE.n)
                b.w("y", SE, SE.n)
                b.op("y", DMA(pl(ESC_C, ESCW), dr["alibi"][hd]), inc=QX, inck=16)
                b.w("s", QX)
                b.op("s", ACT(pl(ESC_C, ESCW), pl(ESC_C, ESCW), AF.Exp),
                     reads=[("POOL", "esc")], writes=[("POOL", "esc")], inc=SE)
                b.w("v", SE)
            negC = cc("negC1") if l1 else cc("negC0")
            SB = [P[0], P[1], P[5], P[6]]
            PTB = [PT[0], PT[1], TMP0[:, 512:1024], TMP0[:, 1024:1536]]
            PTK = [("PT", 0), ("PT", 1), "TMP0", "TMP0"]
            pending_evac = [None]

            def emit_evac():
                if pending_evac[0] is None:
                    return
                pvbank, hd_, qj_, pe_pv = pending_evac[0]
                pending_evac[0] = None
                b.w("s", PE, pe_pv)
                b.op("s", ACT(ROWN[0:1, :], pvbank[64:65, 0:512], AF.Ln),
                     writes=["ROWN"])
                b.op("s", ACT(ROWN[0:1, :], ROWN[0:1, :], AF.Exp, scale=-1.0),
                     reads=["ROWN"], writes=["ROWN"], inc=SE)
                b.w("t", SE, SE.n)
                b.op("t", MM(P[4][0:64, 0:512], ONESR[0:1, 0:64], ROWN[0:1, :],
                             True, True), inc=PE)
                b.w("v", PE, PE.n)
                b.op("v", CPY(TMP0[0:64, 0:512], P[4][0:64, 0:512]),
                     writes=["TMP0"], inc=VE)
                orow = (hd_ % 2) * 64
                oc0 = (hd_ // 2) * S + qj_ * 512
                b.op("v", TT(oall[orow:orow + 64, oc0:oc0 + 512],
                             pvbank[0:64, 0:512], TMP0[0:64, 0:512], ALU.mult),
                     reads=["TMP0"], writes=[(oall.name, "o", hd_, qj_)],
                     inc=VE)

            for qj in range(4):
                pvbank = P[2 + (qj % 2)]
                kts = list(range(16)) if not l1 else \
                    (list(range(4, 8)) if qj == 0 else list(range(8)))
                n_kt = len(kts)
                se_marks, ve_marks, pe_score = {}, {}, {}

                def kslice(kt):
                    if not l1:
                        return POOL[0:64, KT_C + kt * 128: KT_C + (kt + 1) * 128]
                    kg0 = (qj - 1) * 512 + kt * 128
                    return POOL[0:64, KT_C + kg0: KT_C + kg0 + 128]

                def emit_score(idx):
                    kt = kts[idx]
                    r = idx % 4
                    if idx >= 4:
                        b.w("t", SE, se_marks[idx - 4])
                    b.op("t", MM(SB[r][:, 0:512], kslice(kt),
                                 POOL[0:64, QT_C + qj * 512:
                                      QT_C + (qj + 1) * 512],
                                 True, True), inc=PE)
                    pe_score[idx] = PE.n
                    b.w("s", PE, pe_score[idx])
                    if idx >= 4:
                        b.w("s", VE, ve_marks[idx - 4])
                    b.op("s", ACT(PTB[r], SB[r][:, 0:512], AF.Exp, bias=negC,
                                  scale=0.125), writes=[PTK[r]], inc=SE)
                    se_marks[idx] = SE.n
                    b.w("v", SE, se_marks[idx])
                    kt_ = kts[idx]
                    if not l1:
                        OFF = K0 - 128 * kt_ + 512 * qj
                        msl = POOL[:, ESC_C + OFF: ESC_C + OFF + 512]
                        rk = ("POOL", "esc")
                    else:
                        msl = MBAND[:, 896 - 128 * kt_: 1408 - 128 * kt_]
                        rk = "MBAND"
                    b.op("v", TT(PTB[r], PTB[r], msl, ALU.mult),
                         reads=[PTK[r], rk], writes=[PTK[r]], inc=VE)
                    ve_marks[idx] = VE.n

                def emit_pv(idx):
                    r = idx % 4
                    b.w("t", VE, ve_marks[idx])
                    kt = kts[idx]
                    vt = kt if not l1 else (qj - 1) * 4 + kt
                    vaug_t = POOL[:, VAUG_C + vt * 65: VAUG_C + vt * 65 + 65]
                    b.op("t", MM(pvbank[0:65, 0:512], vaug_t, PTB[r],
                                 idx == 0, idx == n_kt - 1),
                         inc=PE if idx == n_kt - 1 else None)

                b.w("t", VE, VE.n)
                emit_score(0)
                emit_evac()
                emit_score(1)
                if n_kt > 2:
                    emit_score(2)
                for idx in range(n_kt):
                    if idx + 3 < n_kt:
                        emit_score(idx + 3)
                    emit_pv(idx)
                pending_evac[0] = (pvbank, hd, qj, PE.n)
            emit_evac()

        def attn_layer(l1, src, oall):
            b.w("g", PE, PE.n)
            b.w("g", VE, VE.n)
            for t in range(16):
                b.op("g", lambda e, a=pl(VAUG_C + t * 65 + 64, 1): e.memset(a, 1.0),
                     writes=[("VAUG1", t)], inc=GE)
            b.w("t", GE, GE.n)
            proj_pe_done = [0, 0]
            ldw_qkv(0, 0, l1)
            for hd in range(H):
                head_attn(l1, src, oall, hd)
                proj_pe_done[hd % 2] = PE.n
                if hd + 1 < H:
                    b.w("y", PE, proj_pe_done[(hd + 1) % 2])
                    ldw_qkv(hd + 1, (hd + 1) % 2, l1)

        attn_layer(False, A2, A1)
        b.barrier(PH)
        if tap == "l0heads":
            return finalize(A1)

        def wo_phase(oall, dst, wod, add_affine):
            for i in range(NC8):
                half = WSA_C if i < 4 else WSB_C
                d0 = half + (i % 4) * 1024
                b.op("y", DMA(POOL[:, d0:d0 + 1024],
                              wod.rearrange("(c p) t -> c p t", p=128)[i]),
                     inc=QX, inck=16)
            b.w("t", QX)
            for s_ in range(4):
                if add_affine:
                    rows_from("a", "bb", s_ * 4)
                b.w("g", VE, VE.n)
                for c in range(NC8):
                    b.op("g", DMA(pl(XNC_C + c * 512, 512),
                                  dch("spill")[c][:, s_ * 512:(s_ + 1) * 512]),
                         writes=[("POOL", "xnc", c)], inc=QR[0], inck=16)
                if add_affine:
                    b.w("t", QG, QG.n)
                    b.w("t", VE)
                    b.op("t", MM(P[7][:, 0:512], ONESR[0:1, :],
                                 ROW_A[0:1, 0:512], True, True))
                    b.op("t", MM(P[3][:, 0:512], ONESR[0:1, :],
                                 ROW_B[0:1, 0:512], True, True), inc=PE)
                b.w("v", QR[0], QR[0].n)
                for dc in range(NC8):
                    bank = P[5 + (dc % 2)]
                    b.w("t", VE)
                    pairs = []
                    for i in range(NC8):
                        half = WSA_C if i < 4 else WSB_C
                        lh = POOL[:, half + (i % 4) * 1024 + dc * 128:
                                  half + (i % 4) * 1024 + (dc + 1) * 128]
                        pairs.append((lh, oall[:, i * S + s_ * 512:
                                               i * S + s_ * 512 + 512]))
                    mm_group(bank[:, 0:512], pairs, inc=PE)
                    b.w("v", PE)
                    if add_affine:
                        b.op("v", TT(PT[1][:, :], pl(XNC_C + dc * 512, 512),
                                     P[7][:, 0:512], ALU.mult),
                             reads=[("POOL", "xnc", dc)], writes=[("PT", 1)])
                        b.op("v", TT(PT[1][:, :], PT[1][:, :], P[3][:, 0:512],
                                     ALU.add), reads=[("PT", 1)], writes=[("PT", 1)])
                        b.op("v", TT(asl(dst, dc, s_ * 512, 512), PT[1][:, :],
                                     bank[:, 0:512], ALU.add),
                             reads=[("PT", 1)],
                             writes=[(dst.name, "wo", dc, s_)], inc=VE)
                    else:
                        b.op("v", TT(asl(dst, dc, s_ * 512, 512),
                                     pl(XNC_C + dc * 512, 512),
                                     bank[:, 0:512], ALU.add),
                             reads=[("POOL", "xnc", dc)],
                             writes=[(dst.name, "wo", dc, s_)], inc=VE)

        wo_phase(A1, A2, dr["wo"], add_affine=False)
        b.barrier(PH)
        if tap == "x1pre":
            return finalize(A2)

        def ffn_A(w1d, harena_b, act_bias, ch, evac_fp32=None, cidx=None,
                  first=True):
            pe_slab = [PE.n, PE.n]
            ve_split = [VE.n, VE.n]
            for fg in range(8):
                sl = fg % 2
                wc = WSA_C if sl == 0 else WSB_C
                b.w("y", PE, pe_slab[sl])
                b.w("y", QC[sl], qcg[sl])
                if cidx is not None and not first:
                    b.op("y", DMA(POOLB[:, 2 * wc: 2 * wc + 8192],
                                  wpc(cidx, fg)), inc=QW[sl], inck=16)
                    b.w("t", QW[sl], QW[sl].n)
                else:
                    b.op("y", DMA(POOL[:, wc:wc + 4096]
                                  .rearrange("p (c f) -> p c f", f=512),
                                  w1d[:, fg * 512:(fg + 1) * 512]
                                  .rearrange("(c p) f -> p c f", p=128)),
                         inc=QW[sl], inck=16)
                    b.w("v", QW[sl], QW[sl].n)
                    stg = TMP0[:, 1024:1536]
                    for c in range(NC8):
                        base = 2 * wc + c * 1024
                        b.op("v", CPY(stg,
                                      POOL[:, wc + c * 512: wc + c * 512 + 512]),
                             reads=[("POOL", "wsl", sl)], writes=["TMP0"])
                        b.op("v", CPY(POOLB[:, base: base + 512], stg),
                             reads=["TMP0"], writes=[("POOL", "wsl", sl)])
                        b.op("v", TT(POOLB[:, base + 512: base + 1024], stg,
                                     POOLB[:, base: base + 512], ALU.subtract),
                             reads=["TMP0", ("POOL", "wsl", sl)],
                             writes=[("POOL", "wsl", sl)],
                             inc=VE if c == NC8 - 1 else None)
                    b.w("t", VE, VE.n)
                    if cidx is not None:
                        b.w("g", VE, VE.n)
                        b.op("g", DMA(wpc(cidx, fg),
                                      POOLB[:, 2 * wc: 2 * wc + 8192]),
                             reads=[("POOL", "wsl", sl)], inc=QC[sl], inck=16)
                        qcg[sl] = QC[sl].n
                for ft4 in range(4):
                    ft = fg * 4 + ft4
                    bank = P[ft % 2]
                    b.w("t", SE, max(0, SE.n - 1))
                    for c in range(NC8):
                        base = 2 * wc + c * 1024
                        wh = POOLB[:, base + ft4 * 128: base + ft4 * 128 + 128]
                        wl = POOLB[:, base + 512 + ft4 * 128:
                                   base + 512 + ft4 * 128 + 128]
                        xb = 2 * XNC_C + c * 1024
                        xh = POOLB[:, xb: xb + 512]
                        xl = POOLB[:, xb + 512: xb + 1024]
                        b.op("t", MM(bank[:, 0:512], wh, xh, c == 0, False))
                        b.op("t", MM(bank[:, 0:512], wh, xl, False, False))
                        b.op("t", MM(bank[:, 0:512], wl, xh, False, c == NC8 - 1),
                             inc=PE if c == NC8 - 1 else None)
                    b.w("s", PE, PE.n)
                    if evac_fp32 is not None:
                        evac_fp32(ft, bank)
                    else:
                        stg2 = PT[0][:, :]
                        b.w("s", VE, ve_split[0])
                        if act_bias is None:
                            b.op("s", ACT(stg2, bank[:, 0:512], AF.Relu),
                                 writes=[("PT", 0)], inc=SE)
                        else:
                            b.op("s", ACT(stg2, bank[:, 0:512], AF.Gelu,
                                          bias=prm(act_bias, ft)),
                                 writes=[("PT", 0)], inc=SE)
                        b.w("v", SE, SE.n)
                        b.op("v", CPY(harena_b[:, ft * 1024: ft * 1024 + 512],
                                      stg2),
                             reads=[("PT", 0)], writes=[("HB", ft)])
                        b.op("v", TT(harena_b[:, ft * 1024 + 512:
                                              ft * 1024 + 1024], stg2,
                                     harena_b[:, ft * 1024: ft * 1024 + 512],
                                     ALU.subtract),
                             reads=[("PT", 0), ("HB", ft)],
                             writes=[("HB", ft)], inc=VE)
                        ve_split[0] = VE.n
                pe_slab[sl] = PE.n
            b.w("t", SE)
            b.w("t", VE, VE.n)

        def ffn_B(w2d, hsrc_b, ch, evac, cidx=None, first=True):
            b.w("t", VE, VE.n)
            pe_slab = [PE.n, PE.n]
            for fg in range(8):
                sl = fg % 2
                wc = WSA_C if sl == 0 else WSB_C
                b.w("y", PE, pe_slab[sl])
                b.w("y", QC[sl], qcg[sl])
                if cidx is not None and not first:
                    b.op("y", DMA(POOLB[:, 2 * wc: 2 * wc + 8192],
                                  wpc(cidx, fg)), inc=QW[sl], inck=16)
                    b.w("t", QW[sl], QW[sl].n)
                else:
                    b.op("y", DMA(POOL[:, wc:wc + 4096]
                                  .rearrange("p (c f) -> p c f", f=1024),
                                  w2d[fg * 512:(fg + 1) * 512, :]
                                  .rearrange("(s p) d -> p s d", p=128)),
                         inc=QW[sl], inck=16)
                    b.w("v", QW[sl], QW[sl].n)
                    stg = TMP0[:, 1024:2048]
                    for s4 in range(4):
                        base = 2 * wc + s4 * 2048
                        b.op("v", CPY(stg, POOL[:, wc + s4 * 1024:
                                                wc + s4 * 1024 + 1024]),
                             reads=[("POOL", "wsl", sl)], writes=["TMP0"])
                        b.op("v", CPY(POOLB[:, base: base + 1024], stg),
                             reads=["TMP0"], writes=[("POOL", "wsl", sl)])
                        b.op("v", TT(POOLB[:, base + 1024: base + 2048], stg,
                                     POOLB[:, base: base + 1024], ALU.subtract),
                             reads=["TMP0", ("POOL", "wsl", sl)],
                             writes=[("POOL", "wsl", sl)],
                             inc=VE if s4 == 3 else None)
                    b.w("t", VE, VE.n)
                    if cidx is not None:
                        b.w("g", VE, VE.n)
                        b.op("g", DMA(wpc(cidx, fg),
                                      POOLB[:, 2 * wc: 2 * wc + 8192]),
                             reads=[("POOL", "wsl", sl)], inc=QC[sl], inck=16)
                        qcg[sl] = QC[sl].n
                for ft4 in range(4):
                    ft = fg * 4 + ft4
                    hh = hsrc_b[:, ft * 1024: ft * 1024 + 512]
                    hl = hsrc_b[:, ft * 1024 + 512: ft * 1024 + 1024]
                    for dc in range(NC8):
                        base = 2 * wc + ft4 * 2048 + dc * 128
                        w_h = POOLB[:, base: base + 128]
                        w_l = POOLB[:, base + 1024: base + 1024 + 128]
                        b.op("t", MM(P[dc][:, 0:512], w_h, hh, ft == 0, False))
                        b.op("t", MM(P[dc][:, 0:512], w_h, hl, False, False))
                        b.op("t", MM(P[dc][:, 0:512], w_l, hh, False, ft == 31),
                             inc=PE if (ft4 == 3 and dc == NC8 - 1) else None)
                pe_slab[sl] = PE.n
            b.w("v", PE)
            b.w("s", PE)
            for dc in range(NC8):
                evac(dc)

        def affine_xnc_pairs(get_src, atag, btag, gnm, bnm, pk_s0, tag):
            rows_from(atag, btag, pk_s0)
            b.w("t", QG, QG.n)
            b.w("t", VE)
            b.op("t", MM(P[3][:, 0:512], ONESR[0:1, :], ROW_A[0:1, 0:512],
                         True, True))
            b.op("t", MM(P[4][:, 0:512], ONESR[0:1, :], ROW_B[0:1, 0:512],
                         True, True), inc=PE)
            b.w("v", PE)
            stg = TMP0[:, 1536:2048]
            for c in range(NC8):
                xb = 2 * XNC_C + c * 1024
                b.op("v", TT(PT[0][:, :], get_src(c), P[3][:, 0:512], ALU.mult),
                     reads=[(tag, "src", c)], writes=[("PT", 0)])
                if gnm is not None:
                    b.op("v", TT(PT[0][:, :], PT[0][:, :], P[4][:, 0:512],
                                 ALU.add), reads=[("PT", 0)], writes=[("PT", 0)])
                    b.op("v", lambda e, o=stg, i=PT[0][:, :], g=prm(gnm, c),
                         bt=prm(bnm, c): e.tensor_scalar(o, i, g, bt, ALU.mult,
                                                         ALU.add),
                         reads=[("PT", 0)], writes=["TMP0"])
                else:
                    b.op("v", TT(stg, PT[0][:, :], P[4][:, 0:512], ALU.add),
                         reads=[("PT", 0)], writes=["TMP0"])
                b.op("v", CPY(POOLB[:, xb: xb + 512], stg),
                     reads=["TMP0"], writes=[(tag, "xp", c)])
                b.op("v", TT(POOLB[:, xb + 512: xb + 1024], stg,
                             POOLB[:, xb: xb + 512], ALU.subtract),
                     reads=["TMP0", (tag, "xp", c)],
                     writes=[(tag, "xp", c)], inc=VE)

        def moe_phase():
            for ch in range(NCH):
                t0 = ch * TCH
                b.w("g", PE, PE.n)
                b.w("g", VE, VE.n)
                for c in range(NC8):
                    b.op("g", DMA(pl(XNC_C + c * 512, 512),
                                  dch("spill")[c][:, t0:t0 + 512]),
                         writes=[("POOL", "xnc", c)], inc=QR[0], inck=16)
                b.w("v", QR[0], QR[0].n)
                affine_xnc_pairs(lambda c: pl(XNC_C + c * 512, 512),
                                 "a", "bb", None, None, ch * 4, "POOLxnc")
                for tt in range(4):
                    b.w("t", VE)
                    for c in range(NC8):
                        xb = 2 * XNC_C + c * 1024
                        xh = POOLB[:, xb + tt * 128: xb + tt * 128 + 128]
                        xl = POOLB[:, xb + 512 + tt * 128: xb + 512 + tt * 128 + 128]
                        gh = GWTBB[:, c * 8:(c + 1) * 8]
                        gl = GWTBB[:, 64 + c * 8: 64 + (c + 1) * 8]
                        b.op("t", MM(P[5][:, 0:8], xh, gh, c == 0, False))
                        b.op("t", MM(P[5][:, 0:8], xh, gl, False, False))
                        b.op("t", MM(P[5][:, 0:8], xl, gh, False, c == NC8 - 1),
                             inc=PE if c == NC8 - 1 else None)
                    b.w("v", PE)
                    b.op("v", CPY(GTb[:, :], P[5][:, 0:8]), writes=["GTb"], inc=VE)
                    b.op("v", RED(GCOL[:, 0:1], GTb[:, :], ALU.max),
                         reads=["GTb"], writes=[("GC", 0)])
                    b.op("v", lambda e: e.tensor_scalar(
                        GS1[:, :], GTb[:, :], GCOL[:, 0:1], None, ALU.is_ge),
                        reads=["GTb", ("GC", 0)], writes=["GS1"])
                    b.op("v", STT(GS2[:, :], GS1[:, :], -1e30, GTb[:, :],
                                  ALU.mult, ALU.add), reads=["GS1", "GTb"],
                         writes=["GS2"])
                    b.op("v", RED(GCOL[:, 1:2], GS2[:, :], ALU.max),
                         reads=["GS2"], writes=[("GC", 1)])
                    b.op("v", lambda e: e.tensor_scalar(
                        GS2[:, :], GS2[:, :], GCOL[:, 1:2], None, ALU.is_ge),
                        reads=["GS2", ("GC", 1)], writes=["GS2"])
                    b.op("v", TT(GCOL[:, 2:3], GCOL[:, 0:1], GCOL[:, 1:2],
                                 ALU.subtract), reads=[("GC", 0), ("GC", 1)],
                         writes=[("GC", 2)], inc=VE)
                    b.w("s", VE)
                    b.op("s", ACT(GCOL[:, 3:4], GCOL[:, 2:3], AF.Sigmoid,
                                  scale=-1.0), reads=[("GC", 2)],
                         writes=[("GC", 3)])
                    b.op("s", ACT(GCOL[:, 2:3], GCOL[:, 2:3], AF.Sigmoid),
                         reads=[("GC", 2)], writes=[("GC", 2)], inc=SE)
                    b.w("v", SE)
                    b.op("v", lambda e: e.tensor_scalar(
                        GS1[:, :], GS1[:, :], GCOL[:, 2:3], None, ALU.mult),
                        reads=["GS1", ("GC", 2)], writes=["GS1"])
                    b.op("v", lambda e: e.scalar_tensor_tensor(
                        GS1[:, :], GS2[:, :], GCOL[:, 3:4], GS1[:, :],
                        ALU.mult, ALU.add),
                        reads=["GS1", "GS2", ("GC", 3)], writes=["GS1"], inc=VE)
                    b.w("t", VE)
                    b.op("t", MM(P[6][0:8, 0:128], GS1[:, :], I128[:, :],
                                 True, True), inc=PE)
                    b.w("v", PE)
                    b.w("v", QG, QG.n)
                    b.op("v", CPY(CTS[:, :], P[6][0:8, 0:128]), writes=["STG32"],
                         inc=VE)
                    b.w("g", VE)
                    b.w("g", PE, PE.n)
                    crd0 = bass.AP(TMP0.tensor, tt * 128, [[2048 * 32, 4], [1, 128]])
                    crd1 = bass.AP(TMP0.tensor, 512 + tt * 128,
                                   [[2048 * 32, 4], [1, 128]])
                    b.op("g", DMA(crd0, CTS[0:4, :]), reads=["STG32"],
                         inc=QG, inck=16)
                    b.op("g", DMA(crd1, CTS[4:8, :]), reads=["STG32"],
                         inc=QG, inck=16)
                b.w("t", QG, QG.n)
                for e_ in range(NE):
                    r0 = 32 * (e_ % 4)
                    crow = TMP0[r0:r0 + 1, (e_ // 4) * 512:(e_ // 4) * 512 + 512]
                    b.w("t", VE)
                    b.op("t", lambda e, o=P[5][:, 0:512],
                         a=ONESR[r0:r0 + 1, :], r=crow, tp=(r0, 0):
                         e.matmul(o, a, r, start=True, stop=True,
                                  tile_position=tp), inc=PE)
                    b.w("v", PE)
                    b.op("v", CPY(PT[1][:, :], P[5][:, 0:512]),
                         writes=[("PT", 1)], inc=VE)
                    b.w("t", VE)
                    ffn_A(dr["w1T"][e_], A1B, None, ch, cidx=e_,
                          first=(ch == 0))

                    def moe_evac(dc, t0=t0, ch=ch):
                        b.op("v", TT(PT[0][:, :], P[dc][:, 0:512], PT[1][:, :],
                                     ALU.mult), reads=[("PT", 1)],
                             writes=[("PT", 0)])
                        b.op("v", TT(asl(A2, dc, t0, 512), asl(A2, dc, t0, 512),
                                     PT[0][:, :], ALU.add),
                             reads=[("PT", 0), ("A2", "x1", dc, ch)],
                             writes=[("A2", "x1", dc, ch)], inc=VE)
                    ffn_B(dr["w2T"][e_], A1B, ch, moe_evac, cidx=8 + e_,
                          first=(ch == 0))

        moe_phase()
        b.barrier(PH)
        if tap == "x1":
            return finalize(A2)

        stats_pass(A2, D, 0, S, inner_too=True)
        b.barrier(PH)
        affine_apply(lambda c, s: asl(A2, c, s * 512, 512),
                     lambda c, s: asl(A1, c, s * 512, 512),
                     "A", "B", "g1", "b1", 0, S, "a21")
        b.w("g", VE)
        for c in range(NC8):
            b.op("g", DMA(dch("spill")[c], asl(A2, c, 0, S)),
                 reads=[("a21", "spx", c)], inc=QG, inck=16)
        b.w("g", QG)
        b.barrier(PH)
        if tap == "xn2p":
            return finalize(A1)

        attn_layer(True, A1, A2)
        b.barrier(PH)
        if tap == "l1heads":
            return finalize(A2)

        wo_phase(A2, A1, dr["woT1"], add_affine=True)
        b.barrier(PH)
        if tap == "l1y":
            return finalize(A1)

        def mlp_phase():
            for ch in range(NCH):
                t0 = ch * TCH
                stats_pass(A1, D, t0, 512, inner_too=False)
                affine_xnc_pairs(lambda c: asl(A1, c, t0, 512),
                                 "a", "bb", "g2", "b2", 0, "mlph")
                b.w("t", VE)
                ffn_A(dr["mw1T"], A2B, "mb1", ch, cidx=16, first=(ch == 0))

                def mlp_evac(dc, t0=t0, ch=ch):
                    tb = TMP0[:, 0:512] if dc % 2 == 0 else TMP0[:, 512:1024]
                    if dc >= 2:
                        b.w("s", VE, VE.n - 1)
                    b.op("s", ACT(tb, P[dc][:, 0:512], AF.Identity,
                                  bias=prm("mb2", dc)),
                         writes=["TMP0"], inc=SE)
                    b.w("v", SE, SE.n)
                    b.op("v", TT(asl(A1, dc, t0, 512), asl(A1, dc, t0, 512), tb,
                                 ALU.add),
                         reads=["TMP0", ("A1", "acc", dc, ch)],
                         writes=[("A1", "acc", dc, ch)], inc=VE)
                ffn_B(dr["mw2T"], A2B, ch, mlp_evac, cidx=17,
                      first=(ch == 0))

        mlp_phase()
        b.barrier(PH)
        if tap == "l1branch":
            return finalize(A1)

        def quant(wsrc, wdst, axis_d):
            b.w("g", VE, VE.n)
            b.op("g", lambda e: e.memset(QSC[:, 0:1], 0.0),
                 writes=[("QSC", 0)], inc=GE)
            b.w("v", GE, GE.n)

            def slab_aps(sl_):
                if axis_d:
                    src_ = wsrc[:, sl_ * 512:(sl_ + 1) * 512].rearrange(
                        "(c p) f -> p c f", p=128)
                    dst_ = wdst[:, sl_ * 512:(sl_ + 1) * 512].rearrange(
                        "(c p) f -> p c f", p=128)
                    z = 512
                else:
                    src_ = wsrc[sl_ * 512:(sl_ + 1) * 512, :].rearrange(
                        "(s p) d -> p s d", p=128)
                    dst_ = wdst[sl_ * 512:(sl_ + 1) * 512, :].rearrange(
                        "(s p) d -> p s d", p=128)
                    z = 1024
                return src_, dst_, z

            for sl_ in range(8):
                src_, _, z = slab_aps(sl_)
                b.w("y", VE, VE.n)
                b.w("y", QG, QG.n)
                b.op("y", DMA(POOL[:, WSA_C:WSA_C + 4096].rearrange(
                    "p (a z) -> p a z", z=z), src_), inc=QW[0], inck=16)
                b.w("v", QW[0], QW[0].n)
                b.op("v", RED(QSC[:, 1:2], POOL[:, WSA_C:WSA_C + 4096], ALU.add,
                              absv=True), reads=[("POOL", "qs")],
                     writes=[("QSC", 1)])
                b.op("v", TT(QSC[:, 0:1], QSC[:, 0:1], QSC[:, 1:2], ALU.add),
                     reads=[("QSC", 0), ("QSC", 1)], writes=[("QSC", 0)], inc=VE)
            b.w("t", VE)
            b.op("t", MM(P[5][0:1, 0:1], QSC[:, 0:1], ONESC[:, :], True, True),
                 inc=PE)
            b.w("v", PE)
            b.op("v", CPY(QV[0:1, 0:1], P[5][0:1, 0:1]), writes=[("QV", 0)])
            b.op("v", TSC(QV[0:1, 1:2], QV[0:1, 0:1], 1.0 / (D * FF), 1e-5,
                          ALU.mult, ALU.max),
                 reads=[("QV", 0)], writes=[("QV", 1)])
            b.op("v", RECIP(QV[0:1, 2:3], QV[0:1, 1:2]), reads=[("QV", 1)],
                 writes=[("QV", 2)])
            b.op("v", RECIP(QV[0:1, 3:4], QV[0:1, 2:3]), reads=[("QV", 2)],
                 writes=[("QV", 3)], inc=VE)
            b.w("t", VE)
            b.op("t", MM(P[5][:, 0:2], ONESR[0:1, :], QV[0:1, 2:4], True, True),
                 inc=PE)
            b.w("v", PE)
            b.op("v", CPY(SCB[:, :], P[5][:, 0:2]), writes=["SCB"], inc=VE)
            for sl_ in range(8):
                src_, dst_, z = slab_aps(sl_)
                b.w("y", QG, QG.n)
                b.w("y", VE, VE.n)
                b.op("y", DMA(POOL[:, WSA_C:WSA_C + 4096].rearrange(
                    "p (a z) -> p a z", z=z), src_), inc=QW[0], inck=16)
                b.w("v", QW[0], QW[0].n)
                for hh in range(2):
                    seg = POOL[:, WSA_C + hh * 2048: WSA_C + hh * 2048 + 2048]
                    b.op("v", lambda e, o=TMP0[:, :], i=seg: e.tensor_scalar(
                        o, i, SCB[:, 0:1], 0.5, ALU.mult, ALU.is_gt),
                        reads=[("POOL", "qs", hh), "SCB"],
                        writes=["TMP0"])
                    b.op("v", lambda e, o=pl(WSB_C, 2048), i=seg: e.tensor_scalar(
                        o, i, SCB[:, 0:1], -0.5, ALU.mult, ALU.is_lt),
                        reads=[("POOL", "qs", hh), "SCB"],
                        writes=[("POOL", "ql")])
                    b.op("v", TT(seg, TMP0[:, :], pl(WSB_C, 2048), ALU.subtract),
                         reads=["TMP0", ("POOL", "ql")],
                         writes=[("POOL", "qs", hh)])
                    b.op("v", lambda e, o=seg, i=seg: e.tensor_scalar(
                        o, i, SCB[:, 1:2], None, ALU.mult),
                        reads=[("POOL", "qs", hh), "SCB"],
                        writes=[("POOL", "qs", hh)], inc=VE)
                b.w("g", VE)
                b.op("g", DMA(dst_, POOL[:, WSA_C:WSA_C + 4096].rearrange(
                    "p (a z) -> p a z", z=z)),
                    reads=[("POOL", "qs")], inc=QG, inck=16)
            b.w("g", QG)

        quant(dr["fw1T"], dr["fq1"], True)
        quant(dr["fw2T"], dr["fq2"], False)
        b.barrier(PH)

        def l1ffn_phase():
            for ch in range(NCH):
                t0 = ch * TCH
                b.w("g", PE, PE.n)
                b.w("g", VE, VE.n)
                for c in range(NC8):
                    b.op("g", DMA(pl(XNC_C + c * 512, 512),
                                  dch("spill")[c][:, t0:t0 + 512]),
                         writes=[("POOL", "xnc", c)], inc=QR[0], inck=16)
                b.w("v", QR[0], QR[0].n)
                affine_xnc_pairs(lambda c: pl(XNC_C + c * 512, 512),
                                 "A", "B", "fg1", "fb1", ch * 4, "POOLxnc")
                b.w("t", VE)

                def ffnq_evac(ft, bank, ch=ch):
                    b.op("s", ACT(A2[:, ft * 512:(ft + 1) * 512], bank[:, 0:512],
                                  AF.Gelu, bias=prm("fb1v", ft)),
                         writes=[("A2", "h", ft, ch)], inc=SE)
                ffn_A(dr["fq1"], None, None, ch, evac_fp32=ffnq_evac,
                      cidx=18, first=(ch == 0))
                b.w("s", VE, VE.n)
                b.drain("s")
                b.w("t", VE, VE.n)
                sqw = [TMP0[:, i * 512:(i + 1) * 512] for i in range(4)]
                pe_sq = [0] * 4
                for g4 in range(8):
                    for i in range(4):
                        ft = g4 * 4 + i
                        b.w("s", PE, pe_sq[i])
                        b.op("s", ACT(sqw[i], A2[:, ft * 512:(ft + 1) * 512],
                                      AF.Square), reads=[("A2", "h", ft, ch)],
                             writes=[("SQ", i)], inc=SE)
                    se_mark = SE.n
                    for tt in range(4):
                        for i in range(4):
                            ft = g4 * 4 + i
                            b.op("t", MM(P[tt][:, 0:1],
                                         A2[:, ft * 512 + tt * 128:
                                            ft * 512 + (tt + 1) * 128],
                                         ONESC[:, :], ft == 0, ft == 31))
                    b.w("t", SE, se_mark)
                    for tt in range(4):
                        for i in range(4):
                            ft = g4 * 4 + i
                            b.op("t", MM(P[4 + tt][:, 0:1],
                                         sqw[i][:, tt * 128:(tt + 1) * 128],
                                         ONESC[:, :], ft == 0, ft == 31),
                                 inc=PE if (tt == 3 and i == 3) else None)
                    for i in range(4):
                        pe_sq[i] = PE.n
                b.w("v", PE)
                for tt in range(4):
                    b.op("v", CPY(pkv("s1")[:, tt:tt + 1], P[tt][:, 0:1]),
                         writes=[("PK", "s1")])
                    b.op("v", CPY(pkv("s2")[:, tt:tt + 1], P[4 + tt][:, 0:1]),
                         writes=[("PK", "s2")],
                         inc=VE if tt == 3 else None)
                _stats_math(4, float(FF), False)
                rows_from("a", "bb", 0)
                b.w("t", QG, QG.n)
                b.w("t", VE)
                b.op("t", MM(P[3][:, 0:512], ONESR[0:1, :], ROW_A[0:1, 0:512],
                             True, True))
                b.op("t", MM(P[4][:, 0:512], ONESR[0:1, :], ROW_B[0:1, 0:512],
                             True, True), inc=PE)
                b.w("v", PE)
                stgh = TMP0[:, 1536:2048]
                for ft in range(32):
                    b.op("v", TT(PT[0][:, :], A2[:, ft * 512:(ft + 1) * 512],
                                 P[3][:, 0:512], ALU.mult),
                         reads=[("A2", "h", ft, ch)], writes=[("PT", 0)])
                    b.op("v", TT(PT[0][:, :], PT[0][:, :], P[4][:, 0:512],
                                 ALU.add), reads=[("PT", 0)], writes=[("PT", 0)])
                    b.op("v", lambda e, o=stgh, i=PT[0][:, :], g=prm("fg2", ft),
                         bt=prm("fb2", ft):
                         e.tensor_scalar(o, i, g, bt, ALU.mult, ALU.add),
                         reads=[("PT", 0)], writes=["TMP0"])
                    b.op("v", CPY(A2B[:, ft * 1024: ft * 1024 + 512], stgh),
                         reads=["TMP0"], writes=[("A2", "h", ft, ch)])
                    b.op("v", TT(A2B[:, ft * 1024 + 512: ft * 1024 + 1024], stgh,
                                 A2B[:, ft * 1024: ft * 1024 + 512],
                                 ALU.subtract),
                         reads=["TMP0", ("A2", "h", ft, ch)],
                         writes=[("A2", "h", ft, ch)], inc=VE)

                def ffn_evac(dc, t0=t0, ch=ch):
                    tb = TMP0[:, 0:512] if dc % 2 == 0 else TMP0[:, 512:1024]
                    if dc >= 2:
                        b.w("s", VE, VE.n - 1)
                    b.op("s", ACT(tb, P[dc][:, 0:512], AF.Identity,
                                  bias=prm("fb2v", dc)),
                         writes=["TMP0"], inc=SE)
                    b.w("v", SE, SE.n)
                    b.op("v", TT(asl(A1, dc, t0, 512), asl(A1, dc, t0, 512), tb,
                                 ALU.add),
                         reads=["TMP0", ("A1", "acc2", dc, ch)],
                         writes=[("A1", "acc2", dc, ch)], inc=VE)
                ffn_B(dr["fq2"], A2B, ch, ffn_evac, cidx=19,
                      first=(ch == 0))

        l1ffn_phase()
        b.barrier(PH)

        for c in range(NC8):
            half = c % 2
            rl = pl(XNC_C + half * 2048, 2048)
            ob = pl(WSA_C + half * 2048, 2048)
            qr = QR[half]
            b.w("g", VE, VE.n)
            b.op("g", DMA(rl, dch("spill")[c]), writes=[("POOL", "rl", half)],
                 inc=qr, inck=16)
            b.w("v", qr, qr.n)
            b.w("v", QO, QO.n)
            b.op("v", TT(ob, rl, asl(A1, c, 0, S), ALU.add),
                 reads=[("POOL", "rl", half), ("A1", "fin", c)],
                 writes=[("POOL", "ob", half)], inc=VE)
            b.w("g", VE)
            b.op("g", DMA(dch("outT")[c], ob), reads=[("POOL", "ob", half)],
                 inc=QO, inck=16)
        b.w("g", QO)
        b.w("y", QO)
        block = es.enter_context(nc.Block())
        b.replay(block)
    return nc


def _alibi_table():
    xu = np.array([1060439283, 1060168325, 1059866229, 1059527426, 1059144984,
                   1058710189, 1058211980, 1057636198, 1056964608, 1055382913,
                   1053503477, 1051256226, 1048576000, 1042367767, 1031798784,
                   998244352], dtype=np.uint32)
    sl = xu.view(np.float32)
    p = np.arange(128)
    c = np.arange(ESCW)
    d32 = -np.abs(p[:, None] - c[None, :] + K0).astype(np.float32)
    return np.ascontiguousarray((sl[:, None, None] * d32[None]).astype(np.float32))



def _prep_inputs(ins):
    f32 = lambda a: np.ascontiguousarray(np.asarray(a), dtype=np.float32)
    pack = lambda v: np.ascontiguousarray(f32(v).reshape(-1, 128).T)
    prm = np.zeros((128, 224), np.float32)
    off = 0
    for nm, wd, val in [
            ("g0a", 8, ins["L0_attn_ln_gamma"]), ("b0a", 8, ins["L0_attn_ln_beta"]),
            ("g1", 8, ins["L1_attn_ln1_g"]), ("b1", 8, ins["L1_attn_ln1_b"]),
            ("g2", 8, ins["L1_attn_ln2_g"]), ("b2", 8, ins["L1_attn_ln2_b"]),
            ("mb1", 32, ins["L1_attn_mlp_b1"]), ("mb2", 8, ins["L1_attn_mlp_b2"]),
            ("fg1", 8, ins["L1_ffn_ln1_g"]), ("fb1", 8, ins["L1_ffn_ln1_b"]),
            ("fb1v", 32, ins["L1_ffn_b1"]), ("fg2", 32, ins["L1_ffn_ln2_g"]),
            ("fb2", 32, ins["L1_ffn_ln2_b"]), ("fb2v", 8, ins["L1_ffn_b2"])]:
        prm[:, off:off + wd] = pack(val)
        off += wd
    shared = {
        "wq": f32(ins["L0_attn_Wq"]),
        "wk": f32(ins["L0_attn_Wk"]),
        "wv": f32(ins["L0_attn_Wv"]),
        "wo": f32(ins["L0_attn_Wo"]),
        "gwT": np.ascontiguousarray(f32(ins["L0_ffn_gate_w"]).T),
        "w1T": np.ascontiguousarray(f32(ins["L0_ffn_w1"]).transpose(0, 2, 1)),
        "w2T": np.ascontiguousarray(f32(ins["L0_ffn_w2"]).transpose(0, 2, 1)),
        "qkvT": np.ascontiguousarray(f32(ins["L1_attn_w_qkv"]).T),
        "woT1": np.ascontiguousarray(f32(ins["L1_attn_w_o"]).T),
        "mw1T": np.ascontiguousarray(f32(ins["L1_attn_mlp_w1"]).T),
        "mw2T": np.ascontiguousarray(f32(ins["L1_attn_mlp_w2"]).T),
        "fw1T": np.ascontiguousarray(f32(ins["L1_ffn_w1"]).T),
        "fw2T": np.ascontiguousarray(f32(ins["L1_ffn_w2"]).T),
        "prm": prm,
        "alibi": _alibi_table(),
    }
    x = f32(ins["L0_attn_x"])
    in_maps = []
    for c in range(8):
        m = dict(shared)
        m["xT"] = np.ascontiguousarray(x[c].T)
        in_maps.append(m)
    return in_maps


def kernel(**inputs):
    in_maps = _prep_inputs(inputs)
    nc = build_program(tap=TAP)
    res = run_bass_kernel_spmd(nc, in_maps, core_ids=list(range(8)))
    return np.stack([np.asarray(r["outT"]).T for r in res.results])


# version 08
# speedup vs baseline: 1.2494x; 1.0023x over previous
import contextlib
import numpy as np
import concourse.bass as bass
import concourse.mybir as mybir
from concourse.bass_utils import run_bass_kernel_spmd

F32 = mybir.dt.float32
BF16 = mybir.dt.bfloat16
AF = mybir.ActivationFunctionType
ALU = mybir.AluOpType
AX = mybir.AxisListType

D, S, NC8, H, DH, FF, NE = 1024, 2048, 8, 16, 64, 4096, 8
TCH, NCH = 512, 4
C0, C1, EPS = 8.0, 4.0, 1e-5
K0, ESCW = 1920, 3968

TAP = None


class Sem:
    __slots__ = ("h", "n")

    def __init__(self, h):
        self.h = h
        self.n = 0


def MM(out, lhsT, rhs, start, stop):
    return lambda e, o=out, a=lhsT, r=rhs, s1=start, s2=stop: e.matmul(
        o, a, r, start=s1, stop=s2)


def CPY(out, in_):
    return lambda e, o=out, i=in_: e.tensor_copy(o, i)


def TT(out, a, b_, op):
    return lambda e, o=out, x=a, y=b_, p=op: e.tensor_tensor(o, x, y, p)


def TSC(out, in_, s1, s2, op0, op1=None):
    if op1 is None:
        return lambda e, o=out, i=in_, a=s1, p=op0: e.tensor_scalar(o, i, a, None, p)
    return lambda e, o=out, i=in_, a=s1, c=s2, p=op0, q=op1: e.tensor_scalar(
        o, i, a, c, p, q)


def STT(out, in0, sc, in1, op0, op1):
    return lambda e, o=out, i=in0, s=sc, j=in1, p=op0, q=op1: \
        e.scalar_tensor_tensor(o, i, s, j, p, q)


def ACT(out, in_, func, bias=0.0, scale=1.0, accum_out=None):
    return lambda e, o=out, i=in_, f=func, b=bias, s=scale, a=accum_out: \
        e.activation(o, i, f, bias=b, scale=s, accum_out=a)


def DMA(out, in_):
    return lambda e, o=out, i=in_: e.dma_start(out=o, in_=i)


def RED(out, in_, op, absv=False):
    return lambda e, o=out, i=in_, p=op, a=absv: e.tensor_reduce(
        o, i, AX.X, p, apply_absolute_value=a)


def RECIP(out, in_):
    return lambda e, o=out, i=in_: e.reciprocal(o, i)


class Bld:
    def __init__(self, nc):
        self.nc = nc
        self.q = {k: [] for k in "ytvsg"}
        self.rd = {k: set() for k in "vsg"}
        self.wr = {k: set() for k in "vsg"}

    def drain(self, ek):
        if ek in self.rd:
            self.rd[ek].clear()
            self.wr[ek].clear()
        self.q[ek].append(lambda e: e.drain())

    def op(self, ek, fn, reads=(), writes=(), inc=None, inck=1):
        if ek in self.rd:
            rs, ws = set(reads), set(writes)
            if (rs & self.wr[ek]) or (ws & (self.wr[ek] | self.rd[ek])):
                self.q[ek].append(lambda e: e.drain())
                self.rd[ek].clear()
                self.wr[ek].clear()
            self.rd[ek] |= rs
            self.wr[ek] |= ws
        if inc is not None:
            inc.n += inck
            self.q[ek].append(lambda e, f=fn, s=inc.h, k=inck: f(e).then_inc(s, k))
        else:
            self.q[ek].append(fn)

    def w(self, ek, sem, val=None):
        v = sem.n if val is None else val
        if v <= 0:
            return
        self.q[ek].append(lambda e, s=sem.h, vv=v: e.wait_ge(s, vv))

    def barrier(self, ph):
        for ek in "ytvsg":
            if ek in self.rd:
                self.rd[ek].clear()
                self.wr[ek].clear()
            self.q[ek].append(lambda e: e.drain())
            ph.n += 1
            self.q[ek].append(lambda e, s=ph.h: e.sem_inc(s, 1))
        for ek in "ytvsg":
            self.q[ek].append(lambda e, s=ph.h, v=ph.n: e.wait_ge(s, v))

    def replay(self, block):
        q = self.q

        @block.sync
        def _(e):
            for f in q["y"]:
                f(e)

        @block.tensor
        def _(e):
            for f in q["t"]:
                f(e)

        @block.vector
        def _(e):
            for f in q["v"]:
                f(e)

        @block.scalar
        def _(e):
            for f in q["s"]:
                f(e)

        @block.gpsimd
        def _(e):
            for f in q["g"]:
                f(e)


def build_program(tap=None):
    nc = bass.Bass("TRN2", target_bir_lowering=False)
    dtn = lambda n, sh, kind: nc.dram_tensor(n, sh, F32, kind=kind).ap()
    dr = {
        "xT": dtn("xT", [D, S], "ExternalInput"),
        "wq": dtn("wq", [D, D], "ExternalInput"),
        "wk": dtn("wk", [D, D], "ExternalInput"),
        "wv": dtn("wv", [D, D], "ExternalInput"),
        "wo": dtn("wo", [D, D], "ExternalInput"),
        "gwT": dtn("gwT", [D, NE], "ExternalInput"),
        "w1T": dtn("w1T", [NE, D, FF], "ExternalInput"),
        "w2T": dtn("w2T", [NE, FF, D], "ExternalInput"),
        "qkvT": dtn("qkvT", [D, 3 * D], "ExternalInput"),
        "woT1": dtn("woT1", [D, D], "ExternalInput"),
        "mw1T": dtn("mw1T", [D, FF], "ExternalInput"),
        "mw2T": dtn("mw2T", [FF, D], "ExternalInput"),
        "fw1T": dtn("fw1T", [D, FF], "ExternalInput"),
        "fw2T": dtn("fw2T", [FF, D], "ExternalInput"),
        "prm": dtn("prm", [128, 224], "ExternalInput"),
        "alibi": dtn("alibi", [H, 128, ESCW], "ExternalInput"),
        "spill": dtn("spill", [D, S], "Internal"),
        "fq1": dtn("fq1", [D, FF], "Internal"),
        "fq2": dtn("fq2", [FF, D], "Internal"),
        "outT": dtn("outT", [D, S], "ExternalOutput"),
        "wpcA": nc.dram_tensor("wpcA", [10, 8, 128, 8192], BF16,
                               kind="Internal").ap(),
        "wpcB": nc.dram_tensor("wpcB", [10, 8, 128, 8192], BF16,
                               kind="Internal").ap(),
    }
    dch = lambda name: dr[name].rearrange("(c p) t -> c p t", p=128)
    wpc = lambda ci, fg: (dr["wpcA"][ci, fg] if ci < 10
                          else dr["wpcB"][ci - 10, fg])

    es = contextlib.ExitStack()
    with es:
        sbt = lambda n, sh: es.enter_context(nc.sbuf_tensor(n, sh, F32)).ap()
        pst = lambda n, sh: es.enter_context(nc.psum_tensor(n, sh, F32)).ap()
        A1 = sbt("A1", [128, NC8 * S])
        A2 = sbt("A2", [128, NC8 * S])
        POOL = sbt("POOL", [128, 12288])
        PT = [sbt("PTa", [128, 512]), sbt("PTb", [128, 512])]
        TMP0 = sbt("TMP0", [128, 2048])
        PK = sbt("PK", [128, 192])
        STG32 = sbt("STG32", [32, 128])
        ROW_A = sbt("ROW_A", [1, 512])
        ROW_B = sbt("ROW_B", [1, 512])
        ROWN = sbt("ROWN", [1, 512])
        GTb = sbt("GTb", [128, 8])
        GS1 = sbt("GS1", [128, 8])
        GS2 = sbt("GS2", [128, 8])
        GCOL = sbt("GCOL", [128, 4])
        I128 = sbt("I128", [128, 128])
        MBAND = sbt("MBAND", [128, 1408])
        ONESR = sbt("ONESR", [97, 128])
        ONESC = sbt("ONESC", [128, 1])
        CONSTS = sbt("CONSTS", [128, 8])
        PRM = sbt("PRM", [128, 224])
        GWT = sbt("GWT", [128, 64])
        GWTB = sbt("GWTB", [128, 64])
        QV = sbt("QV", [1, 4])
        SCB = sbt("SCB", [128, 2])
        QSC = sbt("QSC", [128, 2])
        CTS = STG32[0:8, :]
        P = [pst("P%d" % i, [128, 512]) for i in range(8)]

        SMN = ["PH", "QW0", "QW1", "QX", "QG", "QR0", "QR1", "QO",
               "PE", "VE", "SE", "GE", "QC0", "QC1"]
        SM = {n: Sem(es.enter_context(nc.semaphore(n))) for n in SMN}
        PH, QX, QG, QO = SM["PH"], SM["QX"], SM["QG"], SM["QO"]
        PE, VE, SE, GE = SM["PE"], SM["VE"], SM["SE"], SM["GE"]
        QW = [SM["QW0"], SM["QW1"]]
        QR = [SM["QR0"], SM["QR1"]]
        QC = [SM["QC0"], SM["QC1"]]
        qcg = [0, 0]

        b = Bld(nc)

        CC = {"eps": 0, "negC0": 1, "negC1": 2, "k0": 3}
        cc = lambda n: CONSTS[:, CC[n]:CC[n] + 1]
        PKC = {n: i * 16 for i, n in enumerate(
            ["s1", "s2", "m", "v", "a", "bb", "A", "B", "t0", "t1", "t2", "t3"])}
        pkv = lambda n, w=16: PK[:, PKC[n]:PKC[n] + w]
        PRL = [("g0a", 8), ("b0a", 8), ("g1", 8), ("b1", 8), ("g2", 8), ("b2", 8),
               ("mb1", 32), ("mb2", 8), ("fg1", 8), ("fb1", 8), ("fb1v", 32),
               ("fg2", 32), ("fb2", 32), ("fb2v", 8)]
        PRO = {}
        _o = 0
        for _nm, _wd in PRL:
            PRO[_nm] = _o
            _o += _wd
        assert _o <= 224
        prm = lambda nm, j: PRM[:, PRO[nm] + j:PRO[nm] + j + 1]

        QKVW_C, QT_C, KT_C, VAUG_C, ESC_C = 0, 3072, 5120, 7168, 8208
        WSA_C, WSB_C, XNC_C = 0, 4096, 8192
        pl = lambda c0, w, r=128: POOL[0:r, c0:c0 + w]
        asl = lambda ar, c, t0, w: ar[:, c * S + t0: c * S + t0 + w]
        POOLB = POOL.bitcast(BF16)
        A1B = A1.bitcast(BF16)
        A2B = A2.bitcast(BF16)
        GWTBB = GWTB.bitcast(BF16)
        plb = lambda c0, w, r=128: POOLB[0:r, c0:c0 + w]

        def mm_group(psum_ap, pairs, inc=None):
            n = len(pairs)
            for i, (a_, r_) in enumerate(pairs):
                b.op("t", MM(psum_ap, a_, r_, i == 0, i == n - 1),
                     inc=inc if i == n - 1 else None)

        b.op("y", DMA(PRM[:, :], dr["prm"][:, :]), inc=QX, inck=16)
        b.op("y", DMA(GWT[:, :].rearrange("p (c f) -> p c f", f=8),
                      dr["gwT"].rearrange("(c p) f -> p c f", p=128)), inc=QX, inck=16)
        for c in range(NC8):
            b.op("y", DMA(asl(A1, c, 0, S), dch("xT")[c]), inc=QX, inck=16)
        for nm, val in [("eps", EPS), ("negC0", -C0), ("negC1", -C1),
                        ("k0", float(K0))]:
            b.op("g", lambda e, a=cc(nm), v=val: e.memset(a, v),
                 writes=[("C", nm)], inc=GE)
        b.op("g", lambda e: e.memset(PK[:, :], 0.0), writes=["PKZ"], inc=GE)
        for _r in (0, 32, 64, 96):
            b.op("g", lambda e, a=ONESR[_r:_r + 1, :]: e.memset(a, 1.0),
                 writes=["ONESR"], inc=GE)
        b.op("g", lambda e: e.memset(ONESC[:, :], 1.0), writes=["ONESC"], inc=GE)
        b.op("g", lambda e: e.iota(I128[:, :], [[1, 128]], channel_multiplier=-1,
                                   allow_small_or_imprecise_dtypes=True),
             writes=["I128"], inc=GE)
        b.op("g", lambda e: e.iota(MBAND[:, :], [[1, 1408]], channel_multiplier=-1,
                                   allow_small_or_imprecise_dtypes=True),
             writes=["MBAND"], inc=GE)
        for t in range(16):
            b.op("g", lambda e, a=pl(VAUG_C + t * 65 + 64, 1): e.memset(a, 1.0),
                 writes=[("VAUG1", t)], inc=GE)
        b.w("v", GE)
        b.op("v", TSC(I128[:, :], I128[:, :], 0.0, None, ALU.is_equal),
             reads=["I128"], writes=["I128"], inc=VE)
        b.op("v", TSC(TMP0[:, 0:1408], MBAND[:, :], 384.0, None, ALU.is_ge),
             reads=["MBAND"], writes=["TMP0"])
        b.op("v", TSC(pl(ESC_C, 1408), MBAND[:, :], 896.0, None, ALU.is_lt),
             reads=["MBAND"], writes=[("POOL", "mb")])
        b.op("v", TT(MBAND[:, :], TMP0[:, 0:1408], pl(ESC_C, 1408), ALU.mult),
             reads=["TMP0", ("POOL", "mb")], writes=["MBAND"], inc=VE)
        b.w("v", QX)
        b.op("v", CPY(GWTBB[:, 0:64], GWT[:, :]), reads=["GWT"],
             writes=[("GWTB", 0)])
        b.op("v", TT(GWTBB[:, 64:128], GWT[:, :], GWTBB[:, 0:64], ALU.subtract),
             reads=["GWT", ("GWTB", 0)], writes=[("GWTB", 1)], inc=VE)
        b.w("t", QX)
        b.w("s", QX)
        b.barrier(PH)

        def _stats_math(nct, nf, inner_too):
            b.op("v", TSC(pkv("m")[:, 0:nct], pkv("s1")[:, 0:nct], 1.0 / nf, None,
                          ALU.mult), reads=[("PK", "s1")], writes=[("PK", "m")])
            b.op("v", TSC(pkv("v")[:, 0:nct], pkv("s2")[:, 0:nct], 1.0 / nf, None,
                          ALU.mult), reads=[("PK", "s2")], writes=[("PK", "v")])
            b.op("v", TT(pkv("t0")[:, 0:nct], pkv("m")[:, 0:nct], pkv("m")[:, 0:nct],
                         ALU.mult), reads=[("PK", "m")], writes=[("PK", "t0")])
            b.op("v", TT(pkv("v")[:, 0:nct], pkv("v")[:, 0:nct], pkv("t0")[:, 0:nct],
                         ALU.subtract), reads=[("PK", "t0"), ("PK", "v")],
                 writes=[("PK", "v")], inc=VE)
            b.w("s", VE)
            b.op("s", ACT(pkv("t0")[:, 0:nct], pkv("v")[:, 0:nct], AF.Ln,
                          bias=cc("eps")), reads=[("PK", "v")], writes=[("PK", "t0")])
            b.op("s", ACT(pkv("a")[:, 0:nct], pkv("t0")[:, 0:nct], AF.Exp, scale=-0.5),
                 reads=[("PK", "t0")], writes=[("PK", "a")], inc=SE)
            b.w("v", SE)
            b.op("v", STT(pkv("bb")[:, 0:nct], pkv("m")[:, 0:nct], -1.0,
                          pkv("a")[:, 0:nct], ALU.mult, ALU.mult),
                 reads=[("PK", "m"), ("PK", "a")], writes=[("PK", "bb")], inc=VE)
            if inner_too:
                b.op("v", TSC(pkv("t1")[:, 0:nct], pkv("v")[:, 0:nct], EPS, None,
                              ALU.add), reads=[("PK", "v")], writes=[("PK", "t1")])
                b.op("v", RECIP(pkv("t2")[:, 0:nct], pkv("t1")[:, 0:nct]),
                     reads=[("PK", "t1")], writes=[("PK", "t2")])
                b.op("v", TT(pkv("t1")[:, 0:nct], pkv("v")[:, 0:nct],
                             pkv("t2")[:, 0:nct], ALU.mult),
                     reads=[("PK", "v"), ("PK", "t2")], writes=[("PK", "t1")], inc=VE)
                b.w("s", VE)
                b.op("s", ACT(pkv("t2")[:, 0:nct], pkv("t1")[:, 0:nct], AF.Ln,
                              bias=cc("eps")), reads=[("PK", "t1")],
                     writes=[("PK", "t2")])
                b.op("s", ACT(pkv("t1")[:, 0:nct], pkv("t2")[:, 0:nct], AF.Exp,
                              scale=-0.5), reads=[("PK", "t2")],
                     writes=[("PK", "t1")], inc=SE)
                b.w("v", SE)
                b.op("v", TT(pkv("A")[:, 0:nct], pkv("a")[:, 0:nct],
                             pkv("t1")[:, 0:nct], ALU.mult),
                     reads=[("PK", "a"), ("PK", "t1")], writes=[("PK", "A")])
                b.op("v", TT(pkv("B")[:, 0:nct], pkv("bb")[:, 0:nct],
                             pkv("t1")[:, 0:nct], ALU.mult),
                     reads=[("PK", "bb"), ("PK", "t1")], writes=[("PK", "B")], inc=VE)

        def stats_pass(src, n_feat, tok0, tokens, inner_too):
            nct = tokens // 128
            nsl_ = tokens // 512
            b.w("t", VE, VE.n)
            b.w("s", VE, VE.n)
            b.drain("s")
            sqw = [TMP0[:, i * 512:(i + 1) * 512] for i in range(4)]
            pe_mark = [0] * 4
            for s in range(nsl_):
                t0 = tok0 + s * 512
                for wave in range(2):
                    cs = [4 * wave + i for i in range(4)]
                    for i, c in enumerate(cs):
                        b.w("s", PE, pe_mark[i])
                        b.op("s", ACT(sqw[i], asl(src, c, t0, 512), AF.Square),
                             reads=[(src.name, "st", c)], writes=[("SQ", i)],
                             inc=SE)
                    se_mark = SE.n
                    for tt in range(4):
                        for c in cs:
                            b.op("t", MM(P[tt][:, s:s + 1],
                                         asl(src, c, t0 + tt * 128, 128),
                                         ONESC[:, :], c == 0, c == NC8 - 1))
                    b.w("t", SE, se_mark)
                    for tt in range(4):
                        for i, c in enumerate(cs):
                            b.op("t", MM(P[4 + tt][:, s:s + 1],
                                         sqw[i][:, tt * 128:(tt + 1) * 128],
                                         ONESC[:, :], c == 0, c == NC8 - 1),
                                 inc=PE if (tt == 3 and i == 3) else None)
                    for i in range(4):
                        pe_mark[i] = PE.n
            b.w("v", PE)
            for tt in range(4):
                dstp = pkv("s1").rearrange("p (s t) -> p s t", t=4)[:, 0:nsl_,
                                                                   tt:tt + 1]
                srcp = P[tt][:, 0:nsl_].rearrange("p (s o) -> p s o", o=1)
                b.op("v", CPY(dstp, srcp), writes=[("PK", "s1")])
                dstq = pkv("s2").rearrange("p (s t) -> p s t", t=4)[:, 0:nsl_,
                                                                   tt:tt + 1]
                srcq = P[4 + tt][:, 0:nsl_].rearrange("p (s o) -> p s o", o=1)
                b.op("v", CPY(dstq, srcq), writes=[("PK", "s2")],
                     inc=VE if tt == 3 else None)
            _stats_math(nct, float(n_feat), inner_too)

        def rows_from(atag, btag, scol):
            b.op("v", CPY(pkv("t2")[:, 0:4], pkv(atag)[:, scol:scol + 4]),
                 reads=[("PK", atag)], writes=[("PK", "t2")])
            b.op("v", CPY(pkv("t3")[:, 0:4], pkv(btag)[:, scol:scol + 4]),
                 reads=[("PK", btag)], writes=[("PK", "t3")], inc=VE)
            b.w("t", VE)
            b.w("t", QG, QG.n)
            b.op("t", MM(P[2][0:32, 0:128], PK[:, PKC["t2"]:PKC["t3"] + 16],
                         I128[:, :], True, True), inc=PE)
            b.w("v", PE)
            b.w("v", QG, QG.n)
            b.op("v", CPY(STG32[:, :], P[2][0:32, 0:128]), writes=["STG32"], inc=VE)
            b.w("g", VE)
            b.op("g", DMA(ROW_A[0:1, 0:512], STG32[0:4, :]),
                 reads=["STG32"], writes=["ROW_A"], inc=QG, inck=16)
            b.op("g", DMA(ROW_B[0:1, 0:512], STG32[16:20, :]),
                 reads=["STG32"], writes=["ROW_B"], inc=QG, inck=16)

        def affine_apply(get_src, get_dst, atag, btag, gnm, bnm, pk_s0,
                         tokens, tag):
            nsl_ = tokens // 512
            for s in range(nsl_):
                rows_from(atag, btag, pk_s0 + s * 4)
                b.w("t", QG, QG.n)
                b.w("t", VE)
                b.op("t", MM(P[3][:, 0:512], ONESR[0:1, :], ROW_A[0:1, 0:512],
                             True, True))
                b.op("t", MM(P[4][:, 0:512], ONESR[0:1, :], ROW_B[0:1, 0:512],
                             True, True), inc=PE)
                b.w("v", PE)
                for c in range(NC8):
                    b.op("v", TT(PT[0][:, :], get_src(c, s), P[3][:, 0:512],
                                 ALU.mult), reads=[(tag, "src", c, s)],
                         writes=[("PT", 0)])
                    if gnm is not None:
                        b.op("v", TT(PT[0][:, :], PT[0][:, :], P[4][:, 0:512],
                                     ALU.add), reads=[("PT", 0)], writes=[("PT", 0)])
                        b.op("v", lambda e, o=get_dst(c, s), i=PT[0][:, :],
                             g=prm(gnm, c), bt=prm(bnm, c):
                             e.tensor_scalar(o, i, g, bt, ALU.mult, ALU.add),
                             reads=[("PT", 0)], writes=[(tag, "dst", c, s)], inc=VE)
                    else:
                        b.op("v", TT(get_dst(c, s), PT[0][:, :], P[4][:, 0:512],
                                     ALU.add), reads=[("PT", 0)],
                             writes=[(tag, "dst", c, s)], inc=VE)

        stats_pass(A1, D, 0, S, inner_too=True)
        b.barrier(PH)
        affine_apply(lambda c, s: asl(A1, c, s * 512, 512),
                     lambda c, s: asl(A2, c, s * 512, 512),
                     "A", "B", "g0a", "b0a", 0, S, "a12")
        b.w("g", VE)
        for c in range(NC8):
            b.op("g", DMA(dch("spill")[c], asl(A1, c, 0, S)),
                 reads=[("a12", "spx", c)], inc=QG, inck=16)
        b.w("g", QG)
        b.barrier(PH)

        def finalize(src):
            for c in range(NC8):
                b.op("g", DMA(dch("outT")[c], asl(src, c, 0, S)),
                     reads=[(src.name, "fin", c)], inc=QO, inck=16)
            b.w("g", QO)
            b.w("y", QO)
            block = es.enter_context(nc.Block())
            b.replay(block)
            return nc

        if tap == "xn2":
            return finalize(A2)

        def ldw_qkv(hd, slot, l1):
            base = QKVW_C + slot * 1536
            srcs = ([(dr["qkvT"], hd * DH), (dr["qkvT"], D + hd * DH),
                     (dr["qkvT"], 2 * D + hd * DH)] if l1 else
                    [(dr["wq"], hd * DH), (dr["wk"], hd * DH),
                     (dr["wv"], hd * DH)])
            for j, (mat, col0) in enumerate(srcs):
                src = mat[:, col0:col0 + DH].rearrange("(c p) f -> p c f", p=128)
                dst = POOL[:, base + j * 512: base + j * 512 + 512].rearrange(
                    "p (c f) -> p c f", f=DH)
                b.op("y", DMA(dst, src), inc=QW[slot], inck=16)

        qkvw = lambda slot, j, c: POOL[:, QKVW_C + slot * 1536 + j * 512 + c * DH:
                                       QKVW_C + slot * 1536 + j * 512 + (c + 1) * DH]

        def head_attn(l1, src, oall, hd):
            slot = hd % 2
            slope = float(2.0 ** (-8.0 / (H - hd)))
            b.w("t", QW[slot], QW[slot].n)
            if not l1:
                b.w("y", VE, VE.n)
                b.w("y", SE, SE.n)
                b.op("y", DMA(pl(ESC_C, ESCW), dr["alibi"][hd]), inc=QX, inck=16)
                b.w("s", QX)
                b.op("s", ACT(pl(ESC_C, ESCW), pl(ESC_C, ESCW), AF.Exp),
                     reads=[("POOL", "esc")], writes=[("POOL", "esc")], inc=SE)
            ve_bank = {}
            b.w("t", VE, VE.n)
            b.w("t", SE, SE.n)
            for j, dc0 in ((0, QT_C), (1, KT_C)):
                for s in range(4):
                    bi = (j * 4 + s) % 2
                    bank = P[4 + bi]
                    if (j * 4 + s) >= 2:
                        b.w("t", VE, ve_bank[bi])
                    mm_group(bank[0:64, 0:512],
                             [(qkvw(slot, j, c), asl(src, c, s * 512, 512))
                              for c in range(NC8)], inc=PE)
                    b.w("v", PE, PE.n)
                    b.op("v", CPY(POOL[0:64, dc0 + s * 512: dc0 + (s + 1) * 512],
                                  bank[0:64, 0:512]),
                         writes=[("POOL", "qk", j, s)], inc=VE)
                    ve_bank[bi] = VE.n
            for half in range(2):
                bank = P[6 + half]
                for tt in range(8):
                    t8 = half * 8 + tt
                    mm_group(bank[:, tt * 64:(tt + 1) * 64],
                             [(asl(src, c, t8 * 128, 128), qkvw(slot, 2, c))
                              for c in range(NC8)],
                             inc=PE if tt == 7 else None)
                b.w("v", PE, PE.n)
                sap = bank[:, 0:512].rearrange("p (t f) -> p t f", f=64)
                dap = POOL[:, VAUG_C + half * 520: VAUG_C + half * 520 + 520]\
                    .rearrange("p (t f) -> p t f", f=65)[:, :, 0:64]
                b.op("v", CPY(dap, sap), writes=[("POOL", "vaug", half)], inc=VE)
            if not l1:
                b.w("v", SE, SE.n)
            negC = cc("negC1") if l1 else cc("negC0")
            SB = [P[0], P[1], P[5], P[6]]
            qsl = QKVW_C + slot * 1536
            PTB = [PT[0], PT[1], TMP0[:, 512:1024], TMP0[:, 1024:1536],
                   TMP0[:, 1536:2048], POOL[:, qsl: qsl + 512],
                   POOL[:, qsl + 512: qsl + 1024],
                   POOL[:, qsl + 1024: qsl + 1536]]
            PTK = [("PT", 0), ("PT", 1), "TMP0", "TMP0", "TMP0",
                   ("POOL", "ptb", slot, 0), ("POOL", "ptb", slot, 1),
                   ("POOL", "ptb", slot, 2)]
            pending_evac = [None]

            def emit_evac():
                if pending_evac[0] is None:
                    return
                pvbank, hd_, qj_, pe_pv = pending_evac[0]
                pending_evac[0] = None
                b.w("s", PE, pe_pv)
                b.op("s", ACT(ROWN[0:1, :], pvbank[64:65, 0:512], AF.Ln),
                     writes=["ROWN"])
                b.op("s", ACT(ROWN[0:1, :], ROWN[0:1, :], AF.Exp, scale=-1.0),
                     reads=["ROWN"], writes=["ROWN"], inc=SE)
                b.w("t", SE, SE.n)
                b.op("t", MM(P[4][0:64, 0:512], ONESR[0:1, 0:64], ROWN[0:1, :],
                             True, True), inc=PE)
                b.w("v", PE, PE.n)
                b.op("v", CPY(TMP0[0:64, 0:512], P[4][0:64, 0:512]),
                     writes=["TMP0"], inc=VE)
                orow = (hd_ % 2) * 64
                oc0 = (hd_ // 2) * S + qj_ * 512
                b.op("v", TT(oall[orow:orow + 64, oc0:oc0 + 512],
                             pvbank[0:64, 0:512], TMP0[0:64, 0:512], ALU.mult),
                     reads=["TMP0"], writes=[(oall.name, "o", hd_, qj_)],
                     inc=VE)

            for qj in range(4):
                pvbank = P[2 + (qj % 2)]
                kts = list(range(16)) if not l1 else \
                    (list(range(4, 8)) if qj == 0 else list(range(8)))
                n_kt = len(kts)
                G = n_kt // 4
                se_g, ve_g, pe_g = {}, {}, {}

                def kslice(kt):
                    if not l1:
                        return POOL[0:64, KT_C + kt * 128: KT_C + (kt + 1) * 128]
                    kg0 = (qj - 1) * 512 + kt * 128
                    return POOL[0:64, KT_C + kg0: KT_C + kg0 + 128]

                def emit_pv_group(g):
                    b.w("t", VE, ve_g[g])
                    for i in range(4):
                        idx = g * 4 + i
                        kt = kts[idx]
                        vt = kt if not l1 else (qj - 1) * 4 + kt
                        vaug_t = POOL[:, VAUG_C + vt * 65:
                                      VAUG_C + vt * 65 + 65]
                        b.op("t", MM(pvbank[0:65, 0:512], vaug_t, PTB[idx % 8],
                                     idx == 0, idx == n_kt - 1),
                             inc=PE if idx == n_kt - 1 else None)

                b.w("t", VE, VE.n)
                for g in range(G):
                    if g >= 1:
                        b.w("t", SE, se_g[g - 1])
                    for i in range(4):
                        idx = g * 4 + i
                        b.op("t", MM(SB[i][:, 0:512], kslice(kts[idx]),
                                     POOL[0:64, QT_C + qj * 512:
                                          QT_C + (qj + 1) * 512],
                                     True, True), inc=PE if i == 3 else None)
                    pe_g[g] = PE.n
                    b.w("s", PE, pe_g[g])
                    if g >= 2:
                        b.w("s", VE, ve_g[g - 2])
                    for i in range(4):
                        idx = g * 4 + i
                        r8 = idx % 8
                        b.op("s", ACT(PTB[r8], SB[i][:, 0:512], AF.Exp,
                                      bias=negC, scale=0.125),
                             writes=[PTK[r8]], inc=SE if i == 3 else None)
                    se_g[g] = SE.n
                    b.w("v", SE, se_g[g])
                    for i in range(4):
                        idx = g * 4 + i
                        kt_ = kts[idx]
                        r8 = idx % 8
                        if not l1:
                            OFF = K0 - 128 * kt_ + 512 * qj
                            msl = POOL[:, ESC_C + OFF: ESC_C + OFF + 512]
                            rk = ("POOL", "esc")
                        else:
                            msl = MBAND[:, 896 - 128 * kt_: 1408 - 128 * kt_]
                            rk = "MBAND"
                        b.op("v", TT(PTB[r8], PTB[r8], msl, ALU.mult),
                             reads=[PTK[r8], rk], writes=[PTK[r8]],
                             inc=VE if i == 3 else None)
                    ve_g[g] = VE.n
                    if g == 0:
                        emit_evac()
                    if g >= 1:
                        emit_pv_group(g - 1)
                emit_pv_group(G - 1)
                pending_evac[0] = (pvbank, hd, qj, PE.n)
            emit_evac()

        def attn_layer(l1, src, oall):
            b.w("g", PE, PE.n)
            b.w("g", VE, VE.n)
            for t in range(16):
                b.op("g", lambda e, a=pl(VAUG_C + t * 65 + 64, 1): e.memset(a, 1.0),
                     writes=[("VAUG1", t)], inc=GE)
            b.w("t", GE, GE.n)
            proj_pe_done = [0, 0]
            ldw_qkv(0, 0, l1)
            for hd in range(H):
                head_attn(l1, src, oall, hd)
                proj_pe_done[hd % 2] = PE.n
                if hd + 1 < H:
                    b.w("y", PE, proj_pe_done[(hd + 1) % 2])
                    ldw_qkv(hd + 1, (hd + 1) % 2, l1)

        attn_layer(False, A2, A1)
        b.barrier(PH)
        if tap == "l0heads":
            return finalize(A1)

        def wo_phase(oall, dst, wod, add_affine):
            for i in range(NC8):
                half = WSA_C if i < 4 else WSB_C
                d0 = half + (i % 4) * 1024
                b.op("y", DMA(POOL[:, d0:d0 + 1024],
                              wod.rearrange("(c p) t -> c p t", p=128)[i]),
                     inc=QX, inck=16)
            b.w("t", QX)
            for s_ in range(4):
                if add_affine:
                    rows_from("a", "bb", s_ * 4)
                b.w("g", VE, VE.n)
                for c in range(NC8):
                    b.op("g", DMA(pl(XNC_C + c * 512, 512),
                                  dch("spill")[c][:, s_ * 512:(s_ + 1) * 512]),
                         writes=[("POOL", "xnc", c)], inc=QR[0], inck=16)
                if add_affine:
                    b.w("t", QG, QG.n)
                    b.w("t", VE)
                    b.op("t", MM(P[7][:, 0:512], ONESR[0:1, :],
                                 ROW_A[0:1, 0:512], True, True))
                    b.op("t", MM(P[3][:, 0:512], ONESR[0:1, :],
                                 ROW_B[0:1, 0:512], True, True), inc=PE)
                b.w("v", QR[0], QR[0].n)
                for dc in range(NC8):
                    bank = P[5 + (dc % 2)]
                    b.w("t", VE)
                    pairs = []
                    for i in range(NC8):
                        half = WSA_C if i < 4 else WSB_C
                        lh = POOL[:, half + (i % 4) * 1024 + dc * 128:
                                  half + (i % 4) * 1024 + (dc + 1) * 128]
                        pairs.append((lh, oall[:, i * S + s_ * 512:
                                               i * S + s_ * 512 + 512]))
                    mm_group(bank[:, 0:512], pairs, inc=PE)
                    b.w("v", PE)
                    if add_affine:
                        b.op("v", TT(PT[1][:, :], pl(XNC_C + dc * 512, 512),
                                     P[7][:, 0:512], ALU.mult),
                             reads=[("POOL", "xnc", dc)], writes=[("PT", 1)])
                        b.op("v", TT(PT[1][:, :], PT[1][:, :], P[3][:, 0:512],
                                     ALU.add), reads=[("PT", 1)], writes=[("PT", 1)])
                        b.op("v", TT(asl(dst, dc, s_ * 512, 512), PT[1][:, :],
                                     bank[:, 0:512], ALU.add),
                             reads=[("PT", 1)],
                             writes=[(dst.name, "wo", dc, s_)], inc=VE)
                    else:
                        b.op("v", TT(asl(dst, dc, s_ * 512, 512),
                                     pl(XNC_C + dc * 512, 512),
                                     bank[:, 0:512], ALU.add),
                             reads=[("POOL", "xnc", dc)],
                             writes=[(dst.name, "wo", dc, s_)], inc=VE)

        wo_phase(A1, A2, dr["wo"], add_affine=False)
        b.barrier(PH)
        if tap == "x1pre":
            return finalize(A2)

        def ffn_A(w1d, harena_b, act_bias, ch, evac_fp32=None, cidx=None,
                  first=True):
            pe_slab = [PE.n, PE.n]
            ve_split = [VE.n, VE.n]
            for fg in range(8):
                sl = fg % 2
                wc = WSA_C if sl == 0 else WSB_C
                b.w("y", PE, pe_slab[sl])
                b.w("y", QC[sl], qcg[sl])
                if cidx is not None and not first:
                    b.op("y", DMA(POOLB[:, 2 * wc: 2 * wc + 8192],
                                  wpc(cidx, fg)), inc=QW[sl], inck=16)
                    b.w("t", QW[sl], QW[sl].n)
                else:
                    b.op("y", DMA(POOL[:, wc:wc + 4096]
                                  .rearrange("p (c f) -> p c f", f=512),
                                  w1d[:, fg * 512:(fg + 1) * 512]
                                  .rearrange("(c p) f -> p c f", p=128)),
                         inc=QW[sl], inck=16)
                    b.w("v", QW[sl], QW[sl].n)
                    stg = TMP0[:, 1024:1536]
                    for c in range(NC8):
                        base = 2 * wc + c * 1024
                        b.op("v", CPY(stg,
                                      POOL[:, wc + c * 512: wc + c * 512 + 512]),
                             reads=[("POOL", "wsl", sl)], writes=["TMP0"])
                        b.op("v", CPY(POOLB[:, base: base + 512], stg),
                             reads=["TMP0"], writes=[("POOL", "wsl", sl)])
                        b.op("v", TT(POOLB[:, base + 512: base + 1024], stg,
                                     POOLB[:, base: base + 512], ALU.subtract),
                             reads=["TMP0", ("POOL", "wsl", sl)],
                             writes=[("POOL", "wsl", sl)],
                             inc=VE if c == NC8 - 1 else None)
                    b.w("t", VE, VE.n)
                    if cidx is not None:
                        b.w("g", VE, VE.n)
                        b.op("g", DMA(wpc(cidx, fg),
                                      POOLB[:, 2 * wc: 2 * wc + 8192]),
                             reads=[("POOL", "wsl", sl)], inc=QC[sl], inck=16)
                        qcg[sl] = QC[sl].n
                for ft4 in range(4):
                    ft = fg * 4 + ft4
                    bank = P[ft % 2]
                    b.w("t", SE, max(0, SE.n - 1))
                    for c in range(NC8):
                        base = 2 * wc + c * 1024
                        wh = POOLB[:, base + ft4 * 128: base + ft4 * 128 + 128]
                        wl = POOLB[:, base + 512 + ft4 * 128:
                                   base + 512 + ft4 * 128 + 128]
                        xb = 2 * XNC_C + c * 1024
                        xh = POOLB[:, xb: xb + 512]
                        xl = POOLB[:, xb + 512: xb + 1024]
                        b.op("t", MM(bank[:, 0:512], wh, xh, c == 0, False))
                        b.op("t", MM(bank[:, 0:512], wh, xl, False, False))
                        b.op("t", MM(bank[:, 0:512], wl, xh, False, c == NC8 - 1),
                             inc=PE if c == NC8 - 1 else None)
                    b.w("s", PE, PE.n)
                    if evac_fp32 is not None:
                        evac_fp32(ft, bank)
                    else:
                        stg2 = PT[0][:, :]
                        b.w("s", VE, ve_split[0])
                        if act_bias is None:
                            b.op("s", ACT(stg2, bank[:, 0:512], AF.Relu),
                                 writes=[("PT", 0)], inc=SE)
                        else:
                            b.op("s", ACT(stg2, bank[:, 0:512], AF.Gelu,
                                          bias=prm(act_bias, ft)),
                                 writes=[("PT", 0)], inc=SE)
                        b.w("v", SE, SE.n)
                        b.op("v", CPY(harena_b[:, ft * 1024: ft * 1024 + 512],
                                      stg2),
                             reads=[("PT", 0)], writes=[("HB", ft)])
                        b.op("v", TT(harena_b[:, ft * 1024 + 512:
                                              ft * 1024 + 1024], stg2,
                                     harena_b[:, ft * 1024: ft * 1024 + 512],
                                     ALU.subtract),
                             reads=[("PT", 0), ("HB", ft)],
                             writes=[("HB", ft)], inc=VE)
                        ve_split[0] = VE.n
                pe_slab[sl] = PE.n
            b.w("t", SE)
            b.w("t", VE, VE.n)

        def ffn_B(w2d, hsrc_b, ch, evac, cidx=None, first=True):
            b.w("t", VE, VE.n)
            pe_slab = [PE.n, PE.n]
            for fg in range(8):
                sl = fg % 2
                wc = WSA_C if sl == 0 else WSB_C
                b.w("y", PE, pe_slab[sl])
                b.w("y", QC[sl], qcg[sl])
                if cidx is not None and not first:
                    b.op("y", DMA(POOLB[:, 2 * wc: 2 * wc + 8192],
                                  wpc(cidx, fg)), inc=QW[sl], inck=16)
                    b.w("t", QW[sl], QW[sl].n)
                else:
                    b.op("y", DMA(POOL[:, wc:wc + 4096]
                                  .rearrange("p (c f) -> p c f", f=1024),
                                  w2d[fg * 512:(fg + 1) * 512, :]
                                  .rearrange("(s p) d -> p s d", p=128)),
                         inc=QW[sl], inck=16)
                    b.w("v", QW[sl], QW[sl].n)
                    stg = TMP0[:, 1024:2048]
                    for s4 in range(4):
                        base = 2 * wc + s4 * 2048
                        b.op("v", CPY(stg, POOL[:, wc + s4 * 1024:
                                                wc + s4 * 1024 + 1024]),
                             reads=[("POOL", "wsl", sl)], writes=["TMP0"])
                        b.op("v", CPY(POOLB[:, base: base + 1024], stg),
                             reads=["TMP0"], writes=[("POOL", "wsl", sl)])
                        b.op("v", TT(POOLB[:, base + 1024: base + 2048], stg,
                                     POOLB[:, base: base + 1024], ALU.subtract),
                             reads=["TMP0", ("POOL", "wsl", sl)],
                             writes=[("POOL", "wsl", sl)],
                             inc=VE if s4 == 3 else None)
                    b.w("t", VE, VE.n)
                    if cidx is not None:
                        b.w("g", VE, VE.n)
                        b.op("g", DMA(wpc(cidx, fg),
                                      POOLB[:, 2 * wc: 2 * wc + 8192]),
                             reads=[("POOL", "wsl", sl)], inc=QC[sl], inck=16)
                        qcg[sl] = QC[sl].n
                for ft4 in range(4):
                    ft = fg * 4 + ft4
                    hh = hsrc_b[:, ft * 1024: ft * 1024 + 512]
                    hl = hsrc_b[:, ft * 1024 + 512: ft * 1024 + 1024]
                    for dc in range(NC8):
                        base = 2 * wc + ft4 * 2048 + dc * 128
                        w_h = POOLB[:, base: base + 128]
                        w_l = POOLB[:, base + 1024: base + 1024 + 128]
                        b.op("t", MM(P[dc][:, 0:512], w_h, hh, ft == 0, False))
                        b.op("t", MM(P[dc][:, 0:512], w_h, hl, False, False))
                        b.op("t", MM(P[dc][:, 0:512], w_l, hh, False, ft == 31),
                             inc=PE if (ft4 == 3 and dc == NC8 - 1) else None)
                pe_slab[sl] = PE.n
            b.w("v", PE)
            b.w("s", PE)
            for dc in range(NC8):
                evac(dc)

        def affine_xnc_pairs(get_src, atag, btag, gnm, bnm, pk_s0, tag):
            rows_from(atag, btag, pk_s0)
            b.w("t", QG, QG.n)
            b.w("t", VE)
            b.op("t", MM(P[3][:, 0:512], ONESR[0:1, :], ROW_A[0:1, 0:512],
                         True, True))
            b.op("t", MM(P[4][:, 0:512], ONESR[0:1, :], ROW_B[0:1, 0:512],
                         True, True), inc=PE)
            b.w("v", PE)
            stg = TMP0[:, 1536:2048]
            for c in range(NC8):
                xb = 2 * XNC_C + c * 1024
                b.op("v", TT(PT[0][:, :], get_src(c), P[3][:, 0:512], ALU.mult),
                     reads=[(tag, "src", c)], writes=[("PT", 0)])
                if gnm is not None:
                    b.op("v", TT(PT[0][:, :], PT[0][:, :], P[4][:, 0:512],
                                 ALU.add), reads=[("PT", 0)], writes=[("PT", 0)])
                    b.op("v", lambda e, o=stg, i=PT[0][:, :], g=prm(gnm, c),
                         bt=prm(bnm, c): e.tensor_scalar(o, i, g, bt, ALU.mult,
                                                         ALU.add),
                         reads=[("PT", 0)], writes=["TMP0"])
                else:
                    b.op("v", TT(stg, PT[0][:, :], P[4][:, 0:512], ALU.add),
                         reads=[("PT", 0)], writes=["TMP0"])
                b.op("v", CPY(POOLB[:, xb: xb + 512], stg),
                     reads=["TMP0"], writes=[(tag, "xp", c)])
                b.op("v", TT(POOLB[:, xb + 512: xb + 1024], stg,
                             POOLB[:, xb: xb + 512], ALU.subtract),
                     reads=["TMP0", (tag, "xp", c)],
                     writes=[(tag, "xp", c)], inc=VE)

        def moe_phase():
            for ch in range(NCH):
                t0 = ch * TCH
                b.w("g", PE, PE.n)
                b.w("g", VE, VE.n)
                for c in range(NC8):
                    b.op("g", DMA(pl(XNC_C + c * 512, 512),
                                  dch("spill")[c][:, t0:t0 + 512]),
                         writes=[("POOL", "xnc", c)], inc=QR[0], inck=16)
                b.w("v", QR[0], QR[0].n)
                affine_xnc_pairs(lambda c: pl(XNC_C + c * 512, 512),
                                 "a", "bb", None, None, ch * 4, "POOLxnc")
                for tt in range(4):
                    b.w("t", VE)
                    for c in range(NC8):
                        xb = 2 * XNC_C + c * 1024
                        xh = POOLB[:, xb + tt * 128: xb + tt * 128 + 128]
                        xl = POOLB[:, xb + 512 + tt * 128: xb + 512 + tt * 128 + 128]
                        gh = GWTBB[:, c * 8:(c + 1) * 8]
                        gl = GWTBB[:, 64 + c * 8: 64 + (c + 1) * 8]
                        b.op("t", MM(P[5][:, 0:8], xh, gh, c == 0, False))
                        b.op("t", MM(P[5][:, 0:8], xh, gl, False, False))
                        b.op("t", MM(P[5][:, 0:8], xl, gh, False, c == NC8 - 1),
                             inc=PE if c == NC8 - 1 else None)
                    b.w("v", PE)
                    b.op("v", CPY(GTb[:, :], P[5][:, 0:8]), writes=["GTb"], inc=VE)
                    b.op("v", RED(GCOL[:, 0:1], GTb[:, :], ALU.max),
                         reads=["GTb"], writes=[("GC", 0)])
                    b.op("v", lambda e: e.tensor_scalar(
                        GS1[:, :], GTb[:, :], GCOL[:, 0:1], None, ALU.is_ge),
                        reads=["GTb", ("GC", 0)], writes=["GS1"])
                    b.op("v", STT(GS2[:, :], GS1[:, :], -1e30, GTb[:, :],
                                  ALU.mult, ALU.add), reads=["GS1", "GTb"],
                         writes=["GS2"])
                    b.op("v", RED(GCOL[:, 1:2], GS2[:, :], ALU.max),
                         reads=["GS2"], writes=[("GC", 1)])
                    b.op("v", lambda e: e.tensor_scalar(
                        GS2[:, :], GS2[:, :], GCOL[:, 1:2], None, ALU.is_ge),
                        reads=["GS2", ("GC", 1)], writes=["GS2"])
                    b.op("v", TT(GCOL[:, 2:3], GCOL[:, 0:1], GCOL[:, 1:2],
                                 ALU.subtract), reads=[("GC", 0), ("GC", 1)],
                         writes=[("GC", 2)], inc=VE)
                    b.w("s", VE)
                    b.op("s", ACT(GCOL[:, 3:4], GCOL[:, 2:3], AF.Sigmoid,
                                  scale=-1.0), reads=[("GC", 2)],
                         writes=[("GC", 3)])
                    b.op("s", ACT(GCOL[:, 2:3], GCOL[:, 2:3], AF.Sigmoid),
                         reads=[("GC", 2)], writes=[("GC", 2)], inc=SE)
                    b.w("v", SE)
                    b.op("v", lambda e: e.tensor_scalar(
                        GS1[:, :], GS1[:, :], GCOL[:, 2:3], None, ALU.mult),
                        reads=["GS1", ("GC", 2)], writes=["GS1"])
                    b.op("v", lambda e: e.scalar_tensor_tensor(
                        GS1[:, :], GS2[:, :], GCOL[:, 3:4], GS1[:, :],
                        ALU.mult, ALU.add),
                        reads=["GS1", "GS2", ("GC", 3)], writes=["GS1"], inc=VE)
                    b.w("t", VE)
                    b.op("t", MM(P[6][0:8, 0:128], GS1[:, :], I128[:, :],
                                 True, True), inc=PE)
                    b.w("v", PE)
                    b.w("v", QG, QG.n)
                    b.op("v", CPY(CTS[:, :], P[6][0:8, 0:128]), writes=["STG32"],
                         inc=VE)
                    b.w("g", VE)
                    b.w("g", PE, PE.n)
                    crd0 = bass.AP(TMP0.tensor, tt * 128, [[2048 * 32, 4], [1, 128]])
                    crd1 = bass.AP(TMP0.tensor, 512 + tt * 128,
                                   [[2048 * 32, 4], [1, 128]])
                    b.op("g", DMA(crd0, CTS[0:4, :]), reads=["STG32"],
                         inc=QG, inck=16)
                    b.op("g", DMA(crd1, CTS[4:8, :]), reads=["STG32"],
                         inc=QG, inck=16)
                b.w("t", QG, QG.n)
                for e_ in range(NE):
                    r0 = 32 * (e_ % 4)
                    crow = TMP0[r0:r0 + 1, (e_ // 4) * 512:(e_ // 4) * 512 + 512]
                    b.w("t", VE)
                    b.op("t", lambda e, o=P[5][:, 0:512],
                         a=ONESR[r0:r0 + 1, :], r=crow, tp=(r0, 0):
                         e.matmul(o, a, r, start=True, stop=True,
                                  tile_position=tp), inc=PE)
                    b.w("v", PE)
                    b.op("v", CPY(PT[1][:, :], P[5][:, 0:512]),
                         writes=[("PT", 1)], inc=VE)
                    b.w("t", VE)
                    ffn_A(dr["w1T"][e_], A1B, None, ch, cidx=e_,
                          first=(ch == 0))

                    def moe_evac(dc, t0=t0, ch=ch):
                        b.op("v", TT(PT[0][:, :], P[dc][:, 0:512], PT[1][:, :],
                                     ALU.mult), reads=[("PT", 1)],
                             writes=[("PT", 0)])
                        b.op("v", TT(asl(A2, dc, t0, 512), asl(A2, dc, t0, 512),
                                     PT[0][:, :], ALU.add),
                             reads=[("PT", 0), ("A2", "x1", dc, ch)],
                             writes=[("A2", "x1", dc, ch)], inc=VE)
                    ffn_B(dr["w2T"][e_], A1B, ch, moe_evac, cidx=8 + e_,
                          first=(ch == 0))

        moe_phase()
        b.barrier(PH)
        if tap == "x1":
            return finalize(A2)

        stats_pass(A2, D, 0, S, inner_too=True)
        b.barrier(PH)
        affine_apply(lambda c, s: asl(A2, c, s * 512, 512),
                     lambda c, s: asl(A1, c, s * 512, 512),
                     "A", "B", "g1", "b1", 0, S, "a21")
        b.w("g", VE)
        for c in range(NC8):
            b.op("g", DMA(dch("spill")[c], asl(A2, c, 0, S)),
                 reads=[("a21", "spx", c)], inc=QG, inck=16)
        b.w("g", QG)
        b.barrier(PH)
        if tap == "xn2p":
            return finalize(A1)

        attn_layer(True, A1, A2)
        b.barrier(PH)
        if tap == "l1heads":
            return finalize(A2)

        wo_phase(A2, A1, dr["woT1"], add_affine=True)
        b.barrier(PH)
        if tap == "l1y":
            return finalize(A1)

        def mlp_phase():
            for ch in range(NCH):
                t0 = ch * TCH
                stats_pass(A1, D, t0, 512, inner_too=False)
                affine_xnc_pairs(lambda c: asl(A1, c, t0, 512),
                                 "a", "bb", "g2", "b2", 0, "mlph")
                b.w("t", VE)
                ffn_A(dr["mw1T"], A2B, "mb1", ch, cidx=16, first=(ch == 0))

                def mlp_evac(dc, t0=t0, ch=ch):
                    tb = TMP0[:, 0:512] if dc % 2 == 0 else TMP0[:, 512:1024]
                    if dc >= 2:
                        b.w("s", VE, VE.n - 1)
                    b.op("s", ACT(tb, P[dc][:, 0:512], AF.Identity,
                                  bias=prm("mb2", dc)),
                         writes=["TMP0"], inc=SE)
                    b.w("v", SE, SE.n)
                    b.op("v", TT(asl(A1, dc, t0, 512), asl(A1, dc, t0, 512), tb,
                                 ALU.add),
                         reads=["TMP0", ("A1", "acc", dc, ch)],
                         writes=[("A1", "acc", dc, ch)], inc=VE)
                ffn_B(dr["mw2T"], A2B, ch, mlp_evac, cidx=17,
                      first=(ch == 0))

        mlp_phase()
        b.barrier(PH)
        if tap == "l1branch":
            return finalize(A1)

        def quant(wsrc, wdst, axis_d):
            b.w("g", VE, VE.n)
            b.op("g", lambda e: e.memset(QSC[:, 0:1], 0.0),
                 writes=[("QSC", 0)], inc=GE)
            b.w("v", GE, GE.n)

            def slab_aps(sl_):
                if axis_d:
                    src_ = wsrc[:, sl_ * 512:(sl_ + 1) * 512].rearrange(
                        "(c p) f -> p c f", p=128)
                    dst_ = wdst[:, sl_ * 512:(sl_ + 1) * 512].rearrange(
                        "(c p) f -> p c f", p=128)
                    z = 512
                else:
                    src_ = wsrc[sl_ * 512:(sl_ + 1) * 512, :].rearrange(
                        "(s p) d -> p s d", p=128)
                    dst_ = wdst[sl_ * 512:(sl_ + 1) * 512, :].rearrange(
                        "(s p) d -> p s d", p=128)
                    z = 1024
                return src_, dst_, z

            for sl_ in range(8):
                src_, _, z = slab_aps(sl_)
                b.w("y", VE, VE.n)
                b.w("y", QG, QG.n)
                b.op("y", DMA(POOL[:, WSA_C:WSA_C + 4096].rearrange(
                    "p (a z) -> p a z", z=z), src_), inc=QW[0], inck=16)
                b.w("v", QW[0], QW[0].n)
                b.op("v", RED(QSC[:, 1:2], POOL[:, WSA_C:WSA_C + 4096], ALU.add,
                              absv=True), reads=[("POOL", "qs")],
                     writes=[("QSC", 1)])
                b.op("v", TT(QSC[:, 0:1], QSC[:, 0:1], QSC[:, 1:2], ALU.add),
                     reads=[("QSC", 0), ("QSC", 1)], writes=[("QSC", 0)], inc=VE)
            b.w("t", VE)
            b.op("t", MM(P[5][0:1, 0:1], QSC[:, 0:1], ONESC[:, :], True, True),
                 inc=PE)
            b.w("v", PE)
            b.op("v", CPY(QV[0:1, 0:1], P[5][0:1, 0:1]), writes=[("QV", 0)])
            b.op("v", TSC(QV[0:1, 1:2], QV[0:1, 0:1], 1.0 / (D * FF), 1e-5,
                          ALU.mult, ALU.max),
                 reads=[("QV", 0)], writes=[("QV", 1)])
            b.op("v", RECIP(QV[0:1, 2:3], QV[0:1, 1:2]), reads=[("QV", 1)],
                 writes=[("QV", 2)])
            b.op("v", RECIP(QV[0:1, 3:4], QV[0:1, 2:3]), reads=[("QV", 2)],
                 writes=[("QV", 3)], inc=VE)
            b.w("t", VE)
            b.op("t", MM(P[5][:, 0:2], ONESR[0:1, :], QV[0:1, 2:4], True, True),
                 inc=PE)
            b.w("v", PE)
            b.op("v", CPY(SCB[:, :], P[5][:, 0:2]), writes=["SCB"], inc=VE)
            for sl_ in range(8):
                src_, dst_, z = slab_aps(sl_)
                b.w("y", QG, QG.n)
                b.w("y", VE, VE.n)
                b.op("y", DMA(POOL[:, WSA_C:WSA_C + 4096].rearrange(
                    "p (a z) -> p a z", z=z), src_), inc=QW[0], inck=16)
                b.w("v", QW[0], QW[0].n)
                for hh in range(2):
                    seg = POOL[:, WSA_C + hh * 2048: WSA_C + hh * 2048 + 2048]
                    b.op("v", lambda e, o=TMP0[:, :], i=seg: e.tensor_scalar(
                        o, i, SCB[:, 0:1], 0.5, ALU.mult, ALU.is_gt),
                        reads=[("POOL", "qs", hh), "SCB"],
                        writes=["TMP0"])
                    b.op("v", lambda e, o=pl(WSB_C, 2048), i=seg: e.tensor_scalar(
                        o, i, SCB[:, 0:1], -0.5, ALU.mult, ALU.is_lt),
                        reads=[("POOL", "qs", hh), "SCB"],
                        writes=[("POOL", "ql")])
                    b.op("v", TT(seg, TMP0[:, :], pl(WSB_C, 2048), ALU.subtract),
                         reads=["TMP0", ("POOL", "ql")],
                         writes=[("POOL", "qs", hh)])
                    b.op("v", lambda e, o=seg, i=seg: e.tensor_scalar(
                        o, i, SCB[:, 1:2], None, ALU.mult),
                        reads=[("POOL", "qs", hh), "SCB"],
                        writes=[("POOL", "qs", hh)], inc=VE)
                b.w("g", VE)
                b.op("g", DMA(dst_, POOL[:, WSA_C:WSA_C + 4096].rearrange(
                    "p (a z) -> p a z", z=z)),
                    reads=[("POOL", "qs")], inc=QG, inck=16)
            b.w("g", QG)

        quant(dr["fw1T"], dr["fq1"], True)
        quant(dr["fw2T"], dr["fq2"], False)
        b.barrier(PH)

        def l1ffn_phase():
            for ch in range(NCH):
                t0 = ch * TCH
                b.w("g", PE, PE.n)
                b.w("g", VE, VE.n)
                for c in range(NC8):
                    b.op("g", DMA(pl(XNC_C + c * 512, 512),
                                  dch("spill")[c][:, t0:t0 + 512]),
                         writes=[("POOL", "xnc", c)], inc=QR[0], inck=16)
                b.w("v", QR[0], QR[0].n)
                affine_xnc_pairs(lambda c: pl(XNC_C + c * 512, 512),
                                 "A", "B", "fg1", "fb1", ch * 4, "POOLxnc")
                b.w("t", VE)

                def ffnq_evac(ft, bank, ch=ch):
                    b.op("s", ACT(A2[:, ft * 512:(ft + 1) * 512], bank[:, 0:512],
                                  AF.Gelu, bias=prm("fb1v", ft)),
                         writes=[("A2", "h", ft, ch)], inc=SE)
                ffn_A(dr["fq1"], None, None, ch, evac_fp32=ffnq_evac,
                      cidx=18, first=(ch == 0))
                b.w("s", VE, VE.n)
                b.drain("s")
                b.w("t", VE, VE.n)
                sqw = [TMP0[:, i * 512:(i + 1) * 512] for i in range(4)]
                pe_sq = [0] * 4
                for g4 in range(8):
                    for i in range(4):
                        ft = g4 * 4 + i
                        b.w("s", PE, pe_sq[i])
                        b.op("s", ACT(sqw[i], A2[:, ft * 512:(ft + 1) * 512],
                                      AF.Square), reads=[("A2", "h", ft, ch)],
                             writes=[("SQ", i)], inc=SE)
                    se_mark = SE.n
                    for tt in range(4):
                        for i in range(4):
                            ft = g4 * 4 + i
                            b.op("t", MM(P[tt][:, 0:1],
                                         A2[:, ft * 512 + tt * 128:
                                            ft * 512 + (tt + 1) * 128],
                                         ONESC[:, :], ft == 0, ft == 31))
                    b.w("t", SE, se_mark)
                    for tt in range(4):
                        for i in range(4):
                            ft = g4 * 4 + i
                            b.op("t", MM(P[4 + tt][:, 0:1],
                                         sqw[i][:, tt * 128:(tt + 1) * 128],
                                         ONESC[:, :], ft == 0, ft == 31),
                                 inc=PE if (tt == 3 and i == 3) else None)
                    for i in range(4):
                        pe_sq[i] = PE.n
                b.w("v", PE)
                for tt in range(4):
                    b.op("v", CPY(pkv("s1")[:, tt:tt + 1], P[tt][:, 0:1]),
                         writes=[("PK", "s1")])
                    b.op("v", CPY(pkv("s2")[:, tt:tt + 1], P[4 + tt][:, 0:1]),
                         writes=[("PK", "s2")],
                         inc=VE if tt == 3 else None)
                _stats_math(4, float(FF), False)
                rows_from("a", "bb", 0)
                b.w("t", QG, QG.n)
                b.w("t", VE)
                b.op("t", MM(P[3][:, 0:512], ONESR[0:1, :], ROW_A[0:1, 0:512],
                             True, True))
                b.op("t", MM(P[4][:, 0:512], ONESR[0:1, :], ROW_B[0:1, 0:512],
                             True, True), inc=PE)
                b.w("v", PE)
                stgh = TMP0[:, 1536:2048]
                for ft in range(32):
                    b.op("v", TT(PT[0][:, :], A2[:, ft * 512:(ft + 1) * 512],
                                 P[3][:, 0:512], ALU.mult),
                         reads=[("A2", "h", ft, ch)], writes=[("PT", 0)])
                    b.op("v", TT(PT[0][:, :], PT[0][:, :], P[4][:, 0:512],
                                 ALU.add), reads=[("PT", 0)], writes=[("PT", 0)])
                    b.op("v", lambda e, o=stgh, i=PT[0][:, :], g=prm("fg2", ft),
                         bt=prm("fb2", ft):
                         e.tensor_scalar(o, i, g, bt, ALU.mult, ALU.add),
                         reads=[("PT", 0)], writes=["TMP0"])
                    b.op("v", CPY(A2B[:, ft * 1024: ft * 1024 + 512], stgh),
                         reads=["TMP0"], writes=[("A2", "h", ft, ch)])
                    b.op("v", TT(A2B[:, ft * 1024 + 512: ft * 1024 + 1024], stgh,
                                 A2B[:, ft * 1024: ft * 1024 + 512],
                                 ALU.subtract),
                         reads=["TMP0", ("A2", "h", ft, ch)],
                         writes=[("A2", "h", ft, ch)], inc=VE)

                def ffn_evac(dc, t0=t0, ch=ch):
                    tb = TMP0[:, 0:512] if dc % 2 == 0 else TMP0[:, 512:1024]
                    if dc >= 2:
                        b.w("s", VE, VE.n - 1)
                    b.op("s", ACT(tb, P[dc][:, 0:512], AF.Identity,
                                  bias=prm("fb2v", dc)),
                         writes=["TMP0"], inc=SE)
                    b.w("v", SE, SE.n)
                    b.op("v", TT(asl(A1, dc, t0, 512), asl(A1, dc, t0, 512), tb,
                                 ALU.add),
                         reads=["TMP0", ("A1", "acc2", dc, ch)],
                         writes=[("A1", "acc2", dc, ch)], inc=VE)
                ffn_B(dr["fq2"], A2B, ch, ffn_evac, cidx=19,
                      first=(ch == 0))

        l1ffn_phase()
        b.barrier(PH)

        for c in range(NC8):
            half = c % 2
            rl = pl(XNC_C + half * 2048, 2048)
            ob = pl(WSA_C + half * 2048, 2048)
            qr = QR[half]
            b.w("g", VE, VE.n)
            b.op("g", DMA(rl, dch("spill")[c]), writes=[("POOL", "rl", half)],
                 inc=qr, inck=16)
            b.w("v", qr, qr.n)
            b.w("v", QO, QO.n)
            b.op("v", TT(ob, rl, asl(A1, c, 0, S), ALU.add),
                 reads=[("POOL", "rl", half), ("A1", "fin", c)],
                 writes=[("POOL", "ob", half)], inc=VE)
            b.w("g", VE)
            b.op("g", DMA(dch("outT")[c], ob), reads=[("POOL", "ob", half)],
                 inc=QO, inck=16)
        b.w("g", QO)
        b.w("y", QO)
        block = es.enter_context(nc.Block())
        b.replay(block)
    return nc


def _alibi_table():
    xu = np.array([1060439283, 1060168325, 1059866229, 1059527426, 1059144984,
                   1058710189, 1058211980, 1057636198, 1056964608, 1055382913,
                   1053503477, 1051256226, 1048576000, 1042367767, 1031798784,
                   998244352], dtype=np.uint32)
    sl = xu.view(np.float32)
    p = np.arange(128)
    c = np.arange(ESCW)
    d32 = -np.abs(p[:, None] - c[None, :] + K0).astype(np.float32)
    return np.ascontiguousarray((sl[:, None, None] * d32[None]).astype(np.float32))



def _prep_inputs(ins):
    f32 = lambda a: np.ascontiguousarray(np.asarray(a), dtype=np.float32)
    pack = lambda v: np.ascontiguousarray(f32(v).reshape(-1, 128).T)
    prm = np.zeros((128, 224), np.float32)
    off = 0
    for nm, wd, val in [
            ("g0a", 8, ins["L0_attn_ln_gamma"]), ("b0a", 8, ins["L0_attn_ln_beta"]),
            ("g1", 8, ins["L1_attn_ln1_g"]), ("b1", 8, ins["L1_attn_ln1_b"]),
            ("g2", 8, ins["L1_attn_ln2_g"]), ("b2", 8, ins["L1_attn_ln2_b"]),
            ("mb1", 32, ins["L1_attn_mlp_b1"]), ("mb2", 8, ins["L1_attn_mlp_b2"]),
            ("fg1", 8, ins["L1_ffn_ln1_g"]), ("fb1", 8, ins["L1_ffn_ln1_b"]),
            ("fb1v", 32, ins["L1_ffn_b1"]), ("fg2", 32, ins["L1_ffn_ln2_g"]),
            ("fb2", 32, ins["L1_ffn_ln2_b"]), ("fb2v", 8, ins["L1_ffn_b2"])]:
        prm[:, off:off + wd] = pack(val)
        off += wd
    shared = {
        "wq": f32(ins["L0_attn_Wq"]),
        "wk": f32(ins["L0_attn_Wk"]),
        "wv": f32(ins["L0_attn_Wv"]),
        "wo": f32(ins["L0_attn_Wo"]),
        "gwT": np.ascontiguousarray(f32(ins["L0_ffn_gate_w"]).T),
        "w1T": np.ascontiguousarray(f32(ins["L0_ffn_w1"]).transpose(0, 2, 1)),
        "w2T": np.ascontiguousarray(f32(ins["L0_ffn_w2"]).transpose(0, 2, 1)),
        "qkvT": np.ascontiguousarray(f32(ins["L1_attn_w_qkv"]).T),
        "woT1": np.ascontiguousarray(f32(ins["L1_attn_w_o"]).T),
        "mw1T": np.ascontiguousarray(f32(ins["L1_attn_mlp_w1"]).T),
        "mw2T": np.ascontiguousarray(f32(ins["L1_attn_mlp_w2"]).T),
        "fw1T": np.ascontiguousarray(f32(ins["L1_ffn_w1"]).T),
        "fw2T": np.ascontiguousarray(f32(ins["L1_ffn_w2"]).T),
        "prm": prm,
        "alibi": _alibi_table(),
    }
    x = f32(ins["L0_attn_x"])
    in_maps = []
    for c in range(8):
        m = dict(shared)
        m["xT"] = np.ascontiguousarray(x[c].T)
        in_maps.append(m)
    return in_maps


def kernel(**inputs):
    in_maps = _prep_inputs(inputs)
    nc = build_program(tap=TAP)
    res = run_bass_kernel_spmd(nc, in_maps, core_ids=list(range(8)))
    return np.stack([np.asarray(r["outT"]).T for r in res.results])


# version 09 (final)
# speedup vs baseline: 1.2790x; 1.0237x over previous
import contextlib
import numpy as np
import concourse.bass as bass
import concourse.mybir as mybir
from concourse.bass_utils import run_bass_kernel_spmd

F32 = mybir.dt.float32
BF16 = mybir.dt.bfloat16
AF = mybir.ActivationFunctionType
ALU = mybir.AluOpType
AX = mybir.AxisListType

D, S, NC8, H, DH, FF, NE = 1024, 2048, 8, 16, 64, 4096, 8
TCH, NCH = 512, 4
C0, C1, EPS = 8.0, 4.0, 1e-5
K0, ESCW = 1920, 3968

TAP = None


class Sem:
    __slots__ = ("h", "n")

    def __init__(self, h):
        self.h = h
        self.n = 0


def MM(out, lhsT, rhs, start, stop):
    return lambda e, o=out, a=lhsT, r=rhs, s1=start, s2=stop: e.matmul(
        o, a, r, start=s1, stop=s2)


def CPY(out, in_):
    return lambda e, o=out, i=in_: e.tensor_copy(o, i)


def TT(out, a, b_, op):
    return lambda e, o=out, x=a, y=b_, p=op: e.tensor_tensor(o, x, y, p)


def TSC(out, in_, s1, s2, op0, op1=None):
    if op1 is None:
        return lambda e, o=out, i=in_, a=s1, p=op0: e.tensor_scalar(o, i, a, None, p)
    return lambda e, o=out, i=in_, a=s1, c=s2, p=op0, q=op1: e.tensor_scalar(
        o, i, a, c, p, q)


def STT(out, in0, sc, in1, op0, op1):
    return lambda e, o=out, i=in0, s=sc, j=in1, p=op0, q=op1: \
        e.scalar_tensor_tensor(o, i, s, j, p, q)


def ACT(out, in_, func, bias=0.0, scale=1.0, accum_out=None):
    return lambda e, o=out, i=in_, f=func, b=bias, s=scale, a=accum_out: \
        e.activation(o, i, f, bias=b, scale=s, accum_out=a)


def DMA(out, in_):
    return lambda e, o=out, i=in_: e.dma_start(out=o, in_=i)


def RED(out, in_, op, absv=False):
    return lambda e, o=out, i=in_, p=op, a=absv: e.tensor_reduce(
        o, i, AX.X, p, apply_absolute_value=a)


def RECIP(out, in_):
    return lambda e, o=out, i=in_: e.reciprocal(o, i)


class Bld:
    def __init__(self, nc):
        self.nc = nc
        self.q = {k: [] for k in "ytvsg"}
        self.rd = {k: set() for k in "vsg"}
        self.wr = {k: set() for k in "vsg"}

    def drain(self, ek):
        if ek in self.rd:
            self.rd[ek].clear()
            self.wr[ek].clear()
        self.q[ek].append(lambda e: e.drain())

    def op(self, ek, fn, reads=(), writes=(), inc=None, inck=1):
        if ek in self.rd:
            rs, ws = set(reads), set(writes)
            if (rs & self.wr[ek]) or (ws & (self.wr[ek] | self.rd[ek])):
                self.q[ek].append(lambda e: e.drain())
                self.rd[ek].clear()
                self.wr[ek].clear()
            self.rd[ek] |= rs
            self.wr[ek] |= ws
        if inc is not None:
            inc.n += inck
            self.q[ek].append(lambda e, f=fn, s=inc.h, k=inck: f(e).then_inc(s, k))
        else:
            self.q[ek].append(fn)

    def w(self, ek, sem, val=None):
        v = sem.n if val is None else val
        if v <= 0:
            return
        self.q[ek].append(lambda e, s=sem.h, vv=v: e.wait_ge(s, vv))

    def barrier(self, ph):
        for ek in "ytvsg":
            if ek in self.rd:
                self.rd[ek].clear()
                self.wr[ek].clear()
            self.q[ek].append(lambda e: e.drain())
            ph.n += 1
            self.q[ek].append(lambda e, s=ph.h: e.sem_inc(s, 1))
        for ek in "ytvsg":
            self.q[ek].append(lambda e, s=ph.h, v=ph.n: e.wait_ge(s, v))

    def replay(self, block):
        q = self.q

        @block.sync
        def _(e):
            for f in q["y"]:
                f(e)

        @block.tensor
        def _(e):
            for f in q["t"]:
                f(e)

        @block.vector
        def _(e):
            for f in q["v"]:
                f(e)

        @block.scalar
        def _(e):
            for f in q["s"]:
                f(e)

        @block.gpsimd
        def _(e):
            for f in q["g"]:
                f(e)


def build_program(tap=None):
    nc = bass.Bass("TRN2", target_bir_lowering=False)
    dtn = lambda n, sh, kind: nc.dram_tensor(n, sh, F32, kind=kind).ap()
    dr = {
        "xT": dtn("xT", [D, S], "ExternalInput"),
        "wq": dtn("wq", [D, D], "ExternalInput"),
        "wk": dtn("wk", [D, D], "ExternalInput"),
        "wv": dtn("wv", [D, D], "ExternalInput"),
        "wo": dtn("wo", [D, D], "ExternalInput"),
        "gwT": dtn("gwT", [D, NE], "ExternalInput"),
        "w1T": dtn("w1T", [NE, D, FF], "ExternalInput"),
        "w2T": dtn("w2T", [NE, FF, D], "ExternalInput"),
        "qkvT": dtn("qkvT", [D, 3 * D], "ExternalInput"),
        "woT1": dtn("woT1", [D, D], "ExternalInput"),
        "mw1T": dtn("mw1T", [D, FF], "ExternalInput"),
        "mw2T": dtn("mw2T", [FF, D], "ExternalInput"),
        "fw1T": dtn("fw1T", [D, FF], "ExternalInput"),
        "fw2T": dtn("fw2T", [FF, D], "ExternalInput"),
        "prm": dtn("prm", [128, 224], "ExternalInput"),
        "alibi": dtn("alibi", [H, 128, ESCW], "ExternalInput"),
        "spill": dtn("spill", [D, S], "Internal"),
        "fq1": dtn("fq1", [D, FF], "Internal"),
        "fq2": dtn("fq2", [FF, D], "Internal"),
        "outT": dtn("outT", [D, S], "ExternalOutput"),
        "wpcA": nc.dram_tensor("wpcA", [10, 8, 128, 8192], BF16,
                               kind="Internal").ap(),
        "wpcB": nc.dram_tensor("wpcB", [10, 8, 128, 8192], BF16,
                               kind="Internal").ap(),
    }
    dch = lambda name: dr[name].rearrange("(c p) t -> c p t", p=128)
    wpc = lambda ci, fg: (dr["wpcA"][ci, fg] if ci < 10
                          else dr["wpcB"][ci - 10, fg])

    es = contextlib.ExitStack()
    with es:
        sbt = lambda n, sh: es.enter_context(nc.sbuf_tensor(n, sh, F32)).ap()
        pst = lambda n, sh: es.enter_context(nc.psum_tensor(n, sh, F32)).ap()
        A1 = sbt("A1", [128, NC8 * S])
        A2 = sbt("A2", [128, NC8 * S])
        POOL = sbt("POOL", [128, 12288])
        PT = [sbt("PTa", [128, 512]), sbt("PTb", [128, 512])]
        TMP0 = sbt("TMP0", [128, 2048])
        PK = sbt("PK", [128, 192])
        STG32 = sbt("STG32", [32, 128])
        ROW_A = sbt("ROW_A", [1, 512])
        ROW_B = sbt("ROW_B", [1, 512])
        ROWN = sbt("ROWN", [1, 512])
        GTb = sbt("GTb", [128, 8])
        GS1 = sbt("GS1", [128, 8])
        GS2 = sbt("GS2", [128, 8])
        GCOL = sbt("GCOL", [128, 4])
        I128 = sbt("I128", [128, 128])
        MBAND = sbt("MBAND", [128, 1408])
        ONESR = sbt("ONESR", [97, 128])
        ONESC = sbt("ONESC", [128, 1])
        CONSTS = sbt("CONSTS", [128, 8])
        PRM = sbt("PRM", [128, 224])
        GWT = sbt("GWT", [128, 64])
        GWTB = sbt("GWTB", [128, 64])
        QV = sbt("QV", [1, 4])
        SCB = sbt("SCB", [128, 2])
        QSC = sbt("QSC", [128, 2])
        CTS = STG32[0:8, :]
        P = [pst("P%d" % i, [128, 512]) for i in range(8)]

        SMN = ["PH", "QW0", "QW1", "QX", "QG", "QR0", "QR1", "QO",
               "PE", "VE", "SE", "GE", "QC0", "QC1"]
        SM = {n: Sem(es.enter_context(nc.semaphore(n))) for n in SMN}
        PH, QX, QG, QO = SM["PH"], SM["QX"], SM["QG"], SM["QO"]
        PE, VE, SE, GE = SM["PE"], SM["VE"], SM["SE"], SM["GE"]
        QW = [SM["QW0"], SM["QW1"]]
        QR = [SM["QR0"], SM["QR1"]]
        QC = [SM["QC0"], SM["QC1"]]
        qcg = [0, 0]

        b = Bld(nc)

        CC = {"eps": 0, "negC0": 1, "negC1": 2, "k0": 3}
        cc = lambda n: CONSTS[:, CC[n]:CC[n] + 1]
        PKC = {n: i * 16 for i, n in enumerate(
            ["s1", "s2", "m", "v", "a", "bb", "A", "B", "t0", "t1", "t2", "t3"])}
        pkv = lambda n, w=16: PK[:, PKC[n]:PKC[n] + w]
        PRL = [("g0a", 8), ("b0a", 8), ("g1", 8), ("b1", 8), ("g2", 8), ("b2", 8),
               ("mb1", 32), ("mb2", 8), ("fg1", 8), ("fb1", 8), ("fb1v", 32),
               ("fg2", 32), ("fb2", 32), ("fb2v", 8)]
        PRO = {}
        _o = 0
        for _nm, _wd in PRL:
            PRO[_nm] = _o
            _o += _wd
        assert _o <= 224
        prm = lambda nm, j: PRM[:, PRO[nm] + j:PRO[nm] + j + 1]

        QKVW_C, QT_C, KT_C, VAUG_C, ESC_C = 0, 3072, 5120, 7168, 8208
        WSA_C, WSB_C, XNC_C = 0, 4096, 8192
        pl = lambda c0, w, r=128: POOL[0:r, c0:c0 + w]
        asl = lambda ar, c, t0, w: ar[:, c * S + t0: c * S + t0 + w]
        POOLB = POOL.bitcast(BF16)
        A1B = A1.bitcast(BF16)
        A2B = A2.bitcast(BF16)
        GWTBB = GWTB.bitcast(BF16)
        plb = lambda c0, w, r=128: POOLB[0:r, c0:c0 + w]

        def mm_group(psum_ap, pairs, inc=None):
            n = len(pairs)
            for i, (a_, r_) in enumerate(pairs):
                b.op("t", MM(psum_ap, a_, r_, i == 0, i == n - 1),
                     inc=inc if i == n - 1 else None)

        b.op("y", DMA(PRM[:, :], dr["prm"][:, :]), inc=QX, inck=16)
        b.op("y", DMA(GWT[:, :].rearrange("p (c f) -> p c f", f=8),
                      dr["gwT"].rearrange("(c p) f -> p c f", p=128)), inc=QX, inck=16)
        for c in range(NC8):
            b.op("y", DMA(asl(A1, c, 0, S), dch("xT")[c]), inc=QX, inck=16)
        for nm, val in [("eps", EPS), ("negC0", -C0), ("negC1", -C1),
                        ("k0", float(K0))]:
            b.op("g", lambda e, a=cc(nm), v=val: e.memset(a, v),
                 writes=[("C", nm)], inc=GE)
        b.op("g", lambda e: e.memset(PK[:, :], 0.0), writes=["PKZ"], inc=GE)
        for _r in (0, 32, 64, 96):
            b.op("g", lambda e, a=ONESR[_r:_r + 1, :]: e.memset(a, 1.0),
                 writes=["ONESR"], inc=GE)
        b.op("g", lambda e: e.memset(ONESC[:, :], 1.0), writes=["ONESC"], inc=GE)
        b.op("g", lambda e: e.iota(I128[:, :], [[1, 128]], channel_multiplier=-1,
                                   allow_small_or_imprecise_dtypes=True),
             writes=["I128"], inc=GE)
        b.op("g", lambda e: e.iota(MBAND[:, :], [[1, 1408]], channel_multiplier=-1,
                                   allow_small_or_imprecise_dtypes=True),
             writes=["MBAND"], inc=GE)
        for t in range(16):
            b.op("g", lambda e, a=pl(VAUG_C + t * 65 + 64, 1): e.memset(a, 1.0),
                 writes=[("VAUG1", t)], inc=GE)
        b.w("v", GE)
        b.op("v", TSC(I128[:, :], I128[:, :], 0.0, None, ALU.is_equal),
             reads=["I128"], writes=["I128"], inc=VE)
        b.op("v", TSC(TMP0[:, 0:1408], MBAND[:, :], 384.0, None, ALU.is_ge),
             reads=["MBAND"], writes=["TMP0"])
        b.op("v", TSC(pl(ESC_C, 1408), MBAND[:, :], 896.0, None, ALU.is_lt),
             reads=["MBAND"], writes=[("POOL", "mb")])
        b.op("v", TT(MBAND[:, :], TMP0[:, 0:1408], pl(ESC_C, 1408), ALU.mult),
             reads=["TMP0", ("POOL", "mb")], writes=["MBAND"], inc=VE)
        b.w("v", QX)
        b.op("v", CPY(GWTBB[:, 0:64], GWT[:, :]), reads=["GWT"],
             writes=[("GWTB", 0)])
        b.op("v", TT(GWTBB[:, 64:128], GWT[:, :], GWTBB[:, 0:64], ALU.subtract),
             reads=["GWT", ("GWTB", 0)], writes=[("GWTB", 1)], inc=VE)
        b.w("t", QX)
        b.w("s", QX)
        b.barrier(PH)

        def _stats_math(nct, nf, inner_too):
            b.op("v", TSC(pkv("m")[:, 0:nct], pkv("s1")[:, 0:nct], 1.0 / nf, None,
                          ALU.mult), reads=[("PK", "s1")], writes=[("PK", "m")])
            b.op("v", TSC(pkv("v")[:, 0:nct], pkv("s2")[:, 0:nct], 1.0 / nf, None,
                          ALU.mult), reads=[("PK", "s2")], writes=[("PK", "v")])
            b.op("v", TT(pkv("t0")[:, 0:nct], pkv("m")[:, 0:nct], pkv("m")[:, 0:nct],
                         ALU.mult), reads=[("PK", "m")], writes=[("PK", "t0")])
            b.op("v", TT(pkv("v")[:, 0:nct], pkv("v")[:, 0:nct], pkv("t0")[:, 0:nct],
                         ALU.subtract), reads=[("PK", "t0"), ("PK", "v")],
                 writes=[("PK", "v")], inc=VE)
            b.w("s", VE)
            b.op("s", ACT(pkv("t0")[:, 0:nct], pkv("v")[:, 0:nct], AF.Ln,
                          bias=cc("eps")), reads=[("PK", "v")], writes=[("PK", "t0")])
            b.op("s", ACT(pkv("a")[:, 0:nct], pkv("t0")[:, 0:nct], AF.Exp, scale=-0.5),
                 reads=[("PK", "t0")], writes=[("PK", "a")], inc=SE)
            b.w("v", SE)
            b.op("v", STT(pkv("bb")[:, 0:nct], pkv("m")[:, 0:nct], -1.0,
                          pkv("a")[:, 0:nct], ALU.mult, ALU.mult),
                 reads=[("PK", "m"), ("PK", "a")], writes=[("PK", "bb")], inc=VE)
            if inner_too:
                b.op("v", TSC(pkv("t1")[:, 0:nct], pkv("v")[:, 0:nct], EPS, None,
                              ALU.add), reads=[("PK", "v")], writes=[("PK", "t1")])
                b.op("v", RECIP(pkv("t2")[:, 0:nct], pkv("t1")[:, 0:nct]),
                     reads=[("PK", "t1")], writes=[("PK", "t2")])
                b.op("v", TT(pkv("t1")[:, 0:nct], pkv("v")[:, 0:nct],
                             pkv("t2")[:, 0:nct], ALU.mult),
                     reads=[("PK", "v"), ("PK", "t2")], writes=[("PK", "t1")], inc=VE)
                b.w("s", VE)
                b.op("s", ACT(pkv("t2")[:, 0:nct], pkv("t1")[:, 0:nct], AF.Ln,
                              bias=cc("eps")), reads=[("PK", "t1")],
                     writes=[("PK", "t2")])
                b.op("s", ACT(pkv("t1")[:, 0:nct], pkv("t2")[:, 0:nct], AF.Exp,
                              scale=-0.5), reads=[("PK", "t2")],
                     writes=[("PK", "t1")], inc=SE)
                b.w("v", SE)
                b.op("v", TT(pkv("A")[:, 0:nct], pkv("a")[:, 0:nct],
                             pkv("t1")[:, 0:nct], ALU.mult),
                     reads=[("PK", "a"), ("PK", "t1")], writes=[("PK", "A")])
                b.op("v", TT(pkv("B")[:, 0:nct], pkv("bb")[:, 0:nct],
                             pkv("t1")[:, 0:nct], ALU.mult),
                     reads=[("PK", "bb"), ("PK", "t1")], writes=[("PK", "B")], inc=VE)

        def stats_pass(src, n_feat, tok0, tokens, inner_too):
            nct = tokens // 128
            nsl_ = tokens // 512
            b.w("t", VE, VE.n)
            b.w("s", VE, VE.n)
            b.drain("s")
            sqw = [TMP0[:, i * 512:(i + 1) * 512] for i in range(4)]
            pe_mark = [0] * 4
            for s in range(nsl_):
                t0 = tok0 + s * 512
                for wave in range(2):
                    cs = [4 * wave + i for i in range(4)]
                    for i, c in enumerate(cs):
                        b.w("s", PE, pe_mark[i])
                        b.op("s", ACT(sqw[i], asl(src, c, t0, 512), AF.Square),
                             reads=[(src.name, "st", c)], writes=[("SQ", i)],
                             inc=SE)
                    se_mark = SE.n
                    for tt in range(4):
                        for c in cs:
                            b.op("t", MM(P[tt][:, s:s + 1],
                                         asl(src, c, t0 + tt * 128, 128),
                                         ONESC[:, :], c == 0, c == NC8 - 1))
                    b.w("t", SE, se_mark)
                    for tt in range(4):
                        for i, c in enumerate(cs):
                            b.op("t", MM(P[4 + tt][:, s:s + 1],
                                         sqw[i][:, tt * 128:(tt + 1) * 128],
                                         ONESC[:, :], c == 0, c == NC8 - 1),
                                 inc=PE if (tt == 3 and i == 3) else None)
                    for i in range(4):
                        pe_mark[i] = PE.n
            b.w("v", PE)
            for tt in range(4):
                dstp = pkv("s1").rearrange("p (s t) -> p s t", t=4)[:, 0:nsl_,
                                                                   tt:tt + 1]
                srcp = P[tt][:, 0:nsl_].rearrange("p (s o) -> p s o", o=1)
                b.op("v", CPY(dstp, srcp), writes=[("PK", "s1")])
                dstq = pkv("s2").rearrange("p (s t) -> p s t", t=4)[:, 0:nsl_,
                                                                   tt:tt + 1]
                srcq = P[4 + tt][:, 0:nsl_].rearrange("p (s o) -> p s o", o=1)
                b.op("v", CPY(dstq, srcq), writes=[("PK", "s2")],
                     inc=VE if tt == 3 else None)
            _stats_math(nct, float(n_feat), inner_too)

        def rows_from(atag, btag, scol):
            b.op("v", CPY(pkv("t2")[:, 0:4], pkv(atag)[:, scol:scol + 4]),
                 reads=[("PK", atag)], writes=[("PK", "t2")])
            b.op("v", CPY(pkv("t3")[:, 0:4], pkv(btag)[:, scol:scol + 4]),
                 reads=[("PK", btag)], writes=[("PK", "t3")], inc=VE)
            b.w("t", VE)
            b.w("t", QG, QG.n)
            b.op("t", MM(P[2][0:32, 0:128], PK[:, PKC["t2"]:PKC["t3"] + 16],
                         I128[:, :], True, True), inc=PE)
            b.w("v", PE)
            b.w("v", QG, QG.n)
            b.op("v", CPY(STG32[:, :], P[2][0:32, 0:128]), writes=["STG32"], inc=VE)
            b.w("g", VE)
            b.op("g", DMA(ROW_A[0:1, 0:512], STG32[0:4, :]),
                 reads=["STG32"], writes=["ROW_A"], inc=QG, inck=16)
            b.op("g", DMA(ROW_B[0:1, 0:512], STG32[16:20, :]),
                 reads=["STG32"], writes=["ROW_B"], inc=QG, inck=16)

        def affine_apply(get_src, get_dst, atag, btag, gnm, bnm, pk_s0,
                         tokens, tag):
            nsl_ = tokens // 512
            for s in range(nsl_):
                rows_from(atag, btag, pk_s0 + s * 4)
                b.w("t", QG, QG.n)
                b.w("t", VE)
                b.op("t", MM(P[3][:, 0:512], ONESR[0:1, :], ROW_A[0:1, 0:512],
                             True, True))
                b.op("t", MM(P[4][:, 0:512], ONESR[0:1, :], ROW_B[0:1, 0:512],
                             True, True), inc=PE)
                b.w("v", PE)
                for c in range(NC8):
                    b.op("v", TT(PT[0][:, :], get_src(c, s), P[3][:, 0:512],
                                 ALU.mult), reads=[(tag, "src", c, s)],
                         writes=[("PT", 0)])
                    if gnm is not None:
                        b.op("v", TT(PT[0][:, :], PT[0][:, :], P[4][:, 0:512],
                                     ALU.add), reads=[("PT", 0)], writes=[("PT", 0)])
                        b.op("v", lambda e, o=get_dst(c, s), i=PT[0][:, :],
                             g=prm(gnm, c), bt=prm(bnm, c):
                             e.tensor_scalar(o, i, g, bt, ALU.mult, ALU.add),
                             reads=[("PT", 0)], writes=[(tag, "dst", c, s)], inc=VE)
                    else:
                        b.op("v", TT(get_dst(c, s), PT[0][:, :], P[4][:, 0:512],
                                     ALU.add), reads=[("PT", 0)],
                             writes=[(tag, "dst", c, s)], inc=VE)

        stats_pass(A1, D, 0, S, inner_too=True)
        b.barrier(PH)
        affine_apply(lambda c, s: asl(A1, c, s * 512, 512),
                     lambda c, s: asl(A2, c, s * 512, 512),
                     "A", "B", "g0a", "b0a", 0, S, "a12")
        b.w("g", VE)
        for c in range(NC8):
            b.op("g", DMA(dch("spill")[c], asl(A1, c, 0, S)),
                 reads=[("a12", "spx", c)], inc=QG, inck=16)
        b.w("g", QG)
        b.barrier(PH)

        def finalize(src):
            for c in range(NC8):
                b.op("g", DMA(dch("outT")[c], asl(src, c, 0, S)),
                     reads=[(src.name, "fin", c)], inc=QO, inck=16)
            b.w("g", QO)
            b.w("y", QO)
            block = es.enter_context(nc.Block())
            b.replay(block)
            return nc

        if tap == "xn2":
            return finalize(A2)

        def ldw_qkv(hd, slot, l1):
            base = QKVW_C + slot * 1536
            srcs = ([(dr["qkvT"], hd * DH), (dr["qkvT"], D + hd * DH),
                     (dr["qkvT"], 2 * D + hd * DH)] if l1 else
                    [(dr["wq"], hd * DH), (dr["wk"], hd * DH),
                     (dr["wv"], hd * DH)])
            for j, (mat, col0) in enumerate(srcs):
                src = mat[:, col0:col0 + DH].rearrange("(c p) f -> p c f", p=128)
                dst = POOL[:, base + j * 512: base + j * 512 + 512].rearrange(
                    "p (c f) -> p c f", f=DH)
                b.op("y", DMA(dst, src), inc=QW[slot], inck=16)

        qkvw = lambda slot, j, c: POOL[:, QKVW_C + slot * 1536 + j * 512 + c * DH:
                                       QKVW_C + slot * 1536 + j * 512 + (c + 1) * DH]

        def head_attn(l1, src, oall, hd):
            slot = hd % 2
            slope = float(2.0 ** (-8.0 / (H - hd)))
            b.w("t", QW[slot], QW[slot].n)
            if not l1:
                b.w("y", VE, VE.n)
                b.w("y", SE, SE.n)
                b.op("y", DMA(pl(ESC_C, ESCW), dr["alibi"][hd]), inc=QX, inck=16)
                b.w("s", QX)
                b.op("s", ACT(pl(ESC_C, ESCW), pl(ESC_C, ESCW), AF.Exp),
                     reads=[("POOL", "esc")], writes=[("POOL", "esc")], inc=SE)
            ve_bank = {}
            b.w("t", VE, VE.n)
            b.w("t", SE, SE.n)
            for j, dc0 in ((0, QT_C), (1, KT_C)):
                for s in range(4):
                    bi = (j * 4 + s) % 2
                    bank = P[4 + bi]
                    if (j * 4 + s) >= 2:
                        b.w("t", VE, ve_bank[bi])
                    mm_group(bank[0:64, 0:512],
                             [(qkvw(slot, j, c), asl(src, c, s * 512, 512))
                              for c in range(NC8)], inc=PE)
                    b.w("v", PE, PE.n)
                    b.op("v", CPY(POOL[0:64, dc0 + s * 512: dc0 + (s + 1) * 512],
                                  bank[0:64, 0:512]),
                         writes=[("POOL", "qk", j, s)], inc=VE)
                    ve_bank[bi] = VE.n
            for half in range(2):
                bank = P[6 + half]
                for tt in range(8):
                    t8 = half * 8 + tt
                    mm_group(bank[:, tt * 64:(tt + 1) * 64],
                             [(asl(src, c, t8 * 128, 128), qkvw(slot, 2, c))
                              for c in range(NC8)],
                             inc=PE if tt == 7 else None)
                b.w("v", PE, PE.n)
                sap = bank[:, 0:512].rearrange("p (t f) -> p t f", f=64)
                dap = POOL[:, VAUG_C + half * 520: VAUG_C + half * 520 + 520]\
                    .rearrange("p (t f) -> p t f", f=65)[:, :, 0:64]
                b.op("v", CPY(dap, sap), writes=[("POOL", "vaug", half)], inc=VE)
            if not l1:
                b.w("v", SE, SE.n)
            negC = cc("negC1") if l1 else cc("negC0")
            SB = [P[0], P[1], P[5], P[6]]
            qsl = QKVW_C + slot * 1536
            PTB = [PT[0], PT[1], TMP0[:, 512:1024], TMP0[:, 1024:1536],
                   TMP0[:, 1536:2048], POOL[:, qsl: qsl + 512],
                   POOL[:, qsl + 512: qsl + 1024],
                   POOL[:, qsl + 1024: qsl + 1536]]
            PTK = [("PT", 0), ("PT", 1), "TMP0", "TMP0", "TMP0",
                   ("POOL", "ptb", slot, 0), ("POOL", "ptb", slot, 1),
                   ("POOL", "ptb", slot, 2)]
            pending_evac = [None]

            def emit_evac():
                if pending_evac[0] is None:
                    return
                pvbank, hd_, qj_, pe_pv = pending_evac[0]
                pending_evac[0] = None
                b.w("s", PE, pe_pv)
                b.op("s", ACT(ROWN[0:1, :], pvbank[64:65, 0:512], AF.Ln),
                     writes=["ROWN"])
                b.op("s", ACT(ROWN[0:1, :], ROWN[0:1, :], AF.Exp, scale=-1.0),
                     reads=["ROWN"], writes=["ROWN"], inc=SE)
                b.w("t", SE, SE.n)
                b.op("t", MM(P[4][0:64, 0:512], ONESR[0:1, 0:64], ROWN[0:1, :],
                             True, True), inc=PE)
                b.w("v", PE, PE.n)
                b.op("v", CPY(TMP0[0:64, 0:512], P[4][0:64, 0:512]),
                     writes=["TMP0"], inc=VE)
                orow = (hd_ % 2) * 64
                oc0 = (hd_ // 2) * S + qj_ * 512
                b.op("v", TT(oall[orow:orow + 64, oc0:oc0 + 512],
                             pvbank[0:64, 0:512], TMP0[0:64, 0:512], ALU.mult),
                     reads=["TMP0"], writes=[(oall.name, "o", hd_, qj_)],
                     inc=VE)

            for qj in range(4):
                pvbank = P[2 + (qj % 2)]
                kts = list(range(16)) if not l1 else \
                    (list(range(4, 8)) if qj == 0 else list(range(8)))
                n_kt = len(kts)
                G = n_kt // 4
                se_g, ve_g, pe_g = {}, {}, {}

                def kslice(kt):
                    if not l1:
                        return POOL[0:64, KT_C + kt * 128: KT_C + (kt + 1) * 128]
                    kg0 = (qj - 1) * 512 + kt * 128
                    return POOL[0:64, KT_C + kg0: KT_C + kg0 + 128]

                def emit_pv_group(g):
                    b.w("t", VE, ve_g[g])
                    for i in range(4):
                        idx = g * 4 + i
                        kt = kts[idx]
                        vt = kt if not l1 else (qj - 1) * 4 + kt
                        vaug_t = POOL[:, VAUG_C + vt * 65:
                                      VAUG_C + vt * 65 + 65]
                        b.op("t", MM(pvbank[0:65, 0:512], vaug_t, PTB[idx % 8],
                                     idx == 0, idx == n_kt - 1),
                             inc=PE if idx == n_kt - 1 else None)

                b.w("t", VE, VE.n)
                for g in range(G):
                    if g >= 1:
                        b.w("t", SE, se_g[g - 1])
                    for i in range(4):
                        idx = g * 4 + i
                        b.op("t", MM(SB[i][:, 0:512], kslice(kts[idx]),
                                     POOL[0:64, QT_C + qj * 512:
                                          QT_C + (qj + 1) * 512],
                                     True, True), inc=PE if i == 3 else None)
                    pe_g[g] = PE.n
                    b.w("s", PE, pe_g[g])
                    if g >= 2:
                        b.w("s", VE, ve_g[g - 2])
                    for i in range(4):
                        idx = g * 4 + i
                        r8 = idx % 8
                        b.op("s", ACT(PTB[r8], SB[i][:, 0:512], AF.Exp,
                                      bias=negC, scale=0.125),
                             writes=[PTK[r8]], inc=SE if i == 3 else None)
                    se_g[g] = SE.n
                    b.w("v", SE, se_g[g])
                    for i in range(4):
                        idx = g * 4 + i
                        kt_ = kts[idx]
                        r8 = idx % 8
                        if not l1:
                            OFF = K0 - 128 * kt_ + 512 * qj
                            msl = POOL[:, ESC_C + OFF: ESC_C + OFF + 512]
                            rk = ("POOL", "esc")
                        else:
                            msl = MBAND[:, 896 - 128 * kt_: 1408 - 128 * kt_]
                            rk = "MBAND"
                        b.op("v", TT(PTB[r8], PTB[r8], msl, ALU.mult),
                             reads=[PTK[r8], rk], writes=[PTK[r8]],
                             inc=VE if i == 3 else None)
                    ve_g[g] = VE.n
                    if g == 0:
                        emit_evac()
                    if g >= 1:
                        emit_pv_group(g - 1)
                emit_pv_group(G - 1)
                pending_evac[0] = (pvbank, hd, qj, PE.n)
            emit_evac()

        def attn_layer(l1, src, oall):
            b.w("g", PE, PE.n)
            b.w("g", VE, VE.n)
            for t in range(16):
                b.op("g", lambda e, a=pl(VAUG_C + t * 65 + 64, 1): e.memset(a, 1.0),
                     writes=[("VAUG1", t)], inc=GE)
            b.w("t", GE, GE.n)
            proj_pe_done = [0, 0]
            ldw_qkv(0, 0, l1)
            for hd in range(H):
                head_attn(l1, src, oall, hd)
                proj_pe_done[hd % 2] = PE.n
                if hd + 1 < H:
                    b.w("y", PE, proj_pe_done[(hd + 1) % 2])
                    ldw_qkv(hd + 1, (hd + 1) % 2, l1)

        attn_layer(False, A2, A1)
        b.barrier(PH)
        if tap == "l0heads":
            return finalize(A1)

        def wo_phase(oall, dst, wod, add_affine):
            for i in range(NC8):
                half = WSA_C if i < 4 else WSB_C
                d0 = half + (i % 4) * 1024
                b.op("y", DMA(POOL[:, d0:d0 + 1024],
                              wod.rearrange("(c p) t -> c p t", p=128)[i]),
                     inc=QX, inck=16)
            b.w("t", QX)
            for s_ in range(4):
                if add_affine:
                    rows_from("a", "bb", s_ * 4)
                b.w("g", VE, VE.n)
                for c in range(NC8):
                    b.op("g", DMA(pl(XNC_C + c * 512, 512),
                                  dch("spill")[c][:, s_ * 512:(s_ + 1) * 512]),
                         writes=[("POOL", "xnc", c)], inc=QR[0], inck=16)
                if add_affine:
                    b.w("t", QG, QG.n)
                    b.w("t", VE)
                    b.op("t", MM(P[7][:, 0:512], ONESR[0:1, :],
                                 ROW_A[0:1, 0:512], True, True))
                    b.op("t", MM(P[3][:, 0:512], ONESR[0:1, :],
                                 ROW_B[0:1, 0:512], True, True), inc=PE)
                b.w("v", QR[0], QR[0].n)
                for dc in range(NC8):
                    bank = P[5 + (dc % 2)]
                    b.w("t", VE)
                    pairs = []
                    for i in range(NC8):
                        half = WSA_C if i < 4 else WSB_C
                        lh = POOL[:, half + (i % 4) * 1024 + dc * 128:
                                  half + (i % 4) * 1024 + (dc + 1) * 128]
                        pairs.append((lh, oall[:, i * S + s_ * 512:
                                               i * S + s_ * 512 + 512]))
                    mm_group(bank[:, 0:512], pairs, inc=PE)
                    b.w("v", PE)
                    if add_affine:
                        b.op("v", TT(PT[1][:, :], pl(XNC_C + dc * 512, 512),
                                     P[7][:, 0:512], ALU.mult),
                             reads=[("POOL", "xnc", dc)], writes=[("PT", 1)])
                        b.op("v", TT(PT[1][:, :], PT[1][:, :], P[3][:, 0:512],
                                     ALU.add), reads=[("PT", 1)], writes=[("PT", 1)])
                        b.op("v", TT(asl(dst, dc, s_ * 512, 512), PT[1][:, :],
                                     bank[:, 0:512], ALU.add),
                             reads=[("PT", 1)],
                             writes=[(dst.name, "wo", dc, s_)], inc=VE)
                    else:
                        b.op("v", TT(asl(dst, dc, s_ * 512, 512),
                                     pl(XNC_C + dc * 512, 512),
                                     bank[:, 0:512], ALU.add),
                             reads=[("POOL", "xnc", dc)],
                             writes=[(dst.name, "wo", dc, s_)], inc=VE)

        wo_phase(A1, A2, dr["wo"], add_affine=False)
        b.barrier(PH)
        if tap == "x1pre":
            return finalize(A2)

        def ffn_A(w1d, harena_b, act_bias, ch, evac_fp32=None, cidx=None,
                  first=True):
            pe_slab = [PE.n, PE.n]
            ve_split = [VE.n, VE.n]
            for fg in range(8):
                sl = fg % 2
                wc = WSA_C if sl == 0 else WSB_C
                b.w("y", PE, pe_slab[sl])
                b.w("y", QC[sl], qcg[sl])
                if cidx is not None and not first:
                    b.op("y", DMA(POOLB[:, 2 * wc: 2 * wc + 8192],
                                  wpc(cidx, fg)), inc=QW[sl], inck=16)
                    b.w("t", QW[sl], QW[sl].n)
                else:
                    b.op("y", DMA(POOL[:, wc:wc + 4096]
                                  .rearrange("p (c f) -> p c f", f=512),
                                  w1d[:, fg * 512:(fg + 1) * 512]
                                  .rearrange("(c p) f -> p c f", p=128)),
                         inc=QW[sl], inck=16)
                    b.w("s", QW[sl], QW[sl].n)
                    STG2 = [TMP0[:, 1024:1536], TMP0[:, 1536:2048]]
                    for c0 in range(0, NC8, 2):
                        b.w("s", VE, VE.n)
                        for c in (c0, c0 + 1):
                            stg = STG2[c % 2]
                            base = 2 * wc + c * 1024
                            b.op("s", ACT(stg, POOL[:, wc + c * 512:
                                                    wc + c * 512 + 512],
                                          AF.Identity),
                                 reads=[("POOL", "wsl", sl)], writes=["TMP0"])
                            b.op("s", ACT(POOLB[:, base: base + 512], stg,
                                          AF.Identity),
                                 reads=["TMP0"], writes=[("POOL", "wsl", sl)],
                                 inc=SE if c == c0 + 1 else None)
                        b.w("v", SE, SE.n)
                        for c in (c0, c0 + 1):
                            stg = STG2[c % 2]
                            base = 2 * wc + c * 1024
                            b.op("v", TT(POOLB[:, base + 512: base + 1024], stg,
                                         POOLB[:, base: base + 512],
                                         ALU.subtract),
                                 reads=["TMP0", ("POOL", "wsl", sl)],
                                 writes=[("POOL", "wsl", sl)],
                                 inc=VE if c == c0 + 1 else None)
                    b.w("t", VE, VE.n)
                    if cidx is not None:
                        b.w("g", VE, VE.n)
                        b.op("g", DMA(wpc(cidx, fg),
                                      POOLB[:, 2 * wc: 2 * wc + 8192]),
                             reads=[("POOL", "wsl", sl)], inc=QC[sl], inck=16)
                        qcg[sl] = QC[sl].n
                for ft4 in range(4):
                    ft = fg * 4 + ft4
                    bank = P[ft % 2]
                    b.w("t", SE, max(0, SE.n - 1))
                    for c in range(NC8):
                        base = 2 * wc + c * 1024
                        wh = POOLB[:, base + ft4 * 128: base + ft4 * 128 + 128]
                        wl = POOLB[:, base + 512 + ft4 * 128:
                                   base + 512 + ft4 * 128 + 128]
                        xb = 2 * XNC_C + c * 1024
                        xh = POOLB[:, xb: xb + 512]
                        xl = POOLB[:, xb + 512: xb + 1024]
                        b.op("t", MM(bank[:, 0:512], wh, xh, c == 0, False))
                        b.op("t", MM(bank[:, 0:512], wh, xl, False, False))
                        b.op("t", MM(bank[:, 0:512], wl, xh, False, c == NC8 - 1),
                             inc=PE if c == NC8 - 1 else None)
                    b.w("s", PE, PE.n)
                    if evac_fp32 is not None:
                        evac_fp32(ft, bank)
                    else:
                        stg2 = PT[0][:, :]
                        b.w("s", VE, ve_split[0])
                        if act_bias is None:
                            b.op("s", ACT(stg2, bank[:, 0:512], AF.Relu),
                                 writes=[("PT", 0)], inc=SE)
                        else:
                            b.op("s", ACT(stg2, bank[:, 0:512], AF.Gelu,
                                          bias=prm(act_bias, ft)),
                                 writes=[("PT", 0)], inc=SE)
                        b.w("v", SE, SE.n)
                        b.op("v", CPY(harena_b[:, ft * 1024: ft * 1024 + 512],
                                      stg2),
                             reads=[("PT", 0)], writes=[("HB", ft)])
                        b.op("v", TT(harena_b[:, ft * 1024 + 512:
                                              ft * 1024 + 1024], stg2,
                                     harena_b[:, ft * 1024: ft * 1024 + 512],
                                     ALU.subtract),
                             reads=[("PT", 0), ("HB", ft)],
                             writes=[("HB", ft)], inc=VE)
                        ve_split[0] = VE.n
                pe_slab[sl] = PE.n
            b.w("t", SE)
            b.w("t", VE, VE.n)

        def ffn_B(w2d, hsrc_b, ch, evac, cidx=None, first=True):
            b.w("t", VE, VE.n)
            pe_slab = [PE.n, PE.n]
            for fg in range(8):
                sl = fg % 2
                wc = WSA_C if sl == 0 else WSB_C
                b.w("y", PE, pe_slab[sl])
                b.w("y", QC[sl], qcg[sl])
                if cidx is not None and not first:
                    b.op("y", DMA(POOLB[:, 2 * wc: 2 * wc + 8192],
                                  wpc(cidx, fg)), inc=QW[sl], inck=16)
                    b.w("t", QW[sl], QW[sl].n)
                else:
                    b.op("y", DMA(POOL[:, wc:wc + 4096]
                                  .rearrange("p (c f) -> p c f", f=1024),
                                  w2d[fg * 512:(fg + 1) * 512, :]
                                  .rearrange("(s p) d -> p s d", p=128)),
                         inc=QW[sl], inck=16)
                    b.w("s", QW[sl], QW[sl].n)
                    STG2 = [TMP0[:, 1024:1536], TMP0[:, 1536:2048]]
                    for c0 in range(0, NC8, 2):
                        b.w("s", VE, VE.n)
                        for c in (c0, c0 + 1):
                            stg = STG2[c % 2]
                            s4, hh = c // 2, c % 2
                            f0 = wc + s4 * 1024 + hh * 512
                            hb = 2 * wc + s4 * 2048 + hh * 512
                            b.op("s", ACT(stg, POOL[:, f0: f0 + 512],
                                          AF.Identity),
                                 reads=[("POOL", "wsl", sl)], writes=["TMP0"])
                            b.op("s", ACT(POOLB[:, hb: hb + 512], stg,
                                          AF.Identity),
                                 reads=["TMP0"], writes=[("POOL", "wsl", sl)],
                                 inc=SE if c == c0 + 1 else None)
                        b.w("v", SE, SE.n)
                        for c in (c0, c0 + 1):
                            stg = STG2[c % 2]
                            s4, hh = c // 2, c % 2
                            hb = 2 * wc + s4 * 2048 + hh * 512
                            b.op("v", TT(POOLB[:, hb + 1024: hb + 1536], stg,
                                         POOLB[:, hb: hb + 512],
                                         ALU.subtract),
                                 reads=["TMP0", ("POOL", "wsl", sl)],
                                 writes=[("POOL", "wsl", sl)],
                                 inc=VE if c == c0 + 1 else None)
                    b.w("t", VE, VE.n)
                    if cidx is not None:
                        b.w("g", VE, VE.n)
                        b.op("g", DMA(wpc(cidx, fg),
                                      POOLB[:, 2 * wc: 2 * wc + 8192]),
                             reads=[("POOL", "wsl", sl)], inc=QC[sl], inck=16)
                        qcg[sl] = QC[sl].n
                for ft4 in range(4):
                    ft = fg * 4 + ft4
                    hh = hsrc_b[:, ft * 1024: ft * 1024 + 512]
                    hl = hsrc_b[:, ft * 1024 + 512: ft * 1024 + 1024]
                    for dc in range(NC8):
                        base = 2 * wc + ft4 * 2048 + dc * 128
                        w_h = POOLB[:, base: base + 128]
                        w_l = POOLB[:, base + 1024: base + 1024 + 128]
                        b.op("t", MM(P[dc][:, 0:512], w_h, hh, ft == 0, False))
                        b.op("t", MM(P[dc][:, 0:512], w_h, hl, False, False))
                        b.op("t", MM(P[dc][:, 0:512], w_l, hh, False, ft == 31),
                             inc=PE if (ft4 == 3 and dc == NC8 - 1) else None)
                pe_slab[sl] = PE.n
            b.w("v", PE)
            b.w("s", PE)
            for dc in range(NC8):
                evac(dc)

        def affine_xnc_pairs(get_src, atag, btag, gnm, bnm, pk_s0, tag):
            rows_from(atag, btag, pk_s0)
            b.w("t", QG, QG.n)
            b.w("t", VE)
            b.op("t", MM(P[3][:, 0:512], ONESR[0:1, :], ROW_A[0:1, 0:512],
                         True, True))
            b.op("t", MM(P[4][:, 0:512], ONESR[0:1, :], ROW_B[0:1, 0:512],
                         True, True), inc=PE)
            b.w("v", PE)
            stg = TMP0[:, 1536:2048]
            for c in range(NC8):
                xb = 2 * XNC_C + c * 1024
                b.op("v", TT(PT[0][:, :], get_src(c), P[3][:, 0:512], ALU.mult),
                     reads=[(tag, "src", c)], writes=[("PT", 0)])
                if gnm is not None:
                    b.op("v", TT(PT[0][:, :], PT[0][:, :], P[4][:, 0:512],
                                 ALU.add), reads=[("PT", 0)], writes=[("PT", 0)])
                    b.op("v", lambda e, o=stg, i=PT[0][:, :], g=prm(gnm, c),
                         bt=prm(bnm, c): e.tensor_scalar(o, i, g, bt, ALU.mult,
                                                         ALU.add),
                         reads=[("PT", 0)], writes=["TMP0"])
                else:
                    b.op("v", TT(stg, PT[0][:, :], P[4][:, 0:512], ALU.add),
                         reads=[("PT", 0)], writes=["TMP0"])
                b.op("v", CPY(POOLB[:, xb: xb + 512], stg),
                     reads=["TMP0"], writes=[(tag, "xp", c)])
                b.op("v", TT(POOLB[:, xb + 512: xb + 1024], stg,
                             POOLB[:, xb: xb + 512], ALU.subtract),
                     reads=["TMP0", (tag, "xp", c)],
                     writes=[(tag, "xp", c)], inc=VE)

        def moe_phase():
            for ch in range(NCH):
                t0 = ch * TCH
                b.w("g", PE, PE.n)
                b.w("g", VE, VE.n)
                for c in range(NC8):
                    b.op("g", DMA(pl(XNC_C + c * 512, 512),
                                  dch("spill")[c][:, t0:t0 + 512]),
                         writes=[("POOL", "xnc", c)], inc=QR[0], inck=16)
                b.w("v", QR[0], QR[0].n)
                affine_xnc_pairs(lambda c: pl(XNC_C + c * 512, 512),
                                 "a", "bb", None, None, ch * 4, "POOLxnc")
                for tt in range(4):
                    b.w("t", VE)
                    for c in range(NC8):
                        xb = 2 * XNC_C + c * 1024
                        xh = POOLB[:, xb + tt * 128: xb + tt * 128 + 128]
                        xl = POOLB[:, xb + 512 + tt * 128: xb + 512 + tt * 128 + 128]
                        gh = GWTBB[:, c * 8:(c + 1) * 8]
                        gl = GWTBB[:, 64 + c * 8: 64 + (c + 1) * 8]
                        b.op("t", MM(P[5][:, 0:8], xh, gh, c == 0, False))
                        b.op("t", MM(P[5][:, 0:8], xh, gl, False, False))
                        b.op("t", MM(P[5][:, 0:8], xl, gh, False, c == NC8 - 1),
                             inc=PE if c == NC8 - 1 else None)
                    b.w("v", PE)
                    b.op("v", CPY(GTb[:, :], P[5][:, 0:8]), writes=["GTb"], inc=VE)
                    b.op("v", RED(GCOL[:, 0:1], GTb[:, :], ALU.max),
                         reads=["GTb"], writes=[("GC", 0)])
                    b.op("v", lambda e: e.tensor_scalar(
                        GS1[:, :], GTb[:, :], GCOL[:, 0:1], None, ALU.is_ge),
                        reads=["GTb", ("GC", 0)], writes=["GS1"])
                    b.op("v", STT(GS2[:, :], GS1[:, :], -1e30, GTb[:, :],
                                  ALU.mult, ALU.add), reads=["GS1", "GTb"],
                         writes=["GS2"])
                    b.op("v", RED(GCOL[:, 1:2], GS2[:, :], ALU.max),
                         reads=["GS2"], writes=[("GC", 1)])
                    b.op("v", lambda e: e.tensor_scalar(
                        GS2[:, :], GS2[:, :], GCOL[:, 1:2], None, ALU.is_ge),
                        reads=["GS2", ("GC", 1)], writes=["GS2"])
                    b.op("v", TT(GCOL[:, 2:3], GCOL[:, 0:1], GCOL[:, 1:2],
                                 ALU.subtract), reads=[("GC", 0), ("GC", 1)],
                         writes=[("GC", 2)], inc=VE)
                    b.w("s", VE)
                    b.op("s", ACT(GCOL[:, 3:4], GCOL[:, 2:3], AF.Sigmoid,
                                  scale=-1.0), reads=[("GC", 2)],
                         writes=[("GC", 3)])
                    b.op("s", ACT(GCOL[:, 2:3], GCOL[:, 2:3], AF.Sigmoid),
                         reads=[("GC", 2)], writes=[("GC", 2)], inc=SE)
                    b.w("v", SE)
                    b.op("v", lambda e: e.tensor_scalar(
                        GS1[:, :], GS1[:, :], GCOL[:, 2:3], None, ALU.mult),
                        reads=["GS1", ("GC", 2)], writes=["GS1"])
                    b.op("v", lambda e: e.scalar_tensor_tensor(
                        GS1[:, :], GS2[:, :], GCOL[:, 3:4], GS1[:, :],
                        ALU.mult, ALU.add),
                        reads=["GS1", "GS2", ("GC", 3)], writes=["GS1"], inc=VE)
                    b.w("t", VE)
                    b.op("t", MM(P[6][0:8, 0:128], GS1[:, :], I128[:, :],
                                 True, True), inc=PE)
                    b.w("v", PE)
                    b.w("v", QG, QG.n)
                    b.op("v", CPY(CTS[:, :], P[6][0:8, 0:128]), writes=["STG32"],
                         inc=VE)
                    b.w("g", VE)
                    b.w("g", PE, PE.n)
                    crd0 = bass.AP(TMP0.tensor, tt * 128, [[2048 * 32, 4], [1, 128]])
                    crd1 = bass.AP(TMP0.tensor, 512 + tt * 128,
                                   [[2048 * 32, 4], [1, 128]])
                    b.op("g", DMA(crd0, CTS[0:4, :]), reads=["STG32"],
                         inc=QG, inck=16)
                    b.op("g", DMA(crd1, CTS[4:8, :]), reads=["STG32"],
                         inc=QG, inck=16)
                b.w("t", QG, QG.n)
                for e_ in range(NE):
                    r0 = 32 * (e_ % 4)
                    crow = TMP0[r0:r0 + 1, (e_ // 4) * 512:(e_ // 4) * 512 + 512]
                    b.w("t", VE)
                    b.op("t", lambda e, o=P[5][:, 0:512],
                         a=ONESR[r0:r0 + 1, :], r=crow, tp=(r0, 0):
                         e.matmul(o, a, r, start=True, stop=True,
                                  tile_position=tp), inc=PE)
                    b.w("v", PE)
                    b.op("v", CPY(PT[1][:, :], P[5][:, 0:512]),
                         writes=[("PT", 1)], inc=VE)
                    b.w("t", VE)
                    ffn_A(dr["w1T"][e_], A1B, None, ch, cidx=e_,
                          first=(ch == 0))

                    def moe_evac(dc, t0=t0, ch=ch):
                        b.op("v", TT(PT[0][:, :], P[dc][:, 0:512], PT[1][:, :],
                                     ALU.mult), reads=[("PT", 1)],
                             writes=[("PT", 0)])
                        b.op("v", TT(asl(A2, dc, t0, 512), asl(A2, dc, t0, 512),
                                     PT[0][:, :], ALU.add),
                             reads=[("PT", 0), ("A2", "x1", dc, ch)],
                             writes=[("A2", "x1", dc, ch)], inc=VE)
                    ffn_B(dr["w2T"][e_], A1B, ch, moe_evac, cidx=8 + e_,
                          first=(ch == 0))

        moe_phase()
        b.barrier(PH)
        if tap == "x1":
            return finalize(A2)

        stats_pass(A2, D, 0, S, inner_too=True)
        b.barrier(PH)
        affine_apply(lambda c, s: asl(A2, c, s * 512, 512),
                     lambda c, s: asl(A1, c, s * 512, 512),
                     "A", "B", "g1", "b1", 0, S, "a21")
        b.w("g", VE)
        for c in range(NC8):
            b.op("g", DMA(dch("spill")[c], asl(A2, c, 0, S)),
                 reads=[("a21", "spx", c)], inc=QG, inck=16)
        b.w("g", QG)
        b.barrier(PH)
        if tap == "xn2p":
            return finalize(A1)

        attn_layer(True, A1, A2)
        b.barrier(PH)
        if tap == "l1heads":
            return finalize(A2)

        wo_phase(A2, A1, dr["woT1"], add_affine=True)
        b.barrier(PH)
        if tap == "l1y":
            return finalize(A1)

        def mlp_phase():
            for ch in range(NCH):
                t0 = ch * TCH
                stats_pass(A1, D, t0, 512, inner_too=False)
                affine_xnc_pairs(lambda c: asl(A1, c, t0, 512),
                                 "a", "bb", "g2", "b2", 0, "mlph")
                b.w("t", VE)
                ffn_A(dr["mw1T"], A2B, "mb1", ch, cidx=16, first=(ch == 0))

                def mlp_evac(dc, t0=t0, ch=ch):
                    tb = TMP0[:, 0:512] if dc % 2 == 0 else TMP0[:, 512:1024]
                    if dc >= 2:
                        b.w("s", VE, VE.n - 1)
                    b.op("s", ACT(tb, P[dc][:, 0:512], AF.Identity,
                                  bias=prm("mb2", dc)),
                         writes=["TMP0"], inc=SE)
                    b.w("v", SE, SE.n)
                    b.op("v", TT(asl(A1, dc, t0, 512), asl(A1, dc, t0, 512), tb,
                                 ALU.add),
                         reads=["TMP0", ("A1", "acc", dc, ch)],
                         writes=[("A1", "acc", dc, ch)], inc=VE)
                ffn_B(dr["mw2T"], A2B, ch, mlp_evac, cidx=17,
                      first=(ch == 0))

        mlp_phase()
        b.barrier(PH)
        if tap == "l1branch":
            return finalize(A1)

        def quant(wsrc, wdst, axis_d):
            b.w("g", VE, VE.n)
            b.op("g", lambda e: e.memset(QSC[:, 0:1], 0.0),
                 writes=[("QSC", 0)], inc=GE)
            b.w("v", GE, GE.n)

            def slab_aps(sl_):
                if axis_d:
                    src_ = wsrc[:, sl_ * 512:(sl_ + 1) * 512].rearrange(
                        "(c p) f -> p c f", p=128)
                    dst_ = wdst[:, sl_ * 512:(sl_ + 1) * 512].rearrange(
                        "(c p) f -> p c f", p=128)
                    z = 512
                else:
                    src_ = wsrc[sl_ * 512:(sl_ + 1) * 512, :].rearrange(
                        "(s p) d -> p s d", p=128)
                    dst_ = wdst[sl_ * 512:(sl_ + 1) * 512, :].rearrange(
                        "(s p) d -> p s d", p=128)
                    z = 1024
                return src_, dst_, z

            for sl_ in range(8):
                src_, _, z = slab_aps(sl_)
                b.w("y", VE, VE.n)
                b.w("y", QG, QG.n)
                b.op("y", DMA(POOL[:, WSA_C:WSA_C + 4096].rearrange(
                    "p (a z) -> p a z", z=z), src_), inc=QW[0], inck=16)
                b.w("v", QW[0], QW[0].n)
                b.op("v", RED(QSC[:, 1:2], POOL[:, WSA_C:WSA_C + 4096], ALU.add,
                              absv=True), reads=[("POOL", "qs")],
                     writes=[("QSC", 1)])
                b.op("v", TT(QSC[:, 0:1], QSC[:, 0:1], QSC[:, 1:2], ALU.add),
                     reads=[("QSC", 0), ("QSC", 1)], writes=[("QSC", 0)], inc=VE)
            b.w("t", VE)
            b.op("t", MM(P[5][0:1, 0:1], QSC[:, 0:1], ONESC[:, :], True, True),
                 inc=PE)
            b.w("v", PE)
            b.op("v", CPY(QV[0:1, 0:1], P[5][0:1, 0:1]), writes=[("QV", 0)])
            b.op("v", TSC(QV[0:1, 1:2], QV[0:1, 0:1], 1.0 / (D * FF), 1e-5,
                          ALU.mult, ALU.max),
                 reads=[("QV", 0)], writes=[("QV", 1)])
            b.op("v", RECIP(QV[0:1, 2:3], QV[0:1, 1:2]), reads=[("QV", 1)],
                 writes=[("QV", 2)])
            b.op("v", RECIP(QV[0:1, 3:4], QV[0:1, 2:3]), reads=[("QV", 2)],
                 writes=[("QV", 3)], inc=VE)
            b.w("t", VE)
            b.op("t", MM(P[5][:, 0:2], ONESR[0:1, :], QV[0:1, 2:4], True, True),
                 inc=PE)
            b.w("v", PE)
            b.op("v", CPY(SCB[:, :], P[5][:, 0:2]), writes=["SCB"], inc=VE)
            for sl_ in range(8):
                src_, dst_, z = slab_aps(sl_)
                b.w("y", QG, QG.n)
                b.w("y", VE, VE.n)
                b.op("y", DMA(POOL[:, WSA_C:WSA_C + 4096].rearrange(
                    "p (a z) -> p a z", z=z), src_), inc=QW[0], inck=16)
                b.w("v", QW[0], QW[0].n)
                for hh in range(2):
                    seg = POOL[:, WSA_C + hh * 2048: WSA_C + hh * 2048 + 2048]
                    b.op("v", lambda e, o=TMP0[:, :], i=seg: e.tensor_scalar(
                        o, i, SCB[:, 0:1], 0.5, ALU.mult, ALU.is_gt),
                        reads=[("POOL", "qs", hh), "SCB"],
                        writes=["TMP0"])
                    b.op("v", lambda e, o=pl(WSB_C, 2048), i=seg: e.tensor_scalar(
                        o, i, SCB[:, 0:1], -0.5, ALU.mult, ALU.is_lt),
                        reads=[("POOL", "qs", hh), "SCB"],
                        writes=[("POOL", "ql")])
                    b.op("v", TT(seg, TMP0[:, :], pl(WSB_C, 2048), ALU.subtract),
                         reads=["TMP0", ("POOL", "ql")],
                         writes=[("POOL", "qs", hh)])
                    b.op("v", lambda e, o=seg, i=seg: e.tensor_scalar(
                        o, i, SCB[:, 1:2], None, ALU.mult),
                        reads=[("POOL", "qs", hh), "SCB"],
                        writes=[("POOL", "qs", hh)], inc=VE)
                b.w("g", VE)
                b.op("g", DMA(dst_, POOL[:, WSA_C:WSA_C + 4096].rearrange(
                    "p (a z) -> p a z", z=z)),
                    reads=[("POOL", "qs")], inc=QG, inck=16)
            b.w("g", QG)

        quant(dr["fw1T"], dr["fq1"], True)
        quant(dr["fw2T"], dr["fq2"], False)
        b.barrier(PH)

        def l1ffn_phase():
            for ch in range(NCH):
                t0 = ch * TCH
                b.w("g", PE, PE.n)
                b.w("g", VE, VE.n)
                for c in range(NC8):
                    b.op("g", DMA(pl(XNC_C + c * 512, 512),
                                  dch("spill")[c][:, t0:t0 + 512]),
                         writes=[("POOL", "xnc", c)], inc=QR[0], inck=16)
                b.w("v", QR[0], QR[0].n)
                affine_xnc_pairs(lambda c: pl(XNC_C + c * 512, 512),
                                 "A", "B", "fg1", "fb1", ch * 4, "POOLxnc")
                b.w("t", VE)

                def ffnq_evac(ft, bank, ch=ch):
                    b.op("s", ACT(A2[:, ft * 512:(ft + 1) * 512], bank[:, 0:512],
                                  AF.Gelu, bias=prm("fb1v", ft)),
                         writes=[("A2", "h", ft, ch)], inc=SE)
                ffn_A(dr["fq1"], None, None, ch, evac_fp32=ffnq_evac,
                      cidx=18, first=(ch == 0))
                b.w("s", VE, VE.n)
                b.drain("s")
                b.w("t", VE, VE.n)
                sqw = [TMP0[:, i * 512:(i + 1) * 512] for i in range(4)]
                pe_sq = [0] * 4
                for g4 in range(8):
                    for i in range(4):
                        ft = g4 * 4 + i
                        b.w("s", PE, pe_sq[i])
                        b.op("s", ACT(sqw[i], A2[:, ft * 512:(ft + 1) * 512],
                                      AF.Square), reads=[("A2", "h", ft, ch)],
                             writes=[("SQ", i)], inc=SE)
                    se_mark = SE.n
                    for tt in range(4):
                        for i in range(4):
                            ft = g4 * 4 + i
                            b.op("t", MM(P[tt][:, 0:1],
                                         A2[:, ft * 512 + tt * 128:
                                            ft * 512 + (tt + 1) * 128],
                                         ONESC[:, :], ft == 0, ft == 31))
                    b.w("t", SE, se_mark)
                    for tt in range(4):
                        for i in range(4):
                            ft = g4 * 4 + i
                            b.op("t", MM(P[4 + tt][:, 0:1],
                                         sqw[i][:, tt * 128:(tt + 1) * 128],
                                         ONESC[:, :], ft == 0, ft == 31),
                                 inc=PE if (tt == 3 and i == 3) else None)
                    for i in range(4):
                        pe_sq[i] = PE.n
                b.w("v", PE)
                for tt in range(4):
                    b.op("v", CPY(pkv("s1")[:, tt:tt + 1], P[tt][:, 0:1]),
                         writes=[("PK", "s1")])
                    b.op("v", CPY(pkv("s2")[:, tt:tt + 1], P[4 + tt][:, 0:1]),
                         writes=[("PK", "s2")],
                         inc=VE if tt == 3 else None)
                _stats_math(4, float(FF), False)
                rows_from("a", "bb", 0)
                b.w("t", QG, QG.n)
                b.w("t", VE)
                b.op("t", MM(P[3][:, 0:512], ONESR[0:1, :], ROW_A[0:1, 0:512],
                             True, True))
                b.op("t", MM(P[4][:, 0:512], ONESR[0:1, :], ROW_B[0:1, 0:512],
                             True, True), inc=PE)
                b.w("v", PE)
                stgh = TMP0[:, 1536:2048]
                for ft in range(32):
                    b.op("v", TT(PT[0][:, :], A2[:, ft * 512:(ft + 1) * 512],
                                 P[3][:, 0:512], ALU.mult),
                         reads=[("A2", "h", ft, ch)], writes=[("PT", 0)])
                    b.op("v", TT(PT[0][:, :], PT[0][:, :], P[4][:, 0:512],
                                 ALU.add), reads=[("PT", 0)], writes=[("PT", 0)])
                    b.op("v", lambda e, o=stgh, i=PT[0][:, :], g=prm("fg2", ft),
                         bt=prm("fb2", ft):
                         e.tensor_scalar(o, i, g, bt, ALU.mult, ALU.add),
                         reads=[("PT", 0)], writes=["TMP0"])
                    b.op("v", CPY(A2B[:, ft * 1024: ft * 1024 + 512], stgh),
                         reads=["TMP0"], writes=[("A2", "h", ft, ch)])
                    b.op("v", TT(A2B[:, ft * 1024 + 512: ft * 1024 + 1024], stgh,
                                 A2B[:, ft * 1024: ft * 1024 + 512],
                                 ALU.subtract),
                         reads=["TMP0", ("A2", "h", ft, ch)],
                         writes=[("A2", "h", ft, ch)], inc=VE)

                def ffn_evac(dc, t0=t0, ch=ch):
                    tb = TMP0[:, 0:512] if dc % 2 == 0 else TMP0[:, 512:1024]
                    if dc >= 2:
                        b.w("s", VE, VE.n - 1)
                    b.op("s", ACT(tb, P[dc][:, 0:512], AF.Identity,
                                  bias=prm("fb2v", dc)),
                         writes=["TMP0"], inc=SE)
                    b.w("v", SE, SE.n)
                    b.op("v", TT(asl(A1, dc, t0, 512), asl(A1, dc, t0, 512), tb,
                                 ALU.add),
                         reads=["TMP0", ("A1", "acc2", dc, ch)],
                         writes=[("A1", "acc2", dc, ch)], inc=VE)
                ffn_B(dr["fq2"], A2B, ch, ffn_evac, cidx=19,
                      first=(ch == 0))

        l1ffn_phase()
        b.barrier(PH)

        for c in range(NC8):
            half = c % 2
            rl = pl(XNC_C + half * 2048, 2048)
            ob = pl(WSA_C + half * 2048, 2048)
            qr = QR[half]
            b.w("g", VE, VE.n)
            b.op("g", DMA(rl, dch("spill")[c]), writes=[("POOL", "rl", half)],
                 inc=qr, inck=16)
            b.w("v", qr, qr.n)
            b.w("v", QO, QO.n)
            b.op("v", TT(ob, rl, asl(A1, c, 0, S), ALU.add),
                 reads=[("POOL", "rl", half), ("A1", "fin", c)],
                 writes=[("POOL", "ob", half)], inc=VE)
            b.w("g", VE)
            b.op("g", DMA(dch("outT")[c], ob), reads=[("POOL", "ob", half)],
                 inc=QO, inck=16)
        b.w("g", QO)
        b.w("y", QO)
        block = es.enter_context(nc.Block())
        b.replay(block)
    return nc


def _alibi_table():
    xu = np.array([1060439283, 1060168325, 1059866229, 1059527426, 1059144984,
                   1058710189, 1058211980, 1057636198, 1056964608, 1055382913,
                   1053503477, 1051256226, 1048576000, 1042367767, 1031798784,
                   998244352], dtype=np.uint32)
    sl = xu.view(np.float32)
    p = np.arange(128)
    c = np.arange(ESCW)
    d32 = -np.abs(p[:, None] - c[None, :] + K0).astype(np.float32)
    return np.ascontiguousarray((sl[:, None, None] * d32[None]).astype(np.float32))



def _prep_inputs(ins):
    f32 = lambda a: np.ascontiguousarray(np.asarray(a), dtype=np.float32)
    pack = lambda v: np.ascontiguousarray(f32(v).reshape(-1, 128).T)
    prm = np.zeros((128, 224), np.float32)
    off = 0
    for nm, wd, val in [
            ("g0a", 8, ins["L0_attn_ln_gamma"]), ("b0a", 8, ins["L0_attn_ln_beta"]),
            ("g1", 8, ins["L1_attn_ln1_g"]), ("b1", 8, ins["L1_attn_ln1_b"]),
            ("g2", 8, ins["L1_attn_ln2_g"]), ("b2", 8, ins["L1_attn_ln2_b"]),
            ("mb1", 32, ins["L1_attn_mlp_b1"]), ("mb2", 8, ins["L1_attn_mlp_b2"]),
            ("fg1", 8, ins["L1_ffn_ln1_g"]), ("fb1", 8, ins["L1_ffn_ln1_b"]),
            ("fb1v", 32, ins["L1_ffn_b1"]), ("fg2", 32, ins["L1_ffn_ln2_g"]),
            ("fb2", 32, ins["L1_ffn_ln2_b"]), ("fb2v", 8, ins["L1_ffn_b2"])]:
        prm[:, off:off + wd] = pack(val)
        off += wd
    shared = {
        "wq": f32(ins["L0_attn_Wq"]),
        "wk": f32(ins["L0_attn_Wk"]),
        "wv": f32(ins["L0_attn_Wv"]),
        "wo": f32(ins["L0_attn_Wo"]),
        "gwT": np.ascontiguousarray(f32(ins["L0_ffn_gate_w"]).T),
        "w1T": np.ascontiguousarray(f32(ins["L0_ffn_w1"]).transpose(0, 2, 1)),
        "w2T": np.ascontiguousarray(f32(ins["L0_ffn_w2"]).transpose(0, 2, 1)),
        "qkvT": np.ascontiguousarray(f32(ins["L1_attn_w_qkv"]).T),
        "woT1": np.ascontiguousarray(f32(ins["L1_attn_w_o"]).T),
        "mw1T": np.ascontiguousarray(f32(ins["L1_attn_mlp_w1"]).T),
        "mw2T": np.ascontiguousarray(f32(ins["L1_attn_mlp_w2"]).T),
        "fw1T": np.ascontiguousarray(f32(ins["L1_ffn_w1"]).T),
        "fw2T": np.ascontiguousarray(f32(ins["L1_ffn_w2"]).T),
        "prm": prm,
        "alibi": _alibi_table(),
    }
    x = f32(ins["L0_attn_x"])
    in_maps = []
    for c in range(8):
        m = dict(shared)
        m["xT"] = np.ascontiguousarray(x[c].T)
        in_maps.append(m)
    return in_maps


def kernel(**inputs):
    in_maps = _prep_inputs(inputs)
    nc = build_program(tap=TAP)
    res = run_bass_kernel_spmd(nc, in_maps, core_ids=list(range(8)))
    return np.stack([np.asarray(r["outT"]).T for r in res.results])
